# version 01
# baseline (speedup 1.0000x reference)
from contextlib import ExitStack
import numpy as np
import ml_dtypes
import concourse.bass as bass
import concourse.mybir as mybir
from concourse.bass_utils import run_bass_kernel_spmd

F32 = mybir.dt.float32
BF16 = mybir.dt.bfloat16
AF = mybir.ActivationFunctionType
ALU = mybir.AluOpType
AX = mybir.AxisListType
BF = ml_dtypes.bfloat16

P = 128
NB, S, DM = 4, 4096, 1024
TOK = 2048
NCORE = 8

ENGINES = ("sync", "scalar", "vector", "tensor", "gpsimd")


def own_rows(parity):
    return np.concatenate([np.arange(256 * i + 128 * parity, 256 * i + 128 * parity + 128)
                           for i in range(16)])


class Sem:
    __slots__ = ("h", "n", "name")

    def __init__(self, h, name):
        self.h = h
        self.n = 0
        self.name = name


class Qu:

    def __init__(self, sems):
        self.sems = sems
        self.i = 0

    def barrier(self):
        return [(s, s.n) for s in self.sems]


class Ctx:

    def __init__(self, nc):
        self.nc = nc
        self.stream = {e: [] for e in ENGINES}
        self.waited = {e: {} for e in ENGINES}
        self.engsem = {}

    @staticmethod
    def _flat(deps):
        out = []
        for d in deps:
            if d is None:
                continue
            if isinstance(d, list):
                out.extend([x for x in d if x is not None])
            else:
                out.append(d)
        return out

    def ev(self, eng, f, deps=(), inc=None, inc_by=1):
        deps = self._flat(deps)
        es_ = self.engsem.get(eng)
        if es_ is not None:
            s_, b_ = es_
            deps = deps + [(s_, s_.n)]
            if inc is None:
                inc = s_
                inc_by = b_
        dl = []
        for s, v in deps:
            if v is None:
                continue
            w = self.waited[eng].get(id(s), 0)
            if v > w:
                dl.append((s, v))
                self.waited[eng][id(s)] = v
        if inc is not None:
            inc.n += inc_by
            ret = inc.n
            tgt = (inc, inc_by)
        else:
            ret = None
            tgt = None

        def body(e, dl=dl, f=f, tgt=tgt):
            for s, v in dl:
                e.wait_ge(s.h, v)
            ins = f(e)
            if tgt is not None:
                ins.then_inc(tgt[0].h, tgt[1])

        self.stream[eng].append(body)
        return ret

    def dma(self, eng, f, deps=()):
        q = self.queues[eng]
        s = q.sems[q.i % len(q.sems)]
        q.i += 1
        deps = self._flat(deps) + [(s, s.n)]
        v = self.ev(eng, f, deps=deps, inc=s, inc_by=16)
        return (s, v)

    def flush(self, block):
        for e in ENGINES:
            fl = self.stream[e]
            if fl:
                getattr(block, e)(lambda eng, fl=fl: [f(eng) for f in fl])


class Prog:

    def __init__(self, nc, ctx, sq, gq, sv, sa, st, t):
        self.nc = nc
        self.ctx = ctx
        self.sq = sq
        self.gq = gq
        self.sv = sv
        self.sa = sa
        self.st = st
        self.t = t



def emit_ln(pr, x_dram, row0, ntiles, xln_dram, drow0=0):
    ctx, sv, sa = pr.ctx, pr.sv, pr.sa
    t = pr.t
    if not hasattr(pr, "lnfree"):
        pr.lnfree = {0: None, 1: None}
        pr.lnsfree = {0: None, 1: None}
    free = pr.lnfree
    sfree = pr.lnsfree
    for i in range(ntiles):
        bi = i % 2
        xin = t["xin%d" % bi]
        xln = t["xln%d" % bi]
        r = row0 + i * 128
        dr = drow0 + i * 128
        ld = ctx.dma("sync", lambda e, xin=xin, r=r: e.dma_start(
            out=xin[:], in_=x_dram[r:r + 128, :]), deps=[(sv, free[bi])])
        ctx.ev("vector", lambda e, xin=xin: e.bn_stats(t["xst"][:, 0:6], xin[:, 0:512]),
               deps=[ld])
        ctx.ev("vector", lambda e, xin=xin: e.bn_stats(t["xst"][:, 6:12], xin[:, 512:1024]))
        v1 = ctx.ev("vector", lambda e: e.bn_aggr(t["xmv"][:], t["xst"][:]))
        a1 = ctx.ev("scalar", lambda e: e.activation(t["xsd"][:, 0:1], t["xmv"][:, 1:2],
                                                     AF.Sqrt, bias=t["epsc"][:, 0:1]),
                    deps=[(sv, v1)])
        ctx.ev("vector", lambda e: e.reciprocal(t["xsd"][:, 1:2], t["xsd"][:, 0:1]),
               deps=[(sa, a1)])
        v3 = ctx.ev("vector", lambda e, xin=xin, xln=xln: e.tensor_scalar(
            xln[:], xin[:], t["xmv"][:, 0:1], t["xsd"][:, 1:2], ALU.subtract, ALU.mult),
            deps=[sfree[bi]])
        free[bi] = v3
        sfree[bi] = ctx.dma("sync", lambda e, xln=xln, dr=dr: e.dma_start(
            out=xln_dram[dr:dr + 128, :], in_=xln[:]), deps=[(sv, v3)])


def emit_ln_rope(pr, x_dram, row0, ntiles, xln_dram, rxln_dram, sin_dram, cos_dram,
                 xf, rot, sct_s, sct_c, rx0, rx1):
    ctx, sv, sa = pr.ctx, pr.sv, pr.sa
    t = pr.t
    if not hasattr(pr, "lnfree"):
        pr.lnfree = {0: None, 1: None}
        pr.lnsfree = {0: None, 1: None}
    free = pr.lnfree
    sfree = pr.lnsfree
    rxfree = {0: None, 1: None}
    xffree = None
    for i in range(ntiles):
        bi = i % 2
        xin = t["xin%d" % bi]
        xln = t["xln%d" % bi]
        rx = (rx0, rx1)[bi]
        r = row0 + i * 128
        ld = ctx.dma("sync", lambda e, xin=xin, r=r: e.dma_start(
            out=xin[:], in_=x_dram[r:r + 128, :]), deps=[(sv, free[bi])])
        lds = ctx.dma("sync", lambda e, r=r: e.dma_start(
            out=sct_s[:], in_=sin_dram[r:r + 128, :]), deps=[(sv, xffree)])
        ldc = ctx.dma("sync", lambda e, r=r: e.dma_start(
            out=sct_c[:], in_=cos_dram[r:r + 128, :]), deps=[(sv, xffree)])
        ctx.ev("vector", lambda e, xin=xin: e.bn_stats(t["xst"][:, 0:6], xin[:, 0:512]),
               deps=[ld])
        ctx.ev("vector", lambda e, xin=xin: e.bn_stats(t["xst"][:, 6:12], xin[:, 512:1024]))
        v1 = ctx.ev("vector", lambda e: e.bn_aggr(t["xmv"][:], t["xst"][:]))
        a1 = ctx.ev("scalar", lambda e: e.activation(t["xsd"][:, 0:1], t["xmv"][:, 1:2],
                                                     AF.Sqrt, bias=t["epsc"][:, 0:1]),
                    deps=[(sv, v1)])
        ctx.ev("vector", lambda e: e.reciprocal(t["xsd"][:, 1:2], t["xsd"][:, 0:1]),
               deps=[(sa, a1)])
        v3 = ctx.ev("vector", lambda e, xin=xin: e.tensor_scalar(
            xf[:], xin[:], t["xmv"][:, 0:1], t["xsd"][:, 1:2], ALU.subtract, ALU.mult))
        free[bi] = v3
        v4 = ctx.ev("vector", lambda e, xln=xln: e.tensor_copy(xln[:], xf[:]),
                    deps=[(qsf := sfree[bi])])
        sfree[bi] = ctx.dma("sync", lambda e, xln=xln, r=r: e.dma_start(
            out=xln_dram[r:r + 128, :], in_=xln[:]), deps=[(sv, v4)])
        ctx.ev("vector", lambda e: e.tensor_scalar_mul(
            rot[:].rearrange("p (a b) -> p a b", b=2)[:, :, 0],
            xf[:].rearrange("p (a b) -> p a b", b=2)[:, :, 1], -1.0))
        ctx.ev("vector", lambda e: e.tensor_copy(
            rot[:].rearrange("p (a b) -> p a b", b=2)[:, :, 1],
            xf[:].rearrange("p (a b) -> p a b", b=2)[:, :, 0]))
        ctx.ev("vector", lambda e: e.tensor_mul(xf[:], xf[:], sct_c[:]), deps=[ldc])
        ctx.ev("vector", lambda e: e.tensor_mul(rot[:], rot[:], sct_s[:]), deps=[lds])
        v9 = ctx.ev("vector", lambda e, rx=rx: e.tensor_add(rx[:], xf[:], rot[:]),
                    deps=[rxfree[bi]])
        xffree = v9
        rxfree[bi] = ctx.dma("sync", lambda e, rx=rx, r=r: e.dma_start(
            out=rxln_dram[r:r + 128, :], in_=rx[:]), deps=[(sv, v9)])


def emit_wt_loads(pr, wsb, wbf_dram, deps=(), ncol=DM):
    ctx = pr.ctx
    toks = []
    for c in range(8):
        toks.append(ctx.dma("sync", lambda e, c=c: e.dma_start_transpose(
            wsb[:, c * ncol:(c + 1) * ncol],
            wbf_dram[0:ncol, c * 128:(c + 1) * 128]), deps=deps))
    return toks


def emit_fm_loads(pr, xfm, xln_dram, nrows, dep_toks):
    ctx = pr.ctx
    toks = []
    for c in range(8):
        toks.append(ctx.dma("sync", lambda e, c=c: e.dma_start_transpose(
            xfm[:, c * nrows:(c + 1) * nrows],
            xln_dram[0:nrows, c * 128:(c + 1) * 128]), deps=dep_toks))
    return toks


def emit_w_natural(pr, wsb, w_dram, deps=(), kchunks=8):
    return pr.ctx.dma("gpsimd", lambda e: e.dma_start(
        out=wsb[:].rearrange("p (c m) -> p c m", c=kchunks),
        in_=w_dram.rearrange("(c p) m -> p c m", p=128)), deps=deps)


def proj_accum(pr, ps, wsb, wN, m, wcols, xfm, nrows, babs, bn, deps, last_extra=None):
    ctx, st = pr.ctx, pr.st
    for c in range(8):
        ctx.ev("tensor", lambda e, c=c: e.matmul(
            ps[:, 0:bn], wsb[:, c * wN + m * wcols: c * wN + m * wcols + wcols],
            xfm[:, c * nrows + babs: c * nrows + babs + bn],
            start=(c == 0), stop=(c == 7 and last_extra is None)),
            deps=(deps if c == 0 else []),
            inc=(st if (c == 7 and last_extra is None) else None))
    if last_extra is not None:
        lhsT, rhs = last_extra
        ctx.ev("tensor", lambda e: e.matmul(ps[:, 0:bn], lhsT, rhs, start=False, stop=True),
               inc=st)
    return st.n



def build_l0_attn_a():
    nc = bass.Bass("TRN2", target_bir_lowering=False, debug=False)
    x_d = nc.dram_tensor("x", [S, DM], F32, kind="ExternalInput").ap()
    xo_d = nc.dram_tensor("xo", [TOK, DM], F32, kind="ExternalInput").ap()
    wq_d = nc.dram_tensor("Wq", [DM, DM], F32, kind="ExternalInput").ap()
    wk_d = nc.dram_tensor("Wk", [DM, DM], F32, kind="ExternalInput").ap()
    wv_d = nc.dram_tensor("Wv", [DM, DM], F32, kind="ExternalInput").ap()
    wo_d = nc.dram_tensor("Wo", [DM, DM], F32, kind="ExternalInput").ap()
    bq_d = nc.dram_tensor("bq", [P, 8], F32, kind="ExternalInput").ap()
    bk_d = nc.dram_tensor("bk", [P, 8], F32, kind="ExternalInput").ap()
    bv_d = nc.dram_tensor("bv", [P, 8], F32, kind="ExternalInput").ap()
    bo_d = nc.dram_tensor("bo", [DM], F32, kind="ExternalInput").ap()
    ones_d = nc.dram_tensor("ones", [1, P], F32, kind="ExternalInput").ap()
    y_d = nc.dram_tensor("y", [TOK, DM], F32, kind="ExternalOutput").ap()
    xln_d = nc.dram_tensor("xln", [S, DM], BF16, kind="Internal").ap()
    xlno_d = nc.dram_tensor("xlno", [TOK, DM], BF16, kind="Internal").ap()

    es = ExitStack()
    with es:
        xin0 = es.enter_context(nc.sbuf_tensor("xin0", [P, DM], F32))
        xin1 = es.enter_context(nc.sbuf_tensor("xin1", [P, DM], F32))
        xst = es.enter_context(nc.sbuf_tensor("xst", [P, 12], F32))
        xmv = es.enter_context(nc.sbuf_tensor("xmv", [P, 2], F32))
        xsd = es.enter_context(nc.sbuf_tensor("xsd", [P, 2], F32))
        xln0 = es.enter_context(nc.sbuf_tensor("xln0", [P, DM], BF16))
        xln1 = es.enter_context(nc.sbuf_tensor("xln1", [P, DM], BF16))
        xfb0 = es.enter_context(nc.sbuf_tensor("xfb0", [P, 8 * 512], BF16))
        xfb1 = es.enter_context(nc.sbuf_tensor("xfb1", [P, 8 * 512], BF16))
        xfmo = es.enter_context(nc.sbuf_tensor("xfmo", [P, 8 * TOK], BF16))
        wA = es.enter_context(nc.sbuf_tensor("wA", [P, 8 * DM], BF16))
        wB = es.enter_context(nc.sbuf_tensor("wB", [P, 8 * DM], BF16))
        bqs = es.enter_context(nc.sbuf_tensor("bqs", [P, 8], F32))
        bks = es.enter_context(nc.sbuf_tensor("bks", [P, 8], F32))
        bvs = es.enter_context(nc.sbuf_tensor("bvs", [P, 8], F32))
        bos = es.enter_context(nc.sbuf_tensor("bos", [1, DM], BF16))
        onesr = es.enter_context(nc.sbuf_tensor("onesr", [1, P], BF16))
        qts = es.enter_context(nc.sbuf_tensor("qts", [P, 8 * TOK], BF16))
        ssc = es.enter_context(nc.sbuf_tensor("ssc", [P, 6 * 512], F32))
        kvm = es.enter_context(nc.sbuf_tensor("kvm", [P, 8], F32))
        ksm = es.enter_context(nc.sbuf_tensor("ksm", [P, 8], F32))
        tcol = es.enter_context(nc.sbuf_tensor("tcol", [P, 2], F32))
        epsc = es.enter_context(nc.sbuf_tensor("epsc", [P, 1], F32))
        xres0 = es.enter_context(nc.sbuf_tensor("xres0", [P, DM], F32))
        xres1 = es.enter_context(nc.sbuf_tensor("xres1", [P, DM], F32))
        gsc = es.enter_context(nc.sbuf_tensor("gsc", [P, 3 * 512], F32))
        ybuf0 = es.enter_context(nc.sbuf_tensor("ybuf0", [P, DM], F32))
        ybuf1 = es.enter_context(nc.sbuf_tensor("ybuf1", [P, DM], F32))
        ps_k = es.enter_context(nc.psum_tensor("ps_k", [P, 512], F32))
        ps_v = es.enter_context(nc.psum_tensor("ps_v", [P, 512], F32))
        ps_q = es.enter_context(nc.psum_tensor("ps_q", [P, 512], F32))
        ps_o0 = es.enter_context(nc.psum_tensor("ps_o0", [P, 512], F32))
        ps_o1 = es.enter_context(nc.psum_tensor("ps_o1", [P, 512], F32))
        h_qs = [es.enter_context(nc.semaphore("qs%d" % i)) for i in range(4)]
        h_qg = es.enter_context(nc.semaphore("qg"))
        h_sv = es.enter_context(nc.semaphore("sv"))
        h_sa = es.enter_context(nc.semaphore("sa"))
        h_st = es.enter_context(nc.semaphore("st"))
        block = es.enter_context(nc.Block())
        ctx = Ctx(nc)
        sq = Qu([Sem(h, "qs%d" % i) for i, h in enumerate(h_qs)])
        gq = Qu([Sem(h_qg, "qg")])
        sv = Sem(h_sv, "sv"); sa = Sem(h_sa, "sa"); st = Sem(h_st, "st")
        ctx.queues = {"sync": sq, "gpsimd": gq}
        ctx.engsem = {"vector": (sv, 1), "scalar": (sa, 1)}
        t = dict(xin0=xin0, xin1=xin1, xst=xst, xmv=xmv, xsd=xsd, xln0=xln0, xln1=xln1,
                 epsc=epsc)
        pr = Prog(nc, ctx, sq, gq, sv, sa, st, t)
        ctx.ev("vector", lambda e: e.memset(epsc[:], 1e-5))

        wk_t = emit_w_natural(pr, wA, wk_d)
        wv_t = emit_w_natural(pr, wB, wv_d)
        bq_t = ctx.dma("sync", lambda e: e.dma_start(out=bqs[:], in_=bq_d[:]))
        bk_t = ctx.dma("sync", lambda e: e.dma_start(out=bks[:], in_=bk_d[:]))
        bv_t = ctx.dma("sync", lambda e: e.dma_start(out=bvs[:], in_=bv_d[:]))
        bo_t = ctx.dma("gpsimd", lambda e: e.dma_start(out=bos[:], in_=bo_d.unsqueeze(0)))
        on_t = ctx.dma("gpsimd", lambda e: e.dma_start(out=onesr[:], in_=ones_d[:]))

        emit_ln(pr, x_d, 0, 32, xln_d)
        ln_b = sq.barrier()
        emit_ln(pr, xo_d, 0, 16, xlno_d)
        ln_o = sq.barrier()
        fm_o = emit_fm_loads(pr, xfmo, xlno_d, TOK, ln_o)

        u_ = ssc[:, 0:512]
        e_ = ssc[:, 512:1024]
        r_ = ssc[:, 1024:1536]
        ktil = ssc[:, 1536:2048]
        vr = ssc[:, 2048:2560]
        kv = ssc[:, 2560:3072]

        psfree = {"k": None, "v": None, "q": None}

        def fmap(ps, bcol, outsl, outT, mmv, bdep):
            v1 = ctx.ev("vector", lambda e, ps=ps, bcol=bcol: e.tensor_scalar(
                u_, ps[:], bcol, None, ALU.add), deps=[(st, mmv), bdep])
            ctx.ev("vector", lambda e: e.tensor_scalar_min(e_, u_, 0.0))
            a1 = ctx.ev("scalar", lambda e: e.activation(e_, e_, AF.Exp),
                        deps=[(sv, sv.n)])
            ctx.ev("vector", lambda e: e.tensor_scalar_max(r_, u_, 0.0))
            v4 = ctx.ev("vector", lambda e, outT=outT, outsl=outsl: e.tensor_tensor(
                outT[:, outsl[0]:outsl[1]], e_, r_, ALU.add), deps=[(sa, a1)])
            return v4, v1

        xfb_free = {0: None, 1: None}
        for bi in range(8):
            fb = (xfb0, xfb1)[bi % 2]
            fmv = []
            for c in range(8):
                fmv.append(ctx.dma("sync", lambda e, c=c, bi=bi, fb=fb: e.dma_start_transpose(
                    fb[:, c * 512:(c + 1) * 512],
                    xln_d[bi * 512:(bi + 1) * 512, c * 128:(c + 1) * 128]),
                    deps=[ln_b, (st, xfb_free[bi % 2])]))
            for m in range(8):
                mk = proj_accum(pr, ps_k, wA, DM, m, 128, fb, 512, 0, 512,
                                [wk_t, fmv, (sv, psfree["k"])])
                kt, psfree["k"] = fmap(ps_k, bks[:, m:m + 1], (1536, 2048), ssc, mk, bk_t)
                mv_ = proj_accum(pr, ps_v, wB, DM, m, 128, fb, 512, 0, 512,
                                 [wv_t, fmv, (sv, psfree["v"])])
                v5 = ctx.ev("vector", lambda e, m=m: e.tensor_scalar(
                    vr, ps_v[:], bvs[:, m:m + 1], None, ALU.add),
                    deps=[(st, mv_), bv_t])
                psfree["v"] = v5
                ctx.ev("vector", lambda e: e.tensor_tensor(kv, ktil, vr, ALU.mult),
                       deps=[(sv, kt)])
                if bi == 0:
                    ctx.ev("vector", lambda e, m=m: e.reduce_sum(
                        kvm[:, m:m + 1].unsqueeze(2),
                        kv.rearrange("p (a b) -> p a b", a=1), axis=AX.X))
                    ctx.ev("vector", lambda e, m=m: e.reduce_sum(
                        ksm[:, m:m + 1].unsqueeze(2),
                        ktil.rearrange("p (a b) -> p a b", a=1), axis=AX.X))
                else:
                    ctx.ev("vector", lambda e: e.reduce_sum(
                        tcol[:, 0:1].unsqueeze(2),
                        kv.rearrange("p (a b) -> p a b", a=1), axis=AX.X))
                    ctx.ev("vector", lambda e, m=m: e.tensor_add(
                        kvm[:, m:m + 1], kvm[:, m:m + 1], tcol[:, 0:1]))
                    ctx.ev("vector", lambda e: e.reduce_sum(
                        tcol[:, 1:2].unsqueeze(2),
                        ktil.rearrange("p (a b) -> p a b", a=1), axis=AX.X))
                    ctx.ev("vector", lambda e, m=m: e.tensor_add(
                        ksm[:, m:m + 1], ksm[:, m:m + 1], tcol[:, 1:2]))
            xfb_free[bi % 2] = st.n
        sums_done = sv.n

        wq_t = ctx.dma("gpsimd", lambda e: e.dma_start(
            out=wA[:].rearrange("p (c m) -> p c m", c=8),
            in_=wq_d.rearrange("(c p) m -> p c m", p=128)), deps=[(st, st.n)])
        for m in range(8):
            for oi in range(4):
                mq = proj_accum(pr, ps_q, wA, DM, m, 128, xfmo, TOK, oi * 512, 512,
                                [wq_t, fm_o, (sv, psfree["q"])])
                sl = (m * TOK + oi * 512, m * TOK + oi * 512 + 512)
                _, psfree["q"] = fmap(ps_q, bqs[:, m:m + 1], sl, qts, mq, bq_t)
        for m in range(8):
            for oi in range(4):
                sl = slice(m * TOK + oi * 512, m * TOK + oi * 512 + 512)
                ctx.ev("vector", lambda e, sl=sl, m=m: e.tensor_scalar(
                    u_, qts[:, sl], ksm[:, m:m + 1], 1e-6, ALU.mult, ALU.add),
                    deps=[(sv, sums_done)])
                ctx.ev("vector", lambda e: e.reciprocal(u_, u_))
                ctx.ev("vector", lambda e, sl=sl, m=m: e.tensor_scalar(
                    e_, qts[:, sl], kvm[:, m:m + 1], None, ALU.mult))
                ctx.ev("vector", lambda e, sl=sl: e.tensor_tensor(
                    qts[:, sl], e_, u_, ALU.mult))
        vp_done = sv.n

        wo_t = ctx.dma("gpsimd", lambda e: e.dma_start(
            out=wB[:].rearrange("p (c m) -> p c m", c=8),
            in_=wo_d.rearrange("(c p) m -> p c m", p=128)), deps=[(st, st.n)])

        s1 = gsc[:, 0:512]
        w_ = gsc[:, 512:1024]
        t_ = gsc[:, 1024:1536]
        ofree = {0: None, 1: None}
        yfree = {0: None, 1: None}
        ystore = {0: None, 1: None}
        for ti in range(16):
            bi = ti % 2
            xres = (xres0, xres1)[bi]
            ybuf = (ybuf0, ybuf1)[bi]
            lx = ctx.dma("sync", lambda e, xres=xres, ti=ti: e.dma_start(
                out=xres[:], in_=xo_d[ti * 128:ti * 128 + 128, :]),
                deps=[(sv, yfree[bi])])
            for hf in range(2):
                ps_o = (ps_o0, ps_o1)[hf]
                for c in range(8):
                    ctx.ev("tensor", lambda e, c=c, hf=hf, ti=ti, ps_o=ps_o: e.matmul(
                        ps_o[:], qts[:, c * TOK + ti * 128: c * TOK + ti * 128 + 128],
                        wB[:, c * DM + hf * 512: c * DM + hf * 512 + 512],
                        start=(c == 0), stop=False),
                        deps=([(sv, vp_done), wo_t, (sv, ofree[hf])] if c == 0 else []))
                mo = ctx.ev("tensor", lambda e, hf=hf, ps_o=ps_o: e.matmul(
                    ps_o[:], onesr[:], bos[:, hf * 512: hf * 512 + 512],
                    start=False, stop=True),
                    deps=[bo_t, on_t], inc=st)
                a1 = ctx.ev("scalar", lambda e, ps_o=ps_o: e.activation(
                    s1, ps_o[:], AF.Square), deps=[(st, mo)])
                ctx.ev("vector", lambda e, ps_o=ps_o: e.tensor_mul(w_, ps_o[:], s1),
                       deps=[(sa, a1)])
                ctx.ev("vector", lambda e, ps_o=ps_o: e.scalar_tensor_tensor(
                    w_, w_, 0.044715, ps_o[:], ALU.mult, ALU.add))
                a2 = ctx.ev("scalar", lambda e: e.activation(
                    t_, w_, AF.Tanh, scale=0.7978845608028654), deps=[(sv, sv.n)])
                ctx.ev("vector", lambda e: e.tensor_scalar(
                    t_, t_, 0.5, 0.5, ALU.mult, ALU.add), deps=[(sa, a2)])
                gf = ctx.ev("vector", lambda e, ps_o=ps_o, hf=hf, ybuf=ybuf: e.tensor_mul(
                    ybuf[:, hf * 512:hf * 512 + 512], t_, ps_o[:]),
                    deps=[ystore[bi]])
                ofree[hf] = gf
                ctx.ev("vector", lambda e, hf=hf, ybuf=ybuf, xres=xres: e.tensor_add(
                    ybuf[:, hf * 512:hf * 512 + 512], ybuf[:, hf * 512:hf * 512 + 512],
                    xres[:, hf * 512:hf * 512 + 512]),
                    deps=[lx])
            yv = sv.n
            yfree[bi] = yv
            ystore[bi] = ctx.dma("sync", lambda e, ybuf=ybuf, ti=ti: e.dma_start(
                out=y_d[ti * 128:ti * 128 + 128, :], in_=ybuf[:]),
                deps=[(sv, yv)])
        ctx.ev("sync", lambda e: e.nop(), deps=[sq.barrier(), gq.barrier()])
        ctx.flush(block)
    return nc



def build_l0_ffn_counts():
    nc = bass.Bass("TRN2", target_bir_lowering=False, debug=False)
    xo_d = nc.dram_tensor("xo", [TOK, DM], F32, kind="ExternalInput").ap()
    gw_d = nc.dram_tensor("gw", [8, DM], F32, kind="ExternalInput").ap()
    gb_d = nc.dram_tensor("gb", [8], F32, kind="ExternalInput").ap()
    ones_d = nc.dram_tensor("ones", [1, P], F32, kind="ExternalInput").ap()
    ident_d = nc.dram_tensor("ident", [P, P], F32, kind="ExternalInput").ap()
    cnt_d = nc.dram_tensor("cnt", [8, 1], F32, kind="ExternalOutput").ap()
    xlno_d = nc.dram_tensor("xlno", [TOK, DM], BF16, kind="Internal").ap()

    es = ExitStack()
    with es:
        xin0 = es.enter_context(nc.sbuf_tensor("xin0", [P, DM], F32))
        xin1 = es.enter_context(nc.sbuf_tensor("xin1", [P, DM], F32))
        xst = es.enter_context(nc.sbuf_tensor("xst", [P, 12], F32))
        xmv = es.enter_context(nc.sbuf_tensor("xmv", [P, 2], F32))
        xsd = es.enter_context(nc.sbuf_tensor("xsd", [P, 2], F32))
        xln0 = es.enter_context(nc.sbuf_tensor("xln0", [P, DM], BF16))
        xln1 = es.enter_context(nc.sbuf_tensor("xln1", [P, DM], BF16))
        epsc = es.enter_context(nc.sbuf_tensor("epsc", [P, 1], F32))
        xfmo = es.enter_context(nc.sbuf_tensor("xfmo", [P, 8 * TOK], BF16))
        gwb = es.enter_context(nc.sbuf_tensor("gwb", [8, DM], BF16))
        gwt = es.enter_context(nc.sbuf_tensor("gwt", [P, 8 * 8], BF16))
        gbr = es.enter_context(nc.sbuf_tensor("gbr", [1, 8], BF16))
        idt = es.enter_context(nc.sbuf_tensor("idt", [P, P], BF16))
        onesr = es.enter_context(nc.sbuf_tensor("onesr", [1, P], BF16))
        onec = es.enter_context(nc.sbuf_tensor("onec", [P, 1], BF16))
        gbuf = es.enter_context(nc.sbuf_tensor("gbuf", [P, 8], F32))
        mx8 = es.enter_context(nc.sbuf_tensor("mx8", [P, 8], F32))
        indb = es.enter_context(nc.sbuf_tensor("indb", [P, 8], BF16))
        cbuf = es.enter_context(nc.sbuf_tensor("cbuf", [8, 1], F32))
        ps_g = es.enter_context(nc.psum_tensor("ps_g", [P, 8], F32))
        ps_t = es.enter_context(nc.psum_tensor("ps_t", [P, 8], BF16))
        ps_c = es.enter_context(nc.psum_tensor("ps_c", [8, 1], F32))
        h_qs = [es.enter_context(nc.semaphore("qs%d" % i)) for i in range(4)]
        h_qg = es.enter_context(nc.semaphore("qg"))
        h_sv = es.enter_context(nc.semaphore("sv"))
        h_sa = es.enter_context(nc.semaphore("sa"))
        h_st = es.enter_context(nc.semaphore("st"))
        block = es.enter_context(nc.Block())
        ctx = Ctx(nc)
        sq = Qu([Sem(h, "qs%d" % i) for i, h in enumerate(h_qs)])
        gq = Qu([Sem(h_qg, "qg")])
        sv = Sem(h_sv, "sv"); sa = Sem(h_sa, "sa"); st = Sem(h_st, "st")
        ctx.queues = {"sync": sq, "gpsimd": gq}
        ctx.engsem = {"vector": (sv, 1), "scalar": (sa, 1)}
        t = dict(xin0=xin0, xin1=xin1, xst=xst, xmv=xmv, xsd=xsd, xln0=xln0, xln1=xln1,
                 epsc=epsc)
        pr = Prog(nc, ctx, sq, gq, sv, sa, st, t)
        ctx.ev("vector", lambda e: e.memset(epsc[:], 1e-5))
        ctx.ev("vector", lambda e: e.memset(onec[:], 1.0))

        gw_t = ctx.dma("gpsimd", lambda e: e.dma_start(out=gwb[:], in_=gw_d[:]))
        gb_t = ctx.dma("gpsimd", lambda e: e.dma_start(out=gbr[:], in_=gb_d.unsqueeze(0)))
        id_t = ctx.dma("gpsimd", lambda e: e.dma_start(out=idt[:], in_=ident_d[:]))
        on_t = ctx.dma("gpsimd", lambda e: e.dma_start(out=onesr[:], in_=ones_d[:]))

        tfree = None
        for c in range(8):
            tc = ctx.ev("tensor", lambda e, c=c: e.transpose(
                ps_t[:], gwb[0:8, c * 128:(c + 1) * 128], idt[0:8, 0:8]),
                deps=[gw_t, id_t, (sv, tfree)], inc=st)
            tfree = ctx.ev("vector", lambda e, c=c: e.tensor_copy(
                gwt[:, c * 8:(c + 1) * 8], ps_t[:]), deps=[(st, tc)])
        gwt_v = sv.n

        emit_ln(pr, xo_d, 0, 16, xlno_d)
        ln_o = sq.barrier()
        fm_o = emit_fm_loads(pr, xfmo, xlno_d, TOK, ln_o)

        gfree = None
        for ti in range(16):
            for c in range(8):
                ctx.ev("tensor", lambda e, c=c, ti=ti: e.matmul(
                    ps_g[:], xfmo[:, c * TOK + ti * 128: c * TOK + ti * 128 + 128],
                    gwt[:, c * 8:(c + 1) * 8], start=(c == 0), stop=False),
                    deps=([fm_o, (sv, gwt_v), (sv, gfree)] if c == 0 else []))
            mg = ctx.ev("tensor", lambda e: e.matmul(
                ps_g[:], onesr[:], gbr[:], start=False, stop=True),
                deps=[gb_t, on_t], inc=st)
            gfree = ctx.ev("vector", lambda e: e.tensor_copy(gbuf[:], ps_g[:]),
                           deps=[(st, mg)])
            ctx.ev("vector", lambda e: e.max(out=mx8[:], in_=gbuf[:]))
            iv = ctx.ev("vector", lambda e: e.tensor_scalar(
                indb[:], gbuf[:], mx8[:, 1:2], None, ALU.is_ge))
            ctx.ev("tensor", lambda e, ti=ti: e.matmul(
                ps_c[:], indb[:], onec[:], start=(ti == 0), stop=(ti == 15)),
                deps=[(sv, iv)], inc=(st if ti == 15 else None))
        cv = ctx.ev("vector", lambda e: e.tensor_copy(cbuf[:], ps_c[:]), deps=[(st, st.n)])
        ctx.dma("sync", lambda e: e.dma_start(out=cnt_d[:], in_=cbuf[:]), deps=[(sv, cv)])
        ctx.ev("sync", lambda e: e.nop(), deps=[sq.barrier(), gq.barrier()])
        ctx.flush(block)
    return nc


def build_l0_ffn_main():
    nc = bass.Bass("TRN2", target_bir_lowering=False, debug=False)
    xo_d = nc.dram_tensor("xo", [TOK, DM], F32, kind="ExternalInput").ap()
    parts_d = nc.dram_tensor("parts", [8, 8], F32, kind="ExternalInput").ap()
    we_d = nc.dram_tensor("we", [8, DM, DM], F32, kind="ExternalInput").ap()
    web_d = nc.dram_tensor("web", [1, 8 * DM], F32, kind="ExternalInput").ap()
    ones_d = nc.dram_tensor("ones", [1, P], F32, kind="ExternalInput").ap()
    onesc8_d = nc.dram_tensor("onesc8", [8, 1], F32, kind="ExternalInput").ap()
    y_d = nc.dram_tensor("y", [TOK, DM], F32, kind="ExternalOutput").ap()
    xlno_d = nc.dram_tensor("xlno", [TOK, DM], BF16, kind="Internal").ap()
    webf_d = nc.dram_tensor("webf", [8, DM, DM], BF16, kind="Internal").ap()

    es = ExitStack()
    with es:
        xin0 = es.enter_context(nc.sbuf_tensor("xin0", [P, DM], F32))
        xin1 = es.enter_context(nc.sbuf_tensor("xin1", [P, DM], F32))
        xst = es.enter_context(nc.sbuf_tensor("xst", [P, 12], F32))
        xmv = es.enter_context(nc.sbuf_tensor("xmv", [P, 2], F32))
        xsd = es.enter_context(nc.sbuf_tensor("xsd", [P, 2], F32))
        xln0 = es.enter_context(nc.sbuf_tensor("xln0", [P, DM], BF16))
        xln1 = es.enter_context(nc.sbuf_tensor("xln1", [P, DM], BF16))
        epsc = es.enter_context(nc.sbuf_tensor("epsc", [P, 1], F32))
        xfmo = es.enter_context(nc.sbuf_tensor("xfmo", [P, 8 * TOK], BF16))
        xacc = es.enter_context(nc.sbuf_tensor("xacc", [P, 16 * DM], F32))
        wetA = es.enter_context(nc.sbuf_tensor("wetA", [P, 8 * DM], BF16))
        wetB = es.enter_context(nc.sbuf_tensor("wetB", [P, 8 * DM], BF16))
        webr = es.enter_context(nc.sbuf_tensor("webr", [1, 8 * DM], BF16))
        parts_s = es.enter_context(nc.sbuf_tensor("parts_s", [8, 8], F32))
        onesf = es.enter_context(nc.sbuf_tensor("onesf", [1, P], F32))
        onesr = es.enter_context(nc.sbuf_tensor("onesr", [1, P], BF16))
        oc8 = es.enter_context(nc.sbuf_tensor("oc8", [8, 1], F32))
        crow = es.enter_context(nc.sbuf_tensor("crow", [1, 8], F32))
        cbc = es.enter_context(nc.sbuf_tensor("cbc", [P, 8], F32))
        etmp0 = es.enter_context(nc.sbuf_tensor("etmp0", [P, 512], F32))
        etmp1 = es.enter_context(nc.sbuf_tensor("etmp1", [P, 512], F32))
        ps_e0 = es.enter_context(nc.psum_tensor("ps_e0", [P, 512], F32))
        ps_e1 = es.enter_context(nc.psum_tensor("ps_e1", [P, 512], F32))
        ps_m = es.enter_context(nc.psum_tensor("ps_m", [P, 8], F32))
        h_qs = [es.enter_context(nc.semaphore("qs%d" % i)) for i in range(4)]
        h_qg = es.enter_context(nc.semaphore("qg"))
        h_sv = es.enter_context(nc.semaphore("sv"))
        h_sa = es.enter_context(nc.semaphore("sa"))
        h_st = es.enter_context(nc.semaphore("st"))
        block = es.enter_context(nc.Block())
        ctx = Ctx(nc)
        sq = Qu([Sem(h, "qs%d" % i) for i, h in enumerate(h_qs)])
        gq = Qu([Sem(h_qg, "qg")])
        sv = Sem(h_sv, "sv"); sa = Sem(h_sa, "sa"); st = Sem(h_st, "st")
        ctx.queues = {"sync": sq, "gpsimd": gq}
        ctx.engsem = {"vector": (sv, 1), "scalar": (sa, 1)}
        t = dict(xin0=xin0, xin1=xin1, xst=xst, xmv=xmv, xsd=xsd, xln0=xln0, xln1=xln1,
                 epsc=epsc)
        pr = Prog(nc, ctx, sq, gq, sv, sa, st, t)
        ctx.ev("vector", lambda e: e.memset(epsc[:], 1e-5))

        pt_t = ctx.dma("sync", lambda e: e.dma_start(out=parts_s[:], in_=parts_d[:]))
        of_t = ctx.dma("sync", lambda e: e.dma_start(out=onesf[:], in_=ones_d[:]))
        o8_t = ctx.dma("sync", lambda e: e.dma_start(out=oc8[:], in_=onesc8_d[:]))
        on_t = ctx.dma("gpsimd", lambda e: e.dma_start(out=onesr[:], in_=ones_d[:]))
        wb_t = ctx.dma("gpsimd", lambda e: e.dma_start(out=webr[:], in_=web_d[:]))
        m1 = ctx.ev("tensor", lambda e: e.matmul(
            ps_m[0:1, 0:8], oc8[:], parts_s[:], start=True, stop=True),
            deps=[pt_t, o8_t], inc=st)
        c1 = ctx.ev("vector", lambda e: e.tensor_scalar(
            crow[:], ps_m[0:1, 0:8], 1.0 / 32768.0, None, ALU.mult), deps=[(st, m1)])
        m2 = ctx.ev("tensor", lambda e: e.matmul(
            ps_m[:, 0:8], onesf[:], crow[:], start=True, stop=True),
            deps=[of_t, (sv, c1)], inc=st)
        cb_v = ctx.ev("vector", lambda e: e.tensor_copy(cbc[:], ps_m[:, 0:8]),
                      deps=[(st, m2)])

        we_t = ctx.dma("gpsimd", lambda e: e.dma_start(
            out=webf_d.rearrange("e a b -> e a b"), in_=we_d.rearrange("e a b -> e a b")))

        emit_ln(pr, xo_d, 0, 16, xlno_d)
        ln_o = sq.barrier()
        fm_o = emit_fm_loads(pr, xfmo, xlno_d, TOK, ln_o)

        x_toks = []
        for ti in range(16):
            x_toks.append(ctx.dma("sync", lambda e, ti=ti: e.dma_start(
                out=xacc[:, ti * DM:(ti + 1) * DM], in_=xo_d[ti * 128:ti * 128 + 128, :])))

        wet_tok = {0: None, 1: None}
        wet_free = {0: None, 1: None}
        psfree = {0: None, 1: None}
        etfree = {0: None, 1: None}
        for ei in range(8):
            wet = (wetA, wetB)[ei % 2]
            toks = []
            for c in range(8):
                toks.append(ctx.dma("sync", lambda e, c=c, ei=ei, wet=wet: e.dma_start_transpose(
                    wet[:, c * DM:(c + 1) * DM],
                    webf_d[ei, 0:DM, c * 128:(c + 1) * 128]),
                    deps=[we_t, (st, wet_free[ei % 2])]))
            wet_tok[ei % 2] = toks
            for ti in range(16):
                for hf in range(2):
                    ps_e = (ps_e0, ps_e1)[hf]
                    for c in range(8):
                        ctx.ev("tensor", lambda e, c=c, ti=ti, hf=hf, ps_e=ps_e, wet=wet: e.matmul(
                            ps_e[:], xfmo[:, c * TOK + ti * 128: c * TOK + ti * 128 + 128],
                            wet[:, c * DM + hf * 512: c * DM + hf * 512 + 512],
                            start=(c == 0), stop=False),
                            deps=([toks, fm_o, (sa, psfree[hf])] if c == 0 else []))
                    me = ctx.ev("tensor", lambda e, ei=ei, hf=hf, ps_e=ps_e: e.matmul(
                        ps_e[:], onesr[:], webr[0:1, ei * DM + hf * 512: ei * DM + hf * 512 + 512],
                        start=False, stop=True),
                        deps=[wb_t, on_t], inc=st)
                    etmp = (etmp0, etmp1)[hf]
                    a1 = ctx.ev("scalar", lambda e, ps_e=ps_e, ei=ei, etmp=etmp: e.activation(
                        etmp[:], ps_e[:], AF.Relu, scale=cbc[:, ei:ei + 1]),
                        deps=[(st, me), (sv, cb_v), (sv, etfree[hf])])
                    psfree[hf] = a1
                    etfree[hf] = ctx.ev("vector", lambda e, ti=ti, hf=hf, etmp=etmp: e.tensor_add(
                        xacc[:, ti * DM + hf * 512: ti * DM + hf * 512 + 512],
                        xacc[:, ti * DM + hf * 512: ti * DM + hf * 512 + 512], etmp[:]),
                        deps=[(sa, a1), x_toks[ti]])
            wet_free[ei % 2] = st.n
        acc_v = sv.n
        for ti in range(16):
            ctx.dma("sync", lambda e, ti=ti: e.dma_start(
                out=y_d[ti * 128:ti * 128 + 128, :], in_=xacc[:, ti * DM:(ti + 1) * DM]),
                deps=[(sv, acc_v)])
        ctx.ev("sync", lambda e: e.nop(), deps=[sq.barrier(), gq.barrier()])
        ctx.flush(block)
    return nc


def run_l0_ffn(x, ins):
    nc_a = _get_prog("l0f_cnt", build_l0_ffn_counts)
    nc_b = _get_prog("l0f_main", build_l0_ffn_main)
    ident = np.eye(P, dtype=np.float32)
    in_a = []
    for c in range(NCORE):
        b, p = c // 2, c % 2
        in_a.append({
            "xo": np.ascontiguousarray(x[b][own_rows(p)]),
            "gw": ins["L0_ffn_gate_w"], "gb": ins["L0_ffn_gate_b"],
            "ones": _ONES, "ident": ident,
        })
    res_a = run_bass_kernel_spmd(nc_a, in_a, core_ids=list(range(NCORE)))
    parts = np.stack([res_a.results[c]["cnt"][:, 0] for c in range(NCORE)])
    web = np.ascontiguousarray(ins["L0_ffn_expert_b"].reshape(1, 8 * DM))
    in_b = []
    for c in range(NCORE):
        b, p = c // 2, c % 2
        in_b.append({
            "xo": in_a[c]["xo"],
            "parts": parts, "we": ins["L0_ffn_expert_w"], "web": web,
            "ones": _ONES, "onesc8": np.ones((8, 1), np.float32),
        })
    res_b = run_bass_kernel_spmd(nc_b, in_b, core_ids=list(range(NCORE)))
    y = np.empty((NB, S, DM), np.float32)
    return _scatter(y, [r["y"] for r in res_b.results])



def build_l1_attn_a():
    nc = bass.Bass("TRN2", target_bir_lowering=False, debug=False)
    x_d = nc.dram_tensor("x", [S, DM], F32, kind="ExternalInput").ap()
    xo_d = nc.dram_tensor("xo", [TOK, DM], F32, kind="ExternalInput").ap()
    wa_d = nc.dram_tensor("Wa", [DM, 3 * DM], F32, kind="ExternalInput").ap()
    bqk_d = nc.dram_tensor("bqk", [P, 16], F32, kind="ExternalInput").ap()
    bav_d = nc.dram_tensor("bav", [1, DM], F32, kind="ExternalInput").ap()
    wp_d = nc.dram_tensor("Wp", [DM, DM], F32, kind="ExternalInput").ap()
    bp_d = nc.dram_tensor("bp", [1, DM], F32, kind="ExternalInput").ap()
    ones_d = nc.dram_tensor("ones", [1, P], F32, kind="ExternalInput").ap()
    identf_d = nc.dram_tensor("identf", [P, P], F32, kind="ExternalInput").ap()
    mmq_d = nc.dram_tensor("mmq", [P, 4 * 256], F32, kind="ExternalInput").ap()
    y_d = nc.dram_tensor("y", [TOK, DM], F32, kind="ExternalOutput").ap()
    xln_d = nc.dram_tensor("xln", [S, DM], BF16, kind="Internal").ap()
    xlno_d = nc.dram_tensor("xlno", [TOK, DM], BF16, kind="Internal").ap()
    k_d = nc.dram_tensor("kst", [8, P, S], BF16, kind="Internal").ap()
    mrow_d = nc.dram_tensor("mrowd", [32, P], F32, kind="Internal").ap()
    v_d = nc.dram_tensor("vst", [S, DM], BF16, kind="Internal").ap()

    es = ExitStack()
    with es:
        xin0 = es.enter_context(nc.sbuf_tensor("xin0", [P, DM], F32))
        xin1 = es.enter_context(nc.sbuf_tensor("xin1", [P, DM], F32))
        xst = es.enter_context(nc.sbuf_tensor("xst", [P, 12], F32))
        xmv = es.enter_context(nc.sbuf_tensor("xmv", [P, 2], F32))
        xsd = es.enter_context(nc.sbuf_tensor("xsd", [P, 2], F32))
        xln0 = es.enter_context(nc.sbuf_tensor("xln0", [P, DM], BF16))
        xln1 = es.enter_context(nc.sbuf_tensor("xln1", [P, DM], BF16))
        epsc = es.enter_context(nc.sbuf_tensor("epsc", [P, 1], F32))
        xfmo = es.enter_context(nc.sbuf_tensor("xfmo", [P, 8 * TOK], BF16))
        xfb0 = es.enter_context(nc.sbuf_tensor("xfb0", [P, 8 * 512], BF16))
        wA = es.enter_context(nc.sbuf_tensor("wA", [P, 8 * DM], BF16))
        wB = es.enter_context(nc.sbuf_tensor("wB", [P, 8 * DM], BF16))
        bqk = es.enter_context(nc.sbuf_tensor("bqks", [P, 16], F32))
        bavr = es.enter_context(nc.sbuf_tensor("bavr", [1, DM], BF16))
        bpr = es.enter_context(nc.sbuf_tensor("bpr", [1, DM], BF16))
        onesr = es.enter_context(nc.sbuf_tensor("onesr", [1, P], BF16))
        onesf = es.enter_context(nc.sbuf_tensor("onesf", [1, P], F32))
        identf = es.enter_context(nc.sbuf_tensor("identfs", [P, P], F32))
        qstE = es.enter_context(nc.sbuf_tensor("qstE", [65, 8 * TOK], BF16))
        qstO = es.enter_context(nc.sbuf_tensor("qstO", [65, 8 * TOK], BF16))
        kcur0 = es.enter_context(nc.sbuf_tensor("kcur0", [65, S], BF16))
        kcur1 = es.enter_context(nc.sbuf_tensor("kcur1", [65, S], BF16))
        vg0 = es.enter_context(nc.sbuf_tensor("vg0", [P, 32 * 65], BF16))
        vg1 = es.enter_context(nc.sbuf_tensor("vg1", [P, 32 * 65], BF16))
        mmq = es.enter_context(nc.sbuf_tensor("mmqs", [P, 4 * 256], BF16))
        m_all = es.enter_context(nc.sbuf_tensor("m_all", [P, 32], F32))
        mcols = es.enter_context(nc.sbuf_tensor("mcols", [P, 8], F32))
        mrowt = es.enter_context(nc.sbuf_tensor("mrowt", [32, P], F32))
        ptb0 = es.enter_context(nc.sbuf_tensor("ptb0", [P, 256], BF16))
        ptb1 = es.enter_context(nc.sbuf_tensor("ptb1", [P, 256], BF16))
        lrow = es.enter_context(nc.sbuf_tensor("lrow", [1, 256], F32))
        invlb = es.enter_context(nc.sbuf_tensor("invlb", [64, 256], F32))
        xres0 = es.enter_context(nc.sbuf_tensor("xres0", [P, DM], F32))
        xres1 = es.enter_context(nc.sbuf_tensor("xres1", [P, DM], F32))
        ybuf0 = es.enter_context(nc.sbuf_tensor("ybuf0", [P, DM], F32))
        ybuf1 = es.enter_context(nc.sbuf_tensor("ybuf1", [P, DM], F32))
        psA = es.enter_context(nc.psum_tensor("psA", [P, 512], F32))
        psB = es.enter_context(nc.psum_tensor("psB", [P, 512], F32))
        psP = es.enter_context(nc.psum_tensor("psP", [P, 512], F32))
        psP2 = es.enter_context(nc.psum_tensor("psP2", [P, 512], F32))
        psT0 = es.enter_context(nc.psum_tensor("psT0", [P, 256], F32))
        psT1 = es.enter_context(nc.psum_tensor("psT1", [P, 256], F32))
        psO0 = es.enter_context(nc.psum_tensor("psO0", [P, 512], F32))
        psO1 = es.enter_context(nc.psum_tensor("psO1", [P, 512], F32))
        h_qs = [es.enter_context(nc.semaphore("qs%d" % i)) for i in range(4)]
        h_qg = es.enter_context(nc.semaphore("qg"))
        h_sv = es.enter_context(nc.semaphore("sv"))
        h_sa = es.enter_context(nc.semaphore("sa"))
        h_st = es.enter_context(nc.semaphore("st"))
        block = es.enter_context(nc.Block())
        ctx = Ctx(nc)
        sq = Qu([Sem(h, "qs%d" % i) for i, h in enumerate(h_qs)])
        gq = Qu([Sem(h_qg, "qg")])
        sv = Sem(h_sv, "sv"); sa = Sem(h_sa, "sa"); st = Sem(h_st, "st")
        ctx.queues = {"sync": sq, "gpsimd": gq}
        ctx.engsem = {"vector": (sv, 1), "scalar": (sa, 1)}
        t = dict(xin0=xin0, xin1=xin1, xst=xst, xmv=xmv, xsd=xsd, xln0=xln0, xln1=xln1,
                 epsc=epsc)
        pr = Prog(nc, ctx, sq, gq, sv, sa, st, t)
        ctx.ev("vector", lambda e: e.memset(epsc[:], 1e-5))
        ctx.ev("vector", lambda e: e.memset(kcur0[64:65, :], 1.0))
        kcinit = ctx.ev("vector", lambda e: e.memset(kcur1[64:65, :], 1.0))
        ctx.ev("vector", lambda e: e.memset(vg0[:], 1.0))
        vginit = ctx.ev("vector", lambda e: e.memset(vg1[:], 1.0))

        bqk_t = ctx.dma("sync", lambda e: e.dma_start(out=bqk[:], in_=bqk_d[:]))
        onf_t = ctx.dma("sync", lambda e: e.dma_start(out=onesf[:], in_=ones_d[:]))
        idf_t = ctx.dma("sync", lambda e: e.dma_start(out=identf[:], in_=identf_d[:]))
        bav_t = ctx.dma("gpsimd", lambda e: e.dma_start(out=bavr[:], in_=bav_d[:]))
        bp_t = ctx.dma("gpsimd", lambda e: e.dma_start(out=bpr[:], in_=bp_d[:]))
        on_t = ctx.dma("gpsimd", lambda e: e.dma_start(out=onesr[:], in_=ones_d[:]))
        mm_t = ctx.dma("gpsimd", lambda e: e.dma_start(out=mmq[:], in_=mmq_d[:]))

        emit_ln(pr, x_d, 0, 32, xln_d)
        ln_b = sq.barrier()
        emit_ln(pr, xo_d, 0, 16, xlno_d)
        ln_o = sq.barrier()
        fm_o = emit_fm_loads(pr, xfmo, xlno_d, TOK, ln_o)
        ktile = xln1[:, 0:512]
        vtile = xln0

        wq_t = ctx.dma("gpsimd", lambda e: e.dma_start(
            out=wA[:].rearrange("p (c m) -> p c m", c=8),
            in_=wa_d[:, 0:DM].rearrange("(c p) m -> p c m", p=128)))
        qev = None
        for m in range(8):
            for ob in range(4):
                mq = proj_accum(pr, psP, wA, DM, m, 128, xfmo, TOK, ob * 512, 512,
                                [wq_t, fm_o, (sv, qev)])
                for hh in range(2):
                    qst = (qstE, qstO)[hh]
                    qev = ctx.ev("vector", lambda e, m=m, ob=ob, hh=hh, qst=qst: e.tensor_scalar(
                        qst[0:64, m * TOK + ob * 512: m * TOK + ob * 512 + 512],
                        psP[hh * 64:hh * 64 + 64, :], bqk[hh * 64:hh * 64 + 64, m:m + 1],
                        None, ALU.add),
                        deps=[(st, mq), bqk_t])
        q_mm_done = st.n

        wv_t = ctx.dma("gpsimd", lambda e: e.dma_start(
            out=wA[:].rearrange("p (c m) -> p c m", c=8),
            in_=wa_d[:, 2 * DM:3 * DM].rearrange("(c p) m -> p c m", p=128)),
            deps=[(st, q_mm_done)])
        wk_t = ctx.dma("gpsimd", lambda e: e.dma_start(
            out=wB[:].rearrange("p (c m) -> p c m", c=8),
            in_=wa_d[:, DM:2 * DM].rearrange("(c p) m -> p c m", p=128)))
        xfb_free = {0: None, 1: None}
        kev = qev
        kst_tok = None
        vst_tok = None
        vev = {0: None, 1: None}
        for bi in range(8):
            fb = xfb0
            fmv = []
            for c in range(8):
                fmv.append(ctx.dma("sync", lambda e, c=c, bi=bi, fb=fb: e.dma_start_transpose(
                    fb[:, c * 512:(c + 1) * 512],
                    xln_d[bi * 512:(bi + 1) * 512, c * 128:(c + 1) * 128]),
                    deps=[ln_b, (st, xfb_free[0])]))
            for m in range(8):
                mk = proj_accum(pr, psP, wB, DM, m, 128, fb, 512, 0, 512,
                                [wk_t, fmv, (sv, kev)])
                kev = ctx.ev("vector", lambda e, m=m: e.tensor_scalar(
                    ktile[:], psP[:], bqk[:, 8 + m:8 + m + 1], None, ALU.add),
                    deps=[(st, mk), bqk_t, kst_tok] + ln_o)
                kst_tok = ctx.dma("sync", lambda e, m=m, bi=bi: e.dma_start(
                    out=k_d[m, :, bi * 512:(bi + 1) * 512], in_=ktile[:]),
                    deps=[(sv, kev)])
            for tt in range(4):
                for hf in range(2):
                    ps_v = (psA, psB)[hf]
                    for c in range(8):
                        ctx.ev("tensor", lambda e, c=c, tt=tt, hf=hf, ps_v=ps_v, fb=fb: e.matmul(
                            ps_v[:], fb[:, c * 512 + tt * 128: c * 512 + tt * 128 + 128],
                            wA[:, c * DM + hf * 512: c * DM + hf * 512 + 512],
                            start=(c == 0), stop=False),
                            deps=([wv_t, fmv, (sv, vev[hf])] if c == 0 else []))
                    mv = ctx.ev("tensor", lambda e, hf=hf, ps_v=ps_v: e.matmul(
                        ps_v[:], onesr[:], bavr[0:1, hf * 512: hf * 512 + 512],
                        start=False, stop=True), deps=[bav_t, on_t], inc=st)
                    vev[hf] = ctx.ev("vector", lambda e, hf=hf, ps_v=ps_v: e.tensor_copy(
                        vtile[:, hf * 512:hf * 512 + 512], ps_v[:]),
                        deps=[(st, mv), vst_tok] + ln_o)
                vst_tok = ctx.dma("sync", lambda e, bi=bi, tt=tt: e.dma_start(
                    out=v_d[bi * 512 + tt * 128: bi * 512 + tt * 128 + 128, :], in_=vtile[:]),
                    deps=[(sv, vev[0]), (sv, vev[1])])
            xfb_free[0] = st.n
        kv_store = sq.barrier()
        proj_done = st.n
        wh_toks = []
        for h in range(16):
            wHx = (wA, wB)[h // 8]
            wh_toks.append(ctx.dma("gpsimd", lambda e, h=h, wHx=wHx: e.dma_start(
                out=wHx[0:64, (h % 8) * DM:(h % 8) * DM + DM],
                in_=wp_d[h * 64:(h + 1) * 64, :]),
                deps=[(st, proj_done)]))

        kc_free = {0: None, 1: None}
        vg_free = {0: vginit, 1: vginit}
        ptfree = {0: None, 1: None}
        ptbfree = {0: None, 1: None}
        ovfree = {0: None, 1: None}
        bcfree = None
        lrfree = None
        tr_prev = None
        nm_prev = []
        hdone = None
        for h2 in range(8):
            kc_toks = []
            for hh in range(2):
                kc = (kcur0, kcur1)[hh]
                kc_toks.append(ctx.dma("sync", lambda e, h2=h2, hh=hh, kc=kc: e.dma_start(
                    out=kc[0:64, :], in_=k_d[h2, hh * 64:hh * 64 + 64, :]),
                    deps=[kv_store, (st, kc_free[hh])]))
            pafree = (st, tr_prev)
            mav = None
            first_red = True
            for hh in range(2):
                kc = (kcur0, kcur1)[hh]
                qst = (qstE, qstO)[hh]
                slot = h2 * TOK
                for i in range(16):
                    nb = (2 * i + 2 + 3) // 4
                    for b in range(nb):
                        cols = min(512, (2 * i + 2) * 128 - b * 512)
                        ma = ctx.ev("tensor", lambda e, kc=kc, qst=qst, slot=slot, i=i, b=b, cols=cols: e.matmul(
                            psA[:, 0:cols], qst[0:64, slot + i * 128: slot + i * 128 + 128],
                            kc[0:64, b * 512: b * 512 + cols], start=True, stop=True),
                            deps=[kc_toks[hh], pafree], inc=st)
                        pa_deps = [(st, ma)] + ([(st, tr_prev)] if first_red else [])
                        first_red = False
                        pv_ = ctx.ev("vector", lambda e, b=b, cols=cols: e.reduce_max(
                            mcols[:, b:b + 1].unsqueeze(2),
                            psA[:, 0:cols].rearrange("p (a b) -> p a b", a=1), axis=AX.X),
                            deps=pa_deps)
                        pafree = (sv, pv_)
                    mav = ctx.ev("vector", lambda e, hh=hh, i=i, nb=nb: e.reduce_max(
                        m_all[:, hh * 16 + i:hh * 16 + i + 1].unsqueeze(2),
                        mcols[:, 0:nb].rearrange("p (a b) -> p a b", a=1), axis=AX.X))
            tr = ctx.ev("tensor", lambda e: e.transpose(psB[0:32, 0:128], m_all[:], identf[:]),
                        deps=[(sv, mav), idf_t, (sv, bcfree)], inc=st)
            tr_prev = tr
            mr = ctx.ev("vector", lambda e: e.tensor_scalar(
                mrowt[:], psB[0:32, 0:128], -1.0, None, ALU.mult),
                deps=[(st, tr)] + nm_prev)
            mr_st = ctx.dma("sync", lambda e: e.dma_start(out=mrow_d[:], in_=mrowt[:]),
                            deps=[(sv, mr)])
            nm_toks = []
            for hh in range(2):
                qst = (qstE, qstO)[hh]
                nm_toks.append(ctx.dma("gpsimd", lambda e, hh=hh, qst=qst, h2=h2: e.dma_start(
                    out=qst[64:65, h2 * TOK: h2 * TOK + TOK],
                    in_=mrow_d[hh * 16:hh * 16 + 16, :].rearrange("a b -> (a b)").unsqueeze(0)),
                    deps=[mr_st]))
            nm_prev = list(nm_toks)
            for hh in range(2):
                kc = (kcur0, kcur1)[hh]
                qst = (qstE, qstO)[hh]
                h = 2 * h2 + hh
                slot = h2 * TOK
                vg = (vg0, vg1)[hh]
                vg_toks = []
                for qq in range(4):
                    vg_toks.append(ctx.dma("sync", lambda e, qq=qq, h=h, vg=vg: e.dma_start(
                        out=vg[:].rearrange("p (n d) -> p n d", n=32)[:, qq * 8:(qq + 1) * 8, 0:64],
                        in_=v_d[qq * 1024:(qq + 1) * 1024, h * 64:(h + 1) * 64]
                            .rearrange("(n p) d -> p n d", p=128)),
                        deps=[kv_store, (st, vg_free[hh]), (sv, vginit)]))
                for qb in range(8):
                    nkt = 4 * qb + 4
                    psO = (psO0, psO1)[qb % 2]
                    for kt in range(nkt):
                        psT = (psT0, psT1)[kt % 2]
                        ptb = (ptb0, ptb1)[kt % 2]
                        mt = ctx.ev("tensor", lambda e, kc=kc, qst=qst, slot=slot, qb=qb, kt=kt, psT=psT: e.matmul(
                            psT[:], kc[0:65, kt * 128: kt * 128 + 128],
                            qst[0:65, slot + qb * 256: slot + qb * 256 + 256],
                            start=True, stop=True),
                            deps=[nm_toks[hh], (sa, ptfree[kt % 2]), (sv, kcinit)], inc=st)
                        av = ctx.ev("scalar", lambda e, psT=psT, ptb=ptb: e.activation(
                            ptb[:], psT[:], AF.Exp, scale=0.125),
                            deps=[(st, mt), (st, ptbfree[kt % 2])])
                        ptfree[kt % 2] = av
                        lastop = (sa, av)
                        if kt >= 4 * qb:
                            mk = ctx.ev("vector", lambda e, ptb=ptb, kt=kt, qb=qb: e.tensor_mul(
                                ptb[:], ptb[:], mmq[:, (kt - 4 * qb) * 256: (kt - 4 * qb) * 256 + 256]),
                                deps=[(sa, av), mm_t])
                            lastop = (sv, mk)
                        mpv = ctx.ev("tensor", lambda e, vg=vg, kt=kt, ptb=ptb, psO=psO, nkt=nkt: e.matmul(
                            psO[0:65, 0:256], vg[:, kt * 65: kt * 65 + 65], ptb[:],
                            start=(kt == 0), stop=(kt == nkt - 1)),
                            deps=[lastop] + vg_toks + ([(sv, ovfree[qb % 2])] if kt == 0 else []),
                            inc=st)
                        ptbfree[kt % 2] = mpv
                    pv_done = st.n
                    c1 = ctx.ev("vector", lambda e, psO=psO: e.tensor_copy(lrow[:], psO[64:65, 0:256]),
                                deps=[(st, pv_done), (st, lrfree)])
                    c2 = ctx.ev("vector", lambda e: e.reciprocal(lrow[:], lrow[:]))
                    mb = ctx.ev("tensor", lambda e: e.matmul(
                        psB[0:64, 0:256], onesf[0:1, 0:64], lrow[:], start=True, stop=True),
                        deps=[(sv, c2), onf_t, (sv, bcfree)], inc=st)
                    lrfree = mb
                    c3 = ctx.ev("vector", lambda e: e.tensor_copy(invlb[:], psB[0:64, 0:256]),
                                deps=[(st, mb)])
                    bcfree = c3
                    hv = ctx.ev("vector", lambda e, psO=psO, qst=qst, slot=slot, qb=qb: e.tensor_mul(
                        qst[0:64, slot + qb * 256: slot + qb * 256 + 256],
                        psO[0:64, 0:256], invlb[:]), deps=[(sv, c3)])
                    ovfree[qb % 2] = hv
                    hdone = hv
                vg_free[hh] = st.n
            kc_free = {0: st.n, 1: st.n}

        yfree = {0: None, 1: None}
        ystore = {0: None, 1: None}
        opfree = {0: None, 1: None}
        for ti in range(16):
            bi = ti % 2
            xres = (xres0, xres1)[bi]
            ybuf = (ybuf0, ybuf1)[bi]
            lx = ctx.dma("sync", lambda e, xres=xres, ti=ti: e.dma_start(
                out=xres[:], in_=xo_d[ti * 128:ti * 128 + 128, :]),
                deps=[(sv, yfree[bi])])
            for hf in range(2):
                ps_y = (psP, psP2)[hf]
                for h in range(16):
                    qst = (qstE, qstO)[h % 2]
                    wHx = (wA, wB)[h // 8]
                    ctx.ev("tensor", lambda e, h=h, ti=ti, hf=hf, ps_y=ps_y, qst=qst, wHx=wHx: e.matmul(
                        ps_y[:], qst[0:64, (h // 2) * TOK + ti * 128: (h // 2) * TOK + ti * 128 + 128],
                        wHx[0:64, (h % 8) * DM + hf * 512: (h % 8) * DM + hf * 512 + 512],
                        start=(h == 0), stop=False),
                        deps=(([(sv, hdone), (sv, opfree[hf])] + wh_toks) if h == 0 else []))
                my = ctx.ev("tensor", lambda e, hf=hf, ps_y=ps_y: e.matmul(
                    ps_y[:], onesr[:], bpr[0:1, hf * 512: hf * 512 + 512],
                    start=False, stop=True), deps=[bp_t], inc=st)
                opfree[hf] = ctx.ev("vector", lambda e, ps_y=ps_y, hf=hf, ybuf=ybuf, xres=xres: e.tensor_add(
                    ybuf[:, hf * 512:hf * 512 + 512], ps_y[:], xres[:, hf * 512:hf * 512 + 512]),
                    deps=[(st, my), lx, ystore[bi]])
            yv = sv.n
            yfree[bi] = yv
            ystore[bi] = ctx.dma("sync", lambda e, ybuf=ybuf, ti=ti: e.dma_start(
                out=y_d[ti * 128:ti * 128 + 128, :], in_=ybuf[:]),
                deps=[(sv, yv)])
        ctx.ev("sync", lambda e: e.nop(), deps=[sq.barrier(), gq.barrier()])
        ctx.flush(block)
    return nc


def run_l1_attn_a(x, ins):
    nc = _get_prog("l1aa", build_l1_attn_a)
    ba = ins["L1_attn_a_b_attn"]
    bqk = np.ascontiguousarray(np.concatenate(
        [_pc(ba[0:DM]), _pc(ba[DM:2 * DM])], axis=1))
    bav = np.ascontiguousarray(ba[2 * DM:3 * DM].reshape(1, DM))
    bp = np.ascontiguousarray(ins["L1_attn_a_b_proj"].reshape(1, DM))
    identf = np.eye(P, dtype=np.float32)
    tri = np.tril(np.ones((P, P), np.float32)).T
    onesm = np.ones((P, P), np.float32)
    zerosm = np.zeros((P, P), np.float32)
    in_maps = []
    for c in range(NCORE):
        b, p = c // 2, c % 2
        mm = np.empty((4, P, 256), np.float32)
        for dkt in range(4):
            left = tri if dkt == p else (onesm if dkt < p else zerosm)
            right = tri if dkt == 2 + p else (onesm if dkt < 2 + p else zerosm)
            mm[dkt, :, 0:128] = left
            mm[dkt, :, 128:256] = right
        in_maps.append({
            "x": np.ascontiguousarray(x[b]),
            "xo": np.ascontiguousarray(x[b][own_rows(p)]),
            "Wa": ins["L1_attn_a_W_attn"], "bqk": bqk, "bav": bav,
            "Wp": ins["L1_attn_a_W_proj"], "bp": bp,
            "ones": _ONES, "identf": identf,
            "mmq": np.ascontiguousarray(mm.transpose(1, 0, 2).reshape(P, 4 * 256)),
        })
    res = run_bass_kernel_spmd(nc, in_maps, core_ids=list(range(NCORE)))
    y = np.empty((NB, S, DM), np.float32)
    return _scatter(y, [r["y"] for r in res.results])



def build_l1_attn_b():
    nc = bass.Bass("TRN2", target_bir_lowering=False, debug=False)
    x_d = nc.dram_tensor("x", [S, DM], F32, kind="ExternalInput").ap()
    xo_d = nc.dram_tensor("xo", [TOK, DM], F32, kind="ExternalInput").ap()
    sinb_d = nc.dram_tensor("sinb", [S, DM], F32, kind="ExternalInput").ap()
    cosb_d = nc.dram_tensor("cosb", [S, DM], F32, kind="ExternalInput").ap()
    sino_d = nc.dram_tensor("sino", [TOK, DM], F32, kind="ExternalInput").ap()
    coso_d = nc.dram_tensor("coso", [TOK, DM], F32, kind="ExternalInput").ap()
    wq_d = nc.dram_tensor("Wq", [DM, DM], F32, kind="ExternalInput").ap()
    wk_d = nc.dram_tensor("Wk", [DM, DM], F32, kind="ExternalInput").ap()
    wv_d = nc.dram_tensor("Wv", [DM, DM], F32, kind="ExternalInput").ap()
    wo_d = nc.dram_tensor("Wo", [DM, DM], F32, kind="ExternalInput").ap()
    bo_d = nc.dram_tensor("bo", [1, DM], F32, kind="ExternalInput").ap()
    ones_d = nc.dram_tensor("ones", [1, P], F32, kind="ExternalInput").ap()
    identf_d = nc.dram_tensor("identf", [P, P], F32, kind="ExternalInput").ap()
    mmq_d = nc.dram_tensor("mmq", [P, 4 * 256], F32, kind="ExternalInput").ap()
    y_d = nc.dram_tensor("y", [TOK, DM], F32, kind="ExternalOutput").ap()
    xln_d = nc.dram_tensor("xln", [S, DM], BF16, kind="Internal").ap()
    rxln_d = nc.dram_tensor("rxln", [S, DM], BF16, kind="Internal").ap()
    xlno_d = nc.dram_tensor("xlno", [TOK, DM], BF16, kind="Internal").ap()
    rxlno_d = nc.dram_tensor("rxlno", [TOK, DM], BF16, kind="Internal").ap()
    wbf_d = nc.dram_tensor("wbf", [4, DM, DM], BF16, kind="Internal").ap()
    k_d = nc.dram_tensor("kst", [8, P, S], BF16, kind="Internal").ap()
    v_d = nc.dram_tensor("vst", [S, DM], BF16, kind="Internal").ap()
    mrow_d = nc.dram_tensor("mrowd", [32, P], F32, kind="Internal").ap()

    es = ExitStack()
    with es:
        xin0 = es.enter_context(nc.sbuf_tensor("xin0", [P, DM], F32))
        xin1 = es.enter_context(nc.sbuf_tensor("xin1", [P, DM], F32))
        xst = es.enter_context(nc.sbuf_tensor("xst", [P, 12], F32))
        xmv = es.enter_context(nc.sbuf_tensor("xmv", [P, 2], F32))
        xsd = es.enter_context(nc.sbuf_tensor("xsd", [P, 2], F32))
        xln0 = es.enter_context(nc.sbuf_tensor("xln0", [P, DM], BF16))
        xln1 = es.enter_context(nc.sbuf_tensor("xln1", [P, DM], BF16))
        epsc = es.enter_context(nc.sbuf_tensor("epsc", [P, 1], F32))
        xfmo = es.enter_context(nc.sbuf_tensor("xfmo", [P, 8 * TOK], BF16))
        xfb0 = es.enter_context(nc.sbuf_tensor("xfb0", [P, 8 * 512], BF16))
        wA = es.enter_context(nc.sbuf_tensor("wA", [P, 8 * DM], BF16))
        wB = es.enter_context(nc.sbuf_tensor("wB", [P, 8 * DM], BF16))
        onesr = es.enter_context(nc.sbuf_tensor("oness", [1, P], BF16))
        onesf = es.enter_context(nc.sbuf_tensor("onesf", [1, P], F32))
        identf = es.enter_context(nc.sbuf_tensor("identfs", [P, P], F32))
        bor = es.enter_context(nc.sbuf_tensor("bor", [1, DM], BF16))
        qstE = es.enter_context(nc.sbuf_tensor("qstE", [65, 8 * TOK], BF16))
        qstO = es.enter_context(nc.sbuf_tensor("qstO", [65, 8 * TOK], BF16))
        kcur0 = es.enter_context(nc.sbuf_tensor("kcur0", [65, S], BF16))
        kcur1 = es.enter_context(nc.sbuf_tensor("kcur1", [65, S], BF16))
        vg0 = es.enter_context(nc.sbuf_tensor("vg0", [P, 32 * 65], BF16))
        vg1 = es.enter_context(nc.sbuf_tensor("vg1", [P, 32 * 65], BF16))
        mmq = es.enter_context(nc.sbuf_tensor("mmqs", [P, 4 * 256], BF16))
        m_all = es.enter_context(nc.sbuf_tensor("m_all", [P, 32], F32))
        mcols = es.enter_context(nc.sbuf_tensor("mcols", [P, 8], F32))
        mrowt = es.enter_context(nc.sbuf_tensor("mrowt", [32, P], F32))
        ptb0 = es.enter_context(nc.sbuf_tensor("ptb0", [P, 256], BF16))
        ptb1 = es.enter_context(nc.sbuf_tensor("ptb1", [P, 256], BF16))
        lrow = es.enter_context(nc.sbuf_tensor("lrow", [1, 256], F32))
        invlb = es.enter_context(nc.sbuf_tensor("invlb", [64, 256], F32))
        xres0 = es.enter_context(nc.sbuf_tensor("xres0", [P, DM], F32))
        xres1 = es.enter_context(nc.sbuf_tensor("xres1", [P, DM], F32))
        ybuf0 = es.enter_context(nc.sbuf_tensor("ybuf0", [P, DM], F32))
        ybuf1 = es.enter_context(nc.sbuf_tensor("ybuf1", [P, DM], F32))
        psA = es.enter_context(nc.psum_tensor("psA", [P, 512], F32))
        psB = es.enter_context(nc.psum_tensor("psB", [P, 512], F32))
        psP = es.enter_context(nc.psum_tensor("psP", [P, 512], F32))
        psP2 = es.enter_context(nc.psum_tensor("psP2", [P, 512], F32))
        psT0 = es.enter_context(nc.psum_tensor("psT0", [P, 256], F32))
        psT1 = es.enter_context(nc.psum_tensor("psT1", [P, 256], F32))
        psO0 = es.enter_context(nc.psum_tensor("psO0", [P, 512], F32))
        psO1 = es.enter_context(nc.psum_tensor("psO1", [P, 512], F32))
        h_qs = [es.enter_context(nc.semaphore("qs%d" % i)) for i in range(4)]
        h_qg = es.enter_context(nc.semaphore("qg"))
        h_sv = es.enter_context(nc.semaphore("sv"))
        h_sa = es.enter_context(nc.semaphore("sa"))
        h_st = es.enter_context(nc.semaphore("st"))
        block = es.enter_context(nc.Block())
        ctx = Ctx(nc)
        sq = Qu([Sem(h, "qs%d" % i) for i, h in enumerate(h_qs)])
        gq = Qu([Sem(h_qg, "qg")])
        sv = Sem(h_sv, "sv"); sa = Sem(h_sa, "sa"); st = Sem(h_st, "st")
        ctx.queues = {"sync": sq, "gpsimd": gq}
        ctx.engsem = {"vector": (sv, 1), "scalar": (sa, 1)}
        t = dict(xin0=xin0, xin1=xin1, xst=xst, xmv=xmv, xsd=xsd, xln0=xln0, xln1=xln1,
                 epsc=epsc)
        pr = Prog(nc, ctx, sq, gq, sv, sa, st, t)
        ctx.ev("vector", lambda e: e.memset(epsc[:], 1e-5))
        ctx.ev("vector", lambda e: e.memset(kcur0[64:65, :], 1.0))
        kcinit = ctx.ev("vector", lambda e: e.memset(kcur1[64:65, :], 1.0))
        ctx.ev("vector", lambda e: e.memset(vg0[:], 1.0))
        vginit = ctx.ev("vector", lambda e: e.memset(vg1[:], 1.0))

        onf_t = ctx.dma("sync", lambda e: e.dma_start(out=onesf[:], in_=ones_d[:]))
        idf_t = ctx.dma("sync", lambda e: e.dma_start(out=identf[:], in_=identf_d[:]))
        bo_t = ctx.dma("gpsimd", lambda e: e.dma_start(out=bor[:], in_=bo_d[:]))
        on_t = ctx.dma("gpsimd", lambda e: e.dma_start(out=onesr[:], in_=ones_d[:]))
        mm_t = ctx.dma("gpsimd", lambda e: e.dma_start(out=mmq[:], in_=mmq_d[:]))
        wcast = []
        for i, wd in enumerate((wq_d, wk_d, wv_d, wo_d)):
            wcast.append(ctx.dma("gpsimd", lambda e, i=i, wd=wd: e.dma_start(
                out=wbf_d[i], in_=wd[:])))

        sct_s, sct_c, xf, rot = xres1, ybuf1, xres0, ybuf0
        rx0, rx1 = xfb0[:, 0:DM], xfb0[:, DM:2 * DM]
        emit_ln_rope(pr, x_d, 0, 32, xln_d, rxln_d, sinb_d, cosb_d,
                     xf, rot, sct_s, sct_c, rx0, rx1)
        emit_ln_rope(pr, xo_d, 0, 16, xlno_d, rxlno_d, sino_d, coso_d,
                     xf, rot, sct_s, sct_c, rx0, rx1)
        ln_all = sq.barrier()
        fm_o = emit_fm_loads(pr, xfmo, rxlno_d, TOK, ln_all)

        wqt = emit_wt_loads(pr, wA, wbf_d[0], deps=[wcast[0]])
        qev = None
        for m in range(8):
            for ob in range(4):
                mq = proj_accum(pr, psP, wA, DM, m, 128, xfmo, TOK, ob * 512, 512,
                                wqt + [fm_o, (sv, qev)])
                for hh in range(2):
                    qst = (qstE, qstO)[hh]
                    qev = ctx.ev("vector", lambda e, m=m, ob=ob, hh=hh, qst=qst: e.tensor_copy(
                        qst[0:64, m * TOK + ob * 512: m * TOK + ob * 512 + 512],
                        psP[hh * 64:hh * 64 + 64, :]),
                        deps=[(st, mq)])
        q_mm_done = st.n

        wkt = emit_wt_loads(pr, wB, wbf_d[1], deps=[wcast[1]])
        ktile = xln1[:, 0:512]
        xfb_free = [q_mm_done]
        kev = qev
        kst_tok = None
        for bi in range(8):
            fb = xfb0
            fmv = []
            for c in range(8):
                fmv.append(ctx.dma("sync", lambda e, c=c, bi=bi, fb=fb: e.dma_start_transpose(
                    fb[:, c * 512:(c + 1) * 512],
                    rxln_d[bi * 512:(bi + 1) * 512, c * 128:(c + 1) * 128]),
                    deps=ln_all + [(st, xfb_free[0])]))
            for m in range(8):
                mk = proj_accum(pr, psP, wB, DM, m, 128, fb, 512, 0, 512,
                                wkt + fmv + [(sv, kev)])
                kev = ctx.ev("vector", lambda e, m=m: e.tensor_copy(ktile, psP[:]),
                             deps=[(st, mk), kst_tok] + ln_all)
                kst_tok = ctx.dma("sync", lambda e, m=m, bi=bi: e.dma_start(
                    out=k_d[m, :, bi * 512:(bi + 1) * 512], in_=ktile),
                    deps=[(sv, kev)])
            xfb_free[0] = st.n
        wvt = emit_wt_loads(pr, wA, wbf_d[2], deps=[wcast[2], (st, q_mm_done)])
        vtile = xln0
        vev = {0: None, 1: None}
        vst_tok = None
        for bi in range(8):
            fb = xfb0
            fmv = []
            for c in range(8):
                fmv.append(ctx.dma("sync", lambda e, c=c, bi=bi, fb=fb: e.dma_start_transpose(
                    fb[:, c * 512:(c + 1) * 512],
                    xln_d[bi * 512:(bi + 1) * 512, c * 128:(c + 1) * 128]),
                    deps=ln_all + [(st, xfb_free[0])]))
            for tt in range(4):
                for hf in range(2):
                    ps_v = (psA, psB)[hf]
                    for c in range(8):
                        ctx.ev("tensor", lambda e, c=c, tt=tt, hf=hf, ps_v=ps_v, fb=fb: e.matmul(
                            ps_v[:], fb[:, c * 512 + tt * 128: c * 512 + tt * 128 + 128],
                            wA[:, c * DM + hf * 512: c * DM + hf * 512 + 512],
                            start=(c == 0), stop=(c == 7)),
                            deps=(wvt + fmv + [(sv, vev[hf])] if c == 0 else []),
                            inc=(st if c == 7 else None))
                    mv = st.n
                    vev[hf] = ctx.ev("vector", lambda e, hf=hf, ps_v=ps_v: e.tensor_copy(
                        vtile[:, hf * 512:hf * 512 + 512], ps_v[:]),
                        deps=[(st, mv), vst_tok] + ln_all)
                vst_tok = ctx.dma("sync", lambda e, bi=bi, tt=tt: e.dma_start(
                    out=v_d[bi * 512 + tt * 128: bi * 512 + tt * 128 + 128, :], in_=vtile[:]),
                    deps=[(sv, vev[0]), (sv, vev[1])])
            xfb_free[0] = st.n
        kv_store = sq.barrier()
        proj_done = st.n
        wot = emit_wt_loads(pr, wA, wbf_d[3], deps=[wcast[3], (st, proj_done)])
        wodd = None
        for j in range(8):
            wodd = ctx.ev("vector", lambda e, j=j: e.tensor_copy(
                wB[0:64, j * DM:(j + 1) * DM], wA[64:128, j * DM:(j + 1) * DM]),
                deps=[wot[j], (st, proj_done)])
        wot = wot + [(sv, wodd)]

        kc_free = {0: None, 1: None}
        vg_free = {0: None, 1: None}
        ptfree = {0: None, 1: None}
        ptbfree = {0: None, 1: None}
        ovfree = {0: None, 1: None}
        bcfree = None
        lrfree = None
        tr_prev = None
        nm_prev = []
        hdone = None
        for h2 in range(8):
            kc_toks = []
            for hh in range(2):
                kc = (kcur0, kcur1)[hh]
                kc_toks.append(ctx.dma("sync", lambda e, h2=h2, hh=hh, kc=kc: e.dma_start(
                    out=kc[0:64, :], in_=k_d[h2, hh * 64:hh * 64 + 64, :]),
                    deps=kv_store + [(st, kc_free[hh])]))
            pafree = (st, tr_prev)
            mav = None
            first_red = True
            for hh in range(2):
                kc = (kcur0, kcur1)[hh]
                qst = (qstE, qstO)[hh]
                slot = h2 * TOK
                for i in range(16):
                    nb = (2 * i + 2 + 3) // 4
                    for b in range(nb):
                        cols = min(512, (2 * i + 2) * 128 - b * 512)
                        ma = ctx.ev("tensor", lambda e, kc=kc, qst=qst, slot=slot, i=i, b=b, cols=cols: e.matmul(
                            psA[:, 0:cols], qst[0:64, slot + i * 128: slot + i * 128 + 128],
                            kc[0:64, b * 512: b * 512 + cols], start=True, stop=True),
                            deps=[kc_toks[hh], pafree], inc=st)
                        pa_deps = [(st, ma)] + ([(st, tr_prev)] if first_red else [])
                        first_red = False
                        pv_ = ctx.ev("vector", lambda e, b=b, cols=cols: e.reduce_max(
                            mcols[:, b:b + 1].unsqueeze(2),
                            psA[:, 0:cols].rearrange("p (a b) -> p a b", a=1), axis=AX.X),
                            deps=pa_deps)
                        pafree = (sv, pv_)
                    mav = ctx.ev("vector", lambda e, hh=hh, i=i, nb=nb: e.reduce_max(
                        m_all[:, hh * 16 + i:hh * 16 + i + 1].unsqueeze(2),
                        mcols[:, 0:nb].rearrange("p (a b) -> p a b", a=1), axis=AX.X))
            tr = ctx.ev("tensor", lambda e: e.transpose(psB[0:32, 0:128], m_all[:], identf[:]),
                        deps=[(sv, mav), idf_t, (sv, bcfree)], inc=st)
            tr_prev = tr
            mr = ctx.ev("vector", lambda e: e.tensor_scalar(
                mrowt[:], psB[0:32, 0:128], -1.0, None, ALU.mult),
                deps=[(st, tr)] + nm_prev)
            mr_st = ctx.dma("sync", lambda e: e.dma_start(out=mrow_d[:], in_=mrowt[:]),
                            deps=[(sv, mr)])
            nm_toks = []
            for hh in range(2):
                qst = (qstE, qstO)[hh]
                nm_toks.append(ctx.dma("gpsimd", lambda e, hh=hh, qst=qst, h2=h2: e.dma_start(
                    out=qst[64:65, h2 * TOK: h2 * TOK + TOK],
                    in_=mrow_d[hh * 16:hh * 16 + 16, :].rearrange("a b -> (a b)").unsqueeze(0)),
                    deps=[mr_st]))
            nm_prev = list(nm_toks)
            for hh in range(2):
                kc = (kcur0, kcur1)[hh]
                qst = (qstE, qstO)[hh]
                h = 2 * h2 + hh
                slot = h2 * TOK
                vg = (vg0, vg1)[hh]
                vg_toks = []
                for qq in range(4):
                    vg_toks.append(ctx.dma("sync", lambda e, qq=qq, h=h, vg=vg: e.dma_start(
                        out=vg[:].rearrange("p (n d) -> p n d", n=32)[:, qq * 8:(qq + 1) * 8, 0:64],
                        in_=v_d[qq * 1024:(qq + 1) * 1024, h * 64:(h + 1) * 64]
                            .rearrange("(n p) d -> p n d", p=128)),
                        deps=kv_store + [(st, vg_free[hh]), (sv, vginit)]))
                for qb in range(8):
                    nkt = 4 * qb + 4
                    psO = (psO0, psO1)[qb % 2]
                    for kt in range(nkt):
                        psT = (psT0, psT1)[kt % 2]
                        ptb = (ptb0, ptb1)[kt % 2]
                        mt = ctx.ev("tensor", lambda e, kc=kc, qst=qst, slot=slot, qb=qb, kt=kt, psT=psT: e.matmul(
                            psT[:], kc[0:65, kt * 128: kt * 128 + 128],
                            qst[0:65, slot + qb * 256: slot + qb * 256 + 256],
                            start=True, stop=True),
                            deps=[nm_toks[hh], (sa, ptfree[kt % 2]), (sv, kcinit)], inc=st)
                        av = ctx.ev("scalar", lambda e, psT=psT, ptb=ptb: e.activation(
                            ptb[:], psT[:], AF.Exp, scale=0.125),
                            deps=[(st, mt), (st, ptbfree[kt % 2])])
                        ptfree[kt % 2] = av
                        lastop = (sa, av)
                        if kt >= 4 * qb:
                            mk = ctx.ev("vector", lambda e, ptb=ptb, kt=kt, qb=qb: e.tensor_mul(
                                ptb[:], ptb[:], mmq[:, (kt - 4 * qb) * 256: (kt - 4 * qb) * 256 + 256]),
                                deps=[(sa, av), mm_t])
                            lastop = (sv, mk)
                        mpv = ctx.ev("tensor", lambda e, vg=vg, kt=kt, ptb=ptb, psO=psO, nkt=nkt: e.matmul(
                            psO[0:65, 0:256], vg[:, kt * 65: kt * 65 + 65], ptb[:],
                            start=(kt == 0), stop=(kt == nkt - 1)),
                            deps=[lastop] + vg_toks + ([(sv, ovfree[qb % 2])] if kt == 0 else []),
                            inc=st)
                        ptbfree[kt % 2] = mpv
                    pv_done = st.n
                    c1 = ctx.ev("vector", lambda e, psO=psO: e.tensor_copy(lrow[:], psO[64:65, 0:256]),
                                deps=[(st, pv_done), (st, lrfree)])
                    c2 = ctx.ev("vector", lambda e: e.reciprocal(lrow[:], lrow[:]))
                    mb = ctx.ev("tensor", lambda e: e.matmul(
                        psB[0:64, 0:256], onesf[0:1, 0:64], lrow[:], start=True, stop=True),
                        deps=[(sv, c2), onf_t, (sv, bcfree)], inc=st)
                    lrfree = mb
                    c3 = ctx.ev("vector", lambda e: e.tensor_copy(invlb[:], psB[0:64, 0:256]),
                                deps=[(st, mb)])
                    bcfree = c3
                    hv = ctx.ev("vector", lambda e, psO=psO, qst=qst, slot=slot, qb=qb: e.tensor_mul(
                        qst[0:64, slot + qb * 256: slot + qb * 256 + 256],
                        psO[0:64, 0:256], invlb[:]), deps=[(sv, c3)])
                    ovfree[qb % 2] = hv
                    hdone = hv
                vg_free[hh] = st.n
            kc_free = {0: st.n, 1: st.n}

        yfree = {0: None, 1: None}
        ystore = {0: None, 1: None}
        opfree = {0: None, 1: None}
        for ti in range(16):
            bi = ti % 2
            xres = (xres0, xres1)[bi]
            ybuf = (ybuf0, ybuf1)[bi]
            lx = ctx.dma("sync", lambda e, xres=xres, ti=ti: e.dma_start(
                out=xres[:], in_=xo_d[ti * 128:ti * 128 + 128, :]),
                deps=[(sv, yfree[bi])])
            for hf in range(2):
                ps_y = (psP, psP2)[hf]
                for h in range(16):
                    qst = (qstE, qstO)[h % 2]
                    ctx.ev("tensor", lambda e, h=h, ti=ti, hf=hf, ps_y=ps_y, qst=qst: e.matmul(
                        ps_y[:], qst[0:64, (h // 2) * TOK + ti * 128: (h // 2) * TOK + ti * 128 + 128],
                        ((wA, wB)[h % 2])[0:64,
                           (h // 2) * DM + hf * 512: (h // 2) * DM + hf * 512 + 512],
                        start=(h == 0), stop=False),
                        deps=(([(sv, hdone), (sv, opfree[hf])] + wot) if h == 0 else []))
                my = ctx.ev("tensor", lambda e, hf=hf, ps_y=ps_y: e.matmul(
                    ps_y[:], onesr[:], bor[0:1, hf * 512: hf * 512 + 512],
                    start=False, stop=True), deps=[bo_t, on_t], inc=st)
                opfree[hf] = ctx.ev("vector", lambda e, ps_y=ps_y, hf=hf, ybuf=ybuf, xres=xres: e.tensor_add(
                    ybuf[:, hf * 512:hf * 512 + 512], ps_y[:], xres[:, hf * 512:hf * 512 + 512]),
                    deps=[(st, my), lx, ystore[bi]])
            yv = sv.n
            yfree[bi] = yv
            ystore[bi] = ctx.dma("sync", lambda e, ybuf=ybuf, ti=ti: e.dma_start(
                out=y_d[ti * 128:ti * 128 + 128, :], in_=ybuf[:]),
                deps=[(sv, yv)])
        ctx.ev("sync", lambda e: e.nop(), deps=[sq.barrier(), gq.barrier()])
        ctx.flush(block)
    return nc


_ROPE_CACHE = {}


def _rope_tables():
    if "t" not in _ROPE_CACHE:
        th = np.repeat(10000.0 ** (-np.arange(0, DM, 2, dtype=np.float64) / DM), 2)
        ang = np.arange(S, dtype=np.float64)[:, None] * th[None, :]
        _ROPE_CACHE["t"] = (np.sin(ang).astype(np.float32), np.cos(ang).astype(np.float32))
    return _ROPE_CACHE["t"]


def run_l1_attn_b(x, ins):
    nc = _get_prog("l1ab", build_l1_attn_b)
    sinb, cosb = _rope_tables()
    bo = np.ascontiguousarray(ins["L1_attn_b_bo"].reshape(1, DM))
    identf = np.eye(P, dtype=np.float32)
    tri = np.tril(np.ones((P, P), np.float32)).T
    onesm = np.ones((P, P), np.float32)
    zerosm = np.zeros((P, P), np.float32)
    in_maps = []
    for c in range(NCORE):
        b, p = c // 2, c % 2
        mm = np.empty((4, P, 256), np.float32)
        for dkt in range(4):
            left = tri if dkt == p else (onesm if dkt < p else zerosm)
            right = tri if dkt == 2 + p else (onesm if dkt < 2 + p else zerosm)
            mm[dkt, :, 0:128] = left
            mm[dkt, :, 128:256] = right
        rows = own_rows(p)
        in_maps.append({
            "x": np.ascontiguousarray(x[b]),
            "xo": np.ascontiguousarray(x[b][rows]),
            "sinb": sinb, "cosb": cosb,
            "sino": np.ascontiguousarray(sinb[rows]),
            "coso": np.ascontiguousarray(cosb[rows]),
            "Wq": ins["L1_attn_b_Wq"], "Wk": ins["L1_attn_b_Wk"],
            "Wv": ins["L1_attn_b_Wv"], "Wo": ins["L1_attn_b_Wo"],
            "bo": bo, "ones": _ONES, "identf": identf,
            "mmq": np.ascontiguousarray(mm.transpose(1, 0, 2).reshape(P, 4 * 256)),
        })
    res = run_bass_kernel_spmd(nc, in_maps, core_ids=list(range(NCORE)))
    y = np.empty((NB, S, DM), np.float32)
    return _scatter(y, [r["y"] for r in res.results])



def build_l1_ffn():
    nc = bass.Bass("TRN2", target_bir_lowering=False, debug=False)
    x_d = nc.dram_tensor("x", [S, DM], F32, kind="ExternalInput").ap()
    xo_d = nc.dram_tensor("xo", [TOK, DM], F32, kind="ExternalInput").ap()
    wq_d = nc.dram_tensor("Wq", [DM, DM], F32, kind="ExternalInput").ap()
    wk_d = nc.dram_tensor("Wk", [DM, DM], F32, kind="ExternalInput").ap()
    wv_d = nc.dram_tensor("Wv", [DM, DM], F32, kind="ExternalInput").ap()
    wp_d = nc.dram_tensor("Wp", [DM, DM], F32, kind="ExternalInput").ap()
    bqk_d = nc.dram_tensor("bqk", [P, 16], F32, kind="ExternalInput").ap()
    bav_d = nc.dram_tensor("bav", [1, DM], F32, kind="ExternalInput").ap()
    bp_d = nc.dram_tensor("bp", [1, DM], F32, kind="ExternalInput").ap()
    wrs_d = nc.dram_tensor("wrs", [DM, 17], F32, kind="ExternalInput").ap()
    brs_d = nc.dram_tensor("brs", [1, 17], F32, kind="ExternalInput").ap()
    temp_d = nc.dram_tensor("temp", [1, 16], F32, kind="ExternalInput").ap()
    embc_d = nc.dram_tensor("embc", [64, 16], F32, kind="ExternalInput").ap()
    ohalf_d = nc.dram_tensor("ohalf", [P, 2], F32, kind="ExternalInput").ap()
    ones_d = nc.dram_tensor("ones", [1, P], F32, kind="ExternalInput").ap()
    identf_d = nc.dram_tensor("identf", [P, P], F32, kind="ExternalInput").ap()
    y_d = nc.dram_tensor("y", [TOK, DM], F32, kind="ExternalOutput").ap()
    xln_d = nc.dram_tensor("xln", [S, DM], BF16, kind="Internal").ap()
    xlno_d = nc.dram_tensor("xlno", [TOK, DM], BF16, kind="Internal").ap()
    wbf_d = nc.dram_tensor("wbf", [4, DM, DM], BF16, kind="Internal").ap()
    k_d = nc.dram_tensor("kst", [8, P, S], BF16, kind="Internal").ap()
    v_d = nc.dram_tensor("vst", [S, DM], BF16, kind="Internal").ap()
    mrow_d = nc.dram_tensor("mrowd", [32, P], F32, kind="Internal").ap()

    es = ExitStack()
    with es:
        xin0 = es.enter_context(nc.sbuf_tensor("xin0", [P, DM], F32))
        xin1 = es.enter_context(nc.sbuf_tensor("xin1", [P, DM], F32))
        xst = es.enter_context(nc.sbuf_tensor("xst", [P, 12], F32))
        xmv = es.enter_context(nc.sbuf_tensor("xmv", [P, 2], F32))
        xsd = es.enter_context(nc.sbuf_tensor("xsd", [P, 2], F32))
        xln0 = es.enter_context(nc.sbuf_tensor("xln0", [P, DM], BF16))
        xln1 = es.enter_context(nc.sbuf_tensor("xln1", [P, DM], BF16))
        epsc = es.enter_context(nc.sbuf_tensor("epsc", [P, 1], F32))
        xfmo = es.enter_context(nc.sbuf_tensor("xfmo", [P, 8 * TOK], BF16))
        xfb0 = es.enter_context(nc.sbuf_tensor("xfb0", [P, 8 * 512], BF16))
        wA = es.enter_context(nc.sbuf_tensor("wA", [P, 8 * DM], BF16))
        wB = es.enter_context(nc.sbuf_tensor("wB", [P, 8 * DM], BF16))
        bqk = es.enter_context(nc.sbuf_tensor("bqks", [P, 16], F32))
        bavr = es.enter_context(nc.sbuf_tensor("bavr", [1, DM], BF16))
        onesr = es.enter_context(nc.sbuf_tensor("oness", [1, P], BF16))
        onesf = es.enter_context(nc.sbuf_tensor("onesf", [1, P], F32))
        ohalf = es.enter_context(nc.sbuf_tensor("ohalfs", [P, 2], F32))
        identf = es.enter_context(nc.sbuf_tensor("identfs", [P, P], F32))
        qstE = es.enter_context(nc.sbuf_tensor("qstE", [65, 8 * TOK], BF16))
        qstO = es.enter_context(nc.sbuf_tensor("qstO", [65, 8 * TOK], BF16))
        kcur0 = es.enter_context(nc.sbuf_tensor("kcur0", [65, S], BF16))
        kcur1 = es.enter_context(nc.sbuf_tensor("kcur1", [65, S], BF16))
        vg0 = es.enter_context(nc.sbuf_tensor("vg0", [P, 32 * 65], BF16))
        vg1 = es.enter_context(nc.sbuf_tensor("vg1", [P, 32 * 65], BF16))
        m_all = es.enter_context(nc.sbuf_tensor("m_all", [P, 32], F32))
        mcols = es.enter_context(nc.sbuf_tensor("mcols", [P, 8], F32))
        mrowt = es.enter_context(nc.sbuf_tensor("mrowt", [32, P], F32))
        ptb0 = es.enter_context(nc.sbuf_tensor("ptb0", [P, 512], BF16))
        ptb1 = es.enter_context(nc.sbuf_tensor("ptb1", [P, 512], BF16))
        lrow = es.enter_context(nc.sbuf_tensor("lrow", [1, 512], F32))
        invlb = es.enter_context(nc.sbuf_tensor("invlb", [64, 512], F32))
        qsc = es.enter_context(nc.sbuf_tensor("qsc", [P, 512], F32))
        qsq = es.enter_context(nc.sbuf_tensor("qsq", [P, 512], F32))
        snr = es.enter_context(nc.sbuf_tensor("snr", [1, DM], F32))
        spx = es.enter_context(nc.sbuf_tensor("spx", [1, 64], F32))
        spbc = es.enter_context(nc.sbuf_tensor("spbc", [P, 16], F32))
        embc = es.enter_context(nc.sbuf_tensor("embcs", [64, 16], F32))
        wsr = es.enter_context(nc.sbuf_tensor("wsr", [64, 16 * 17], BF16))
        brs = es.enter_context(nc.sbuf_tensor("brss", [1, 17], BF16))
        grb = es.enter_context(nc.sbuf_tensor("grb", [P, 24], F32))
        gcols = es.enter_context(nc.sbuf_tensor("gcols", [P, 16], F32))
        bpb = es.enter_context(nc.sbuf_tensor("bpb", [P, DM], BF16))
        xres0 = es.enter_context(nc.sbuf_tensor("xres0", [P, DM], F32))
        xres1 = es.enter_context(nc.sbuf_tensor("xres1", [P, DM], F32))
        psA = es.enter_context(nc.psum_tensor("psA", [P, 512], F32))
        psB = es.enter_context(nc.psum_tensor("psB", [P, 512], F32))
        psP = es.enter_context(nc.psum_tensor("psP", [P, 512], F32))
        psP2 = es.enter_context(nc.psum_tensor("psP2", [P, 512], F32))
        psT0 = es.enter_context(nc.psum_tensor("psT0", [P, 512], F32))
        psT1 = es.enter_context(nc.psum_tensor("psT1", [P, 512], F32))
        psO0 = es.enter_context(nc.psum_tensor("psO0", [P, 512], F32))
        psO1 = es.enter_context(nc.psum_tensor("psO1", [P, 512], F32))
        h_qs = [es.enter_context(nc.semaphore("qs%d" % i)) for i in range(4)]
        h_qg = es.enter_context(nc.semaphore("qg"))
        h_sv = es.enter_context(nc.semaphore("sv"))
        h_sa = es.enter_context(nc.semaphore("sa"))
        h_st = es.enter_context(nc.semaphore("st"))
        block = es.enter_context(nc.Block())
        ctx = Ctx(nc)
        sq = Qu([Sem(h, "qs%d" % i) for i, h in enumerate(h_qs)])
        gq = Qu([Sem(h_qg, "qg")])
        sv = Sem(h_sv, "sv"); sa = Sem(h_sa, "sa"); st = Sem(h_st, "st")
        ctx.queues = {"sync": sq, "gpsimd": gq}
        ctx.engsem = {"vector": (sv, 1), "scalar": (sa, 1)}
        t = dict(xin0=xin0, xin1=xin1, xst=xst, xmv=xmv, xsd=xsd, xln0=xln0, xln1=xln1,
                 epsc=epsc)
        pr = Prog(nc, ctx, sq, gq, sv, sa, st, t)
        ctx.ev("vector", lambda e: e.memset(epsc[:], 1e-5))
        ctx.ev("vector", lambda e: e.memset(kcur0[64:65, :], 1.0))
        kcinit = ctx.ev("vector", lambda e: e.memset(kcur1[64:65, :], 1.0))
        ctx.ev("vector", lambda e: e.memset(vg0[:], 1.0))
        vginit = ctx.ev("vector", lambda e: e.memset(vg1[:], 1.0))

        bqk_t = ctx.dma("sync", lambda e: e.dma_start(out=bqk[:], in_=bqk_d[:]))
        onf_t = ctx.dma("sync", lambda e: e.dma_start(out=onesf[:], in_=ones_d[:]))
        idf_t = ctx.dma("sync", lambda e: e.dma_start(out=identf[:], in_=identf_d[:]))
        emb_t = ctx.dma("sync", lambda e: e.dma_start(out=embc[:], in_=embc_d[:]))
        tmp_t = ctx.dma("sync", lambda e: e.dma_start(out=spx[0:1, 0:16], in_=temp_d[:]))
        bav_t = ctx.dma("gpsimd", lambda e: e.dma_start(out=bavr[:], in_=bav_d[:]))
        on_t = ctx.dma("gpsimd", lambda e: e.dma_start(out=onesr[:], in_=ones_d[:]))
        oh_t = ctx.dma("sync", lambda e: e.dma_start(out=ohalf[:], in_=ohalf_d[:]))
        wsr_t = ctx.dma("gpsimd", lambda e: e.dma_start(
            out=wsr[:].rearrange("p (h m) -> p h m", h=16),
            in_=wrs_d.rearrange("(h p) m -> p h m", p=64)))
        brs_t = ctx.dma("gpsimd", lambda e: e.dma_start(out=brs[:], in_=brs_d[:]))
        wcast = []
        for i, wd in enumerate((wq_d, wk_d, wv_d, wp_d)):
            wcast.append(ctx.dma("gpsimd", lambda e, i=i, wd=wd: e.dma_start(
                out=wbf_d[i], in_=wd[:])))

        s1 = ctx.ev("scalar", lambda e: e.activation(spx[0:1, 16:32], spx[0:1, 0:16], AF.Abs),
                    deps=[tmp_t])
        s2 = ctx.ev("scalar", lambda e: e.activation(spx[0:1, 32:48], spx[0:1, 16:32],
                                                     AF.Exp, scale=-1.0))
        s3 = ctx.ev("scalar", lambda e: e.activation(spx[0:1, 32:48], spx[0:1, 32:48],
                                                     AF.Ln, bias=onesf[0:1, 0:1]),
                    deps=[onf_t])
        v1 = ctx.ev("vector", lambda e: e.tensor_scalar_max(spx[0:1, 48:64], spx[0:1, 0:16], 0.0),
                    deps=[(sa, s3)])
        v2 = ctx.ev("vector", lambda e: e.tensor_add(spx[0:1, 48:64], spx[0:1, 48:64],
                                                     spx[0:1, 32:48]))
        v3 = ctx.ev("vector", lambda e: e.tensor_scalar_mul(spx[0:1, 48:64], spx[0:1, 48:64],
                                                            0.125))
        msp = ctx.ev("tensor", lambda e: e.matmul(psB[:, 0:16], onesf[:], spx[0:1, 48:64],
                                                  start=True, stop=True),
                     deps=[(sv, v3), onf_t], inc=st)
        spv = ctx.ev("vector", lambda e: e.tensor_copy(spbc[:], psB[:, 0:16]),
                     deps=[(st, msp)])
        bpld = ctx.dma("sync", lambda e: e.dma_start(out=snr[0:1, :], in_=bp_d[:]))
        bpm0 = ctx.ev("tensor", lambda e: e.matmul(psB[:, 0:512], onesf[:], snr[0:1, 0:512],
                                                   start=True, stop=True),
                      deps=[bpld, (sv, spv)], inc=st)
        bpv0 = ctx.ev("vector", lambda e: e.tensor_copy(bpb[:, 0:512], psB[:, 0:512]),
                      deps=[(st, bpm0)])
        bpm1 = ctx.ev("tensor", lambda e: e.matmul(psB[:, 0:512], onesf[:], snr[0:1, 512:1024],
                                                   start=True, stop=True),
                      deps=[(sv, bpv0)], inc=st)
        bpv1 = ctx.ev("vector", lambda e: e.tensor_copy(bpb[:, 512:1024], psB[:, 0:512]),
                      deps=[(st, bpm1)])

        emit_ln(pr, x_d, 0, 32, xln_d)
        ln_b = sq.barrier()
        emit_ln(pr, xo_d, 0, 16, xlno_d)
        ln_o = sq.barrier()
        fm_o = emit_fm_loads(pr, xfmo, xlno_d, TOK, ln_o)

        wqt = emit_wt_loads(pr, wA, wbf_d[0], deps=[wcast[0]])
        qev = None
        psnfree = None
        qn_prev = None
        for m in range(8):
            for ob in range(4):
                mq = proj_accum(pr, psP, wA, DM, m, 128, xfmo, TOK, ob * 512, 512,
                                wqt + [fm_o, (sv, qev)])
                e1 = ctx.ev("vector", lambda e, m=m: e.tensor_scalar(
                    qsc[:], psP[:], bqk[:, m:m + 1], None, ALU.add),
                    deps=[(st, mq), bqk_t])
                qev = e1
                e2 = ctx.ev("vector", lambda e: e.tensor_mul(qsq[:], qsc[:], qsc[:]))
                ms0 = ctx.ev("tensor", lambda e: e.matmul(
                    psP2[0:1, 0:512], ohalf[:, 0:1], qsq[:], start=True, stop=True),
                    deps=[(sv, e2), oh_t, (sa, psnfree)], inc=st)
                ms1 = ctx.ev("tensor", lambda e: e.matmul(
                    psP2[64:65, 0:512], ohalf[:, 1:2], qsq[:], start=True, stop=True),
                    inc=st)
                a0 = ctx.ev("scalar", lambda e: e.activation(snr[0:1, 0:512],
                                                             psP2[0:1, 0:512], AF.Sqrt),
                            deps=[(st, ms0), (sv, bpv1)])
                a1 = ctx.ev("scalar", lambda e: e.activation(snr[0:1, 512:1024],
                                                             psP2[64:65, 0:512], AF.Sqrt),
                            deps=[(st, ms1)])
                psnfree = a1
                r1 = ctx.ev("vector", lambda e: e.reciprocal(snr[:], snr[:]),
                            deps=[(sa, a1)])
                mb0 = ctx.ev("tensor", lambda e: e.matmul(
                    psA[0:64, 0:512], onesf[0:1, 0:64], snr[0:1, 0:512],
                    start=True, stop=True), deps=[(sv, r1), (sv, qn_prev)], inc=st)
                mb1 = ctx.ev("tensor", lambda e: e.matmul(
                    psA[64:128, 0:512], onesf[0:1, 0:64], snr[0:1, 512:1024],
                    start=True, stop=True), inc=st)
                qn = ctx.ev("vector", lambda e: e.tensor_mul(qsq[:], qsc[:], psA[:]),
                            deps=[(st, mb1)])
                qn_prev = qn
                for hh in range(2):
                    qst = (qstE, qstO)[hh]
                    ctx.ev("vector", lambda e, m=m, ob=ob, hh=hh, qst=qst: e.tensor_scalar(
                        qst[0:64, m * TOK + ob * 512: m * TOK + ob * 512 + 512],
                        qsq[hh * 64:hh * 64 + 64, :], embc[:, 2 * m + hh:2 * m + hh + 1],
                        None, ALU.add),
                        deps=[emb_t])
        q_mm_done = st.n

        wkt = emit_wt_loads(pr, wB, wbf_d[1], deps=[wcast[1]])
        ktile = xln1[:, 0:512]
        xfb_free = [None]
        kev = qev
        kst_tok = None
        for bi in range(8):
            fb = xfb0
            fmv = []
            for c in range(8):
                fmv.append(ctx.dma("sync", lambda e, c=c, bi=bi, fb=fb: e.dma_start_transpose(
                    fb[:, c * 512:(c + 1) * 512],
                    xln_d[bi * 512:(bi + 1) * 512, c * 128:(c + 1) * 128]),
                    deps=ln_b + [(st, xfb_free[0])]))
            for m in range(8):
                mk = proj_accum(pr, psP, wB, DM, m, 128, fb, 512, 0, 512,
                                wkt + fmv + [(sv, kev)])
                kev = ctx.ev("vector", lambda e, m=m: e.tensor_scalar(
                    ktile, psP[:], bqk[:, 8 + m:8 + m + 1], None, ALU.add),
                    deps=[(st, mk), bqk_t, kst_tok] + ln_o)
                kst_tok = ctx.dma("sync", lambda e, m=m, bi=bi: e.dma_start(
                    out=k_d[m, :, bi * 512:(bi + 1) * 512], in_=ktile),
                    deps=[(sv, kev)])
            xfb_free[0] = st.n
        wvt = emit_wt_loads(pr, wA, wbf_d[2], deps=[wcast[2], (st, q_mm_done)])
        vtile = xln0
        vev = {0: None, 1: None}
        vst_tok = None
        for bi in range(8):
            fb = xfb0
            fmv = []
            for c in range(8):
                fmv.append(ctx.dma("sync", lambda e, c=c, bi=bi, fb=fb: e.dma_start_transpose(
                    fb[:, c * 512:(c + 1) * 512],
                    xln_d[bi * 512:(bi + 1) * 512, c * 128:(c + 1) * 128]),
                    deps=ln_b + [(st, xfb_free[0])]))
            for tt in range(4):
                for hf in range(2):
                    ps_v = (psA, psB)[hf]
                    for c in range(8):
                        ctx.ev("tensor", lambda e, c=c, tt=tt, hf=hf, ps_v=ps_v, fb=fb: e.matmul(
                            ps_v[:], fb[:, c * 512 + tt * 128: c * 512 + tt * 128 + 128],
                            wA[:, c * DM + hf * 512: c * DM + hf * 512 + 512],
                            start=(c == 0), stop=False),
                            deps=(wvt + fmv + [(sv, vev[hf])] if c == 0 else []))
                    mv = ctx.ev("tensor", lambda e, hf=hf, ps_v=ps_v: e.matmul(
                        ps_v[:], onesr[:], bavr[0:1, hf * 512: hf * 512 + 512],
                        start=False, stop=True), deps=[bav_t, on_t], inc=st)
                    vev[hf] = ctx.ev("vector", lambda e, hf=hf, ps_v=ps_v: e.tensor_copy(
                        vtile[:, hf * 512:hf * 512 + 512], ps_v[:]),
                        deps=[(st, mv), vst_tok] + ln_o)
                vst_tok = ctx.dma("sync", lambda e, bi=bi, tt=tt: e.dma_start(
                    out=v_d[bi * 512 + tt * 128: bi * 512 + tt * 128 + 128, :], in_=vtile[:]),
                    deps=[(sv, vev[0]), (sv, vev[1])])
            xfb_free[0] = st.n
        kv_store = sq.barrier()
        proj_done = st.n
        wot = emit_wt_loads(pr, wA, wbf_d[3], deps=[wcast[3], (st, proj_done)])
        wodd = None
        for j in range(8):
            wodd = ctx.ev("vector", lambda e, j=j: e.tensor_copy(
                wB[0:64, j * DM:(j + 1) * DM], wA[64:128, j * DM:(j + 1) * DM]),
                deps=[wot[j], (st, proj_done)])
        wot = wot + [(sv, wodd)]

        kc_free = {0: None, 1: None}
        vg_free = {0: None, 1: None}
        ptfree = {0: None, 1: None}
        ptbfree = {0: None, 1: None}
        ovfree = {0: None, 1: None}
        bcfree = None
        lrfree = None
        tr_prev = None
        nm_prev = []
        hdone = None
        for h2 in range(8):
            kc_toks = []
            for hh in range(2):
                kc = (kcur0, kcur1)[hh]
                kc_toks.append(ctx.dma("sync", lambda e, h2=h2, hh=hh, kc=kc: e.dma_start(
                    out=kc[0:64, :], in_=k_d[h2, hh * 64:hh * 64 + 64, :]),
                    deps=kv_store + [(st, kc_free[hh])]))
            pafree = (st, tr_prev)
            mav = None
            first_red = True
            for hh in range(2):
                kc = (kcur0, kcur1)[hh]
                qst = (qstE, qstO)[hh]
                slot = h2 * TOK
                for i in range(16):
                    for b in range(8):
                        ma = ctx.ev("tensor", lambda e, kc=kc, qst=qst, slot=slot, i=i, b=b: e.matmul(
                            psA[:], qst[0:64, slot + i * 128: slot + i * 128 + 128],
                            kc[0:64, b * 512: b * 512 + 512], start=True, stop=True),
                            deps=[kc_toks[hh], pafree], inc=st)
                        pa_deps = [(st, ma)] + ([(st, tr_prev)] if first_red else [])
                        first_red = False
                        pv_ = ctx.ev("vector", lambda e, b=b: e.reduce_max(
                            mcols[:, b:b + 1].unsqueeze(2),
                            psA[:].rearrange("p (a b) -> p a b", a=1), axis=AX.X),
                            deps=pa_deps)
                        pafree = (sv, pv_)
                    mav = ctx.ev("vector", lambda e, hh=hh, i=i: e.reduce_max(
                        m_all[:, hh * 16 + i:hh * 16 + i + 1].unsqueeze(2),
                        mcols[:].rearrange("p (a b) -> p a b", a=1), axis=AX.X))
            tr = ctx.ev("tensor", lambda e: e.transpose(psB[0:32, 0:128], m_all[:], identf[:]),
                        deps=[(sv, mav), idf_t, (sv, bcfree)], inc=st)
            tr_prev = tr
            mr = ctx.ev("vector", lambda e: e.tensor_scalar(
                mrowt[:], psB[0:32, 0:128], -1.0, None, ALU.mult),
                deps=[(st, tr)] + nm_prev)
            mr_st = ctx.dma("sync", lambda e: e.dma_start(out=mrow_d[:], in_=mrowt[:]),
                            deps=[(sv, mr)])
            nm_toks = []
            for hh in range(2):
                qst = (qstE, qstO)[hh]
                nm_toks.append(ctx.dma("gpsimd", lambda e, hh=hh, qst=qst, h2=h2: e.dma_start(
                    out=qst[64:65, h2 * TOK: h2 * TOK + TOK],
                    in_=mrow_d[hh * 16:hh * 16 + 16, :].rearrange("a b -> (a b)").unsqueeze(0)),
                    deps=[mr_st]))
            nm_prev = list(nm_toks)
            for hh in range(2):
                kc = (kcur0, kcur1)[hh]
                qst = (qstE, qstO)[hh]
                h = 2 * h2 + hh
                slot = h2 * TOK
                vg = (vg0, vg1)[hh]
                vg_toks = []
                for qq in range(4):
                    vg_toks.append(ctx.dma("sync", lambda e, qq=qq, h=h, vg=vg: e.dma_start(
                        out=vg[:].rearrange("p (n d) -> p n d", n=32)[:, qq * 8:(qq + 1) * 8, 0:64],
                        in_=v_d[qq * 1024:(qq + 1) * 1024, h * 64:(h + 1) * 64]
                            .rearrange("(n p) d -> p n d", p=128)),
                        deps=kv_store + [(st, vg_free[hh]), (sv, vginit)]))
                for qb in range(4):
                    psO = (psO0, psO1)[qb % 2]
                    for kt in range(32):
                        psT = (psT0, psT1)[kt % 2]
                        ptb = (ptb0, ptb1)[kt % 2]
                        mt = ctx.ev("tensor", lambda e, kc=kc, qst=qst, slot=slot, qb=qb, kt=kt, psT=psT: e.matmul(
                            psT[:], kc[0:65, kt * 128: kt * 128 + 128],
                            qst[0:65, slot + qb * 512: slot + qb * 512 + 512],
                            start=True, stop=True),
                            deps=[nm_toks[hh], (sa, ptfree[kt % 2]), (sv, kcinit)], inc=st)
                        av = ctx.ev("scalar", lambda e, psT=psT, ptb=ptb, h=h: e.activation(
                            ptb[:], psT[:], AF.Exp, scale=spbc[:, h:h + 1]),
                            deps=[(st, mt), (st, ptbfree[kt % 2]), (sv, spv)])
                        ptfree[kt % 2] = av
                        mpv = ctx.ev("tensor", lambda e, vg=vg, kt=kt, ptb=ptb, psO=psO: e.matmul(
                            psO[0:65, :], vg[:, kt * 65: kt * 65 + 65], ptb[:],
                            start=(kt == 0), stop=(kt == 31)),
                            deps=[(sa, av)] + vg_toks + ([(sv, ovfree[qb % 2])] if kt == 0 else []),
                            inc=st)
                        ptbfree[kt % 2] = mpv
                    pv_done = st.n
                    c1 = ctx.ev("vector", lambda e, psO=psO: e.tensor_copy(lrow[:], psO[64:65, :]),
                                deps=[(st, pv_done), (st, lrfree)])
                    c2 = ctx.ev("vector", lambda e: e.reciprocal(lrow[:], lrow[:]))
                    mb = ctx.ev("tensor", lambda e: e.matmul(
                        psB[0:64, 0:512], onesf[0:1, 0:64], lrow[:], start=True, stop=True),
                        deps=[(sv, c2), onf_t, (sv, bcfree)], inc=st)
                    lrfree = mb
                    c3 = ctx.ev("vector", lambda e: e.tensor_copy(invlb[:], psB[0:64, 0:512]),
                                deps=[(st, mb)])
                    bcfree = c3
                    hv = ctx.ev("vector", lambda e, psO=psO, qst=qst, slot=slot, qb=qb: e.tensor_mul(
                        qst[0:64, slot + qb * 512: slot + qb * 512 + 512],
                        psO[0:64, :], invlb[:]), deps=[(sv, c3)])
                    ovfree[qb % 2] = hv
                    hdone = hv
                vg_free[hh] = st.n
            kc_free = {0: st.n, 1: st.n}

        grfree = None
        for ti in range(16):
            for h in range(16):
                qst = (qstE, qstO)[h % 2]
                ctx.ev("tensor", lambda e, h=h, ti=ti, qst=qst: e.matmul(
                    psT0[:, 0:17], qst[0:64, (h // 2) * TOK + ti * 128: (h // 2) * TOK + ti * 128 + 128],
                    wsr[:, h * 17:(h + 1) * 17], start=(h == 0), stop=False),
                    deps=(([(sv, hdone), wsr_t, (sv, grfree)]) if h == 0 else []))
            mgr = ctx.ev("tensor", lambda e: e.matmul(
                psT0[:, 0:17], onesr[:], brs[:], start=False, stop=True),
                deps=[brs_t], inc=st)
            g1 = ctx.ev("vector", lambda e: e.tensor_copy(grb[:, 0:17], psT0[:, 0:17]),
                        deps=[(st, mgr)])
            grfree = g1
            ctx.ev("vector", lambda e: e.max(out=grb[:, 17:25] if False else mcols[:], in_=grb[:, 0:15]))
            ctx.ev("vector", lambda e: e.tensor_scalar_mul(xsd[:, 0:1], mcols[:, 0:1], -1.0))
            aex = ctx.ev("scalar", lambda e: e.activation(
                grb[:, 0:15], grb[:, 0:15], AF.Exp, bias=xsd[:, 0:1],
                accum_out=xsd[:, 1:2]), deps=[(sv, sv.n)])
            ae3 = ctx.ev("scalar", lambda e: e.activation(
                mcols[:, 0:3], mcols[:, 0:3], AF.Exp, bias=xsd[:, 0:1]))
            v3_ = ctx.ev("vector", lambda e: e.reduce_sum(
                mcols[:, 3:4].unsqueeze(2), mcols[:, 0:3].rearrange("p (a b) -> p a b", a=1),
                axis=AX.X), deps=[(sa, ae3)])
            vm = ctx.ev("vector", lambda e: e.tensor_tensor(
                mcols[:, 4:5], grb[:, 15:16], grb[:, 16:17], ALU.max))
            vn = ctx.ev("vector", lambda e: e.tensor_scalar_mul(mcols[:, 4:5], mcols[:, 4:5], -1.0))
            ae2 = ctx.ev("scalar", lambda e: e.activation(
                grb[:, 15:17], grb[:, 15:17], AF.Exp, bias=mcols[:, 4:5],
                accum_out=mcols[:, 5:6]), deps=[(sv, vn)])
            r1_ = ctx.ev("vector", lambda e: e.reciprocal(mcols[:, 6:7], xsd[:, 1:2]),
                         deps=[(sa, aex)])
            r2_ = ctx.ev("vector", lambda e: e.reciprocal(mcols[:, 7:8], mcols[:, 5:6]),
                         deps=[(sa, ae2)])
            t1_ = ctx.ev("vector", lambda e: e.tensor_mul(mcols[:, 3:4], mcols[:, 3:4], mcols[:, 6:7]))
            t2_ = ctx.ev("vector", lambda e: e.tensor_mul(mcols[:, 3:4], mcols[:, 3:4], grb[:, 16:17]))
            t3_ = ctx.ev("vector", lambda e: e.scalar_tensor_tensor(
                mcols[:, 3:4], mcols[:, 3:4], 3.0, grb[:, 15:16], ALU.mult, ALU.add))
            t4_ = ctx.ev("vector", lambda e: e.tensor_scalar_mul(mcols[:, 3:4], mcols[:, 3:4], 2.0))
            ctx.ev("vector", lambda e, ti=ti: e.tensor_mul(
                gcols[:, ti:ti + 1], mcols[:, 3:4], mcols[:, 7:8]))
        g_done = sv.n

        yfree = {0: None, 1: None}
        ystore = {0: None, 1: None}
        opfree = {0: None, 1: None}
        for ti in range(16):
            bi = ti % 2
            xres = (xres0, xres1)[bi]
            lx = ctx.dma("sync", lambda e, xres=xres, ti=ti: e.dma_start(
                out=xres[:], in_=xo_d[ti * 128:ti * 128 + 128, :]),
                deps=[(sv, yfree[bi]), ystore[bi]])
            for hf in range(2):
                ps_y = (psP, psP2)[hf]
                for h in range(16):
                    qst = (qstE, qstO)[h % 2]
                    ctx.ev("tensor", lambda e, h=h, ti=ti, hf=hf, ps_y=ps_y, qst=qst: e.matmul(
                        ps_y[:], qst[0:64, (h // 2) * TOK + ti * 128: (h // 2) * TOK + ti * 128 + 128],
                        ((wA, wB)[h % 2])[0:64,
                           (h // 2) * DM + hf * 512: (h // 2) * DM + hf * 512 + 512],
                        start=(h == 0), stop=(h == 15)),
                        deps=(([(sv, g_done), (sv, opfree[hf])] + wot) if h == 0 else []),
                        inc=(st if h == 15 else None))
                my = st.n
                e1 = ctx.ev("vector", lambda e, ps_y=ps_y, hf=hf, ti=ti: e.scalar_tensor_tensor(
                    qsc[:], ps_y[:], gcols[:, ti:ti + 1],
                    bpb[:, hf * 512:hf * 512 + 512], ALU.mult, ALU.add),
                    deps=[(st, my), ystore[bi], (sv, bpv1)])
                opfree[hf] = ctx.ev("vector", lambda e, hf=hf, xres=xres: e.tensor_add(
                    xres[:, hf * 512:hf * 512 + 512], xres[:, hf * 512:hf * 512 + 512],
                    qsc[:]),
                    deps=[lx])
            yv = sv.n
            yfree[bi] = yv
            ystore[bi] = ctx.dma("sync", lambda e, xres=xres, ti=ti: e.dma_start(
                out=y_d[ti * 128:ti * 128 + 128, :], in_=xres[:]),
                deps=[(sv, yv)])
        ctx.ev("sync", lambda e: e.nop(), deps=[sq.barrier(), gq.barrier()])
        ctx.flush(block)
    return nc


def run_l1_ffn(x, ins):
    nc = _get_prog("l1f", build_l1_ffn)
    bqk = np.ascontiguousarray(np.concatenate(
        [_pc(ins["L1_ffn_bq"]), _pc(ins["L1_ffn_bk"])], axis=1))
    bav = np.ascontiguousarray(ins["L1_ffn_bv"].reshape(1, DM))
    bp = np.ascontiguousarray(ins["L1_ffn_bp"].reshape(1, DM))
    wrs = np.ascontiguousarray(
        np.concatenate([ins["L1_ffn_Wr"].T, ins["L1_ffn_Ws"].T], axis=1))
    brs = np.ascontiguousarray(
        np.concatenate([ins["L1_ffn_br"], ins["L1_ffn_bs"]]).reshape(1, 17))
    temp = np.ascontiguousarray(ins["L1_ffn_temperature"].reshape(1, 16))
    embc = np.ascontiguousarray(ins["L1_ffn_query_embedding"].reshape(16, 64).T)
    ohalf = np.zeros((P, 2), np.float32)
    ohalf[0:64, 0] = 1.0
    ohalf[64:128, 1] = 1.0
    identf = np.eye(P, dtype=np.float32)
    in_maps = []
    for c in range(NCORE):
        b, p = c // 2, c % 2
        in_maps.append({
            "x": np.ascontiguousarray(x[b]),
            "xo": np.ascontiguousarray(x[b][own_rows(p)]),
            "Wq": ins["L1_ffn_Wq"], "Wk": ins["L1_ffn_Wk"],
            "Wv": ins["L1_ffn_Wv"], "Wp": ins["L1_ffn_Wp"],
            "bqk": bqk, "bav": bav, "bp": bp,
            "wrs": wrs, "brs": brs, "temp": temp, "embc": embc,
            "ohalf": ohalf, "ones": _ONES, "identf": identf,
        })
    res = run_bass_kernel_spmd(nc, in_maps, core_ids=list(range(NCORE)))
    y = np.empty((NB, S, DM), np.float32)
    return _scatter(y, [r["y"] for r in res.results])



def build_l0_attn_b():
    nc = bass.Bass("TRN2", target_bir_lowering=False, debug=False)
    xo_d = nc.dram_tensor("xo", [TOK, DM], F32, kind="ExternalInput").ap()
    wq_d = nc.dram_tensor("Wq", [DM, 4 * DM], F32, kind="ExternalInput").ap()
    wk_d = nc.dram_tensor("Wk", [DM, 4 * DM], F32, kind="ExternalInput").ap()
    wv_d = nc.dram_tensor("Wv", [DM, 4 * DM], F32, kind="ExternalInput").ap()
    wo_d = nc.dram_tensor("Wo", [4 * DM, DM], F32, kind="ExternalInput").ap()
    bo_d = nc.dram_tensor("bo", [1, DM], F32, kind="ExternalInput").ap()
    ones_d = nc.dram_tensor("ones", [1, P], F32, kind="ExternalInput").ap()
    y_d = nc.dram_tensor("y", [TOK, DM], F32, kind="ExternalOutput").ap()
    xlno_d = nc.dram_tensor("xlno", [TOK, DM], BF16, kind="Internal").ap()
    w3_d = nc.dram_tensor("w3bf", [3, DM, 4 * DM], BF16, kind="Internal").ap()
    wob_d = nc.dram_tensor("wobf", [4 * DM, DM], BF16, kind="Internal").ap()

    TB = 512
    es = ExitStack()
    with es:
        xin0 = es.enter_context(nc.sbuf_tensor("xin0", [P, DM], F32))
        xin1 = es.enter_context(nc.sbuf_tensor("xin1", [P, DM], F32))
        xst = es.enter_context(nc.sbuf_tensor("xst", [P, 12], F32))
        xmv = es.enter_context(nc.sbuf_tensor("xmv", [P, 2], F32))
        xsd = es.enter_context(nc.sbuf_tensor("xsd", [P, 2], F32))
        xln0 = es.enter_context(nc.sbuf_tensor("xln0", [P, DM], BF16))
        xln1 = es.enter_context(nc.sbuf_tensor("xln1", [P, DM], BF16))
        epsc = es.enter_context(nc.sbuf_tensor("epsc", [P, 1], F32))
        ftb = es.enter_context(nc.sbuf_tensor("ftb", [P, 8 * TB], BF16))
        wt0 = es.enter_context(nc.sbuf_tensor("wt0", [P, 8 * 128], BF16))
        wt1 = es.enter_context(nc.sbuf_tensor("wt1", [P, 8 * 128], BF16))
        wo0 = es.enter_context(nc.sbuf_tensor("wo0", [P, DM], BF16))
        wo1 = es.enter_context(nc.sbuf_tensor("wo1", [P, DM], BF16))
        qsl = es.enter_context(nc.sbuf_tensor("qsl", [64, 64 * TB], BF16))
        wsl = es.enter_context(nc.sbuf_tensor("wsl", [64, 65 * TB], BF16))
        z2 = es.enter_context(nc.sbuf_tensor("z2", [P, 32 * TB], BF16))
        ksum = es.enter_context(nc.sbuf_tensor("ksum", [64, TB], F32))
        ktmp = es.enter_context(nc.sbuf_tensor("ktmp", [64, 2 * TB], F32))
        dsc = es.enter_context(nc.sbuf_tensor("dsc", [64, 32], F32))
        bos = es.enter_context(nc.sbuf_tensor("bos", [1, DM], BF16))
        onesr = es.enter_context(nc.sbuf_tensor("oness", [1, P], BF16))
        xres0 = es.enter_context(nc.sbuf_tensor("xres0", [P, DM], F32))
        xres1 = es.enter_context(nc.sbuf_tensor("xres1", [P, DM], F32))
        ysc = es.enter_context(nc.sbuf_tensor("ysc", [P, 512], F32))
        psA = es.enter_context(nc.psum_tensor("psA", [P, 512], F32))
        psB = es.enter_context(nc.psum_tensor("psB", [P, 512], F32))
        psP = es.enter_context(nc.psum_tensor("psP", [P, 512], F32))
        psP2 = es.enter_context(nc.psum_tensor("psP2", [P, 512], F32))
        psG = [es.enter_context(nc.psum_tensor("psG%d" % i, [P, 512], F32)) for i in range(4)]
        h_qs = [es.enter_context(nc.semaphore("qs%d" % i)) for i in range(4)]
        h_qg = es.enter_context(nc.semaphore("qg"))
        h_sv = es.enter_context(nc.semaphore("sv"))
        h_sa = es.enter_context(nc.semaphore("sa"))
        h_st = es.enter_context(nc.semaphore("st"))
        block = es.enter_context(nc.Block())
        ctx = Ctx(nc)
        sq = Qu([Sem(h, "qs%d" % i) for i, h in enumerate(h_qs)])
        gq = Qu([Sem(h_qg, "qg")])
        sv = Sem(h_sv, "sv"); sa = Sem(h_sa, "sa"); st = Sem(h_st, "st")
        ctx.queues = {"sync": sq, "gpsimd": gq}
        ctx.engsem = {"vector": (sv, 1), "scalar": (sa, 1)}
        t = dict(xin0=xin0, xin1=xin1, xst=xst, xmv=xmv, xsd=xsd, xln0=xln0, xln1=xln1,
                 epsc=epsc)
        pr = Prog(nc, ctx, sq, gq, sv, sa, st, t)
        ctx.ev("vector", lambda e: e.memset(epsc[:], 1e-5))

        bo_t = ctx.dma("gpsimd", lambda e: e.dma_start(out=bos[:], in_=bo_d[:]))
        on_t = ctx.dma("gpsimd", lambda e: e.dma_start(out=onesr[:], in_=ones_d[:]))
        wcast = []
        for i, wd in enumerate((wq_d, wk_d, wv_d)):
            wcast.append(ctx.dma("gpsimd", lambda e, i=i, wd=wd: e.dma_start(
                out=w3_d[i], in_=wd[:])))
        wocast = ctx.dma("gpsimd", lambda e: e.dma_start(out=wob_d[:], in_=wo_d[:]))

        emit_ln(pr, xo_d, 0, 16, xlno_d)
        ln_o = sq.barrier()

        groups = []
        t0 = 0
        while t0 < TB:
            gn = min(7, TB - t0)
            groups.append((t0, gn))
            t0 += gn

        ftb_free = [None]
        slot_free = [None]
        z2_free = [None]
        wt_free = {0: None, 1: None}
        wo_free = {0: None, 1: None}
        psfree = {0: None, 1: None}
        psyfree = {i: None for i in range(8)}
        yfree = {0: None, 1: None}
        ystore = {0: None, 1: None}
        for tb in range(4):
            fmv = []
            for c in range(8):
                fmv.append(ctx.dma("sync", lambda e, c=c, tb=tb: e.dma_start_transpose(
                    ftb[:, c * TB:(c + 1) * TB],
                    xlno_d[tb * TB:(tb + 1) * TB, c * 128:(c + 1) * 128]),
                    deps=ln_o + [(st, ftb_free[0])]))
            kacc = None
            for m in range(32):
                wt = (wt0, wt1)[m % 2]
                wld = ctx.dma("gpsimd", lambda e, m=m, wt=wt: e.dma_start(
                    out=wt[:].rearrange("p (c m2) -> p c m2", c=8),
                    in_=w3_d[1].rearrange("(c p) m2 -> p c m2", p=128)[:, :, m * 128:(m + 1) * 128]),
                    deps=[wcast[1], (st, wt_free[m % 2])])
                ps = (psA, psB)[m % 2]
                for c in range(8):
                    ctx.ev("tensor", lambda e, c=c, m=m, ps=ps, wt=wt: e.matmul(
                        ps[:], wt[:, c * 128:(c + 1) * 128],
                        ftb[:, c * TB:(c + 1) * TB],
                        start=(c == 0), stop=(c == 7)),
                        deps=([wld] + fmv + [psfree[m % 2]] if c == 0 else []),
                        inc=(st if c == 7 else None))
                wt_free[m % 2] = st.n
                mk = st.n
                ctx.ev("vector", lambda e, ps=ps: e.tensor_scalar_max(
                    ktmp[:, 0:TB], ps[0:64, :], 0.0), deps=[(st, mk)])
                kr = ctx.ev("vector", lambda e, ps=ps: e.tensor_scalar_max(
                    ktmp[:, TB:2 * TB], ps[64:128, :], 0.0))
                psfree[m % 2] = (sv, kr)
                if m == 0:
                    kacc = ctx.ev("vector", lambda e: e.tensor_add(
                        ksum[:], ktmp[:, 0:TB], ktmp[:, TB:2 * TB]))
                else:
                    ctx.ev("vector", lambda e: e.tensor_add(ksum[:], ksum[:], ktmp[:, 0:TB]))
                    kacc = ctx.ev("vector", lambda e: e.tensor_add(
                        ksum[:], ksum[:], ktmp[:, TB:2 * TB]))
            ks_v = ctx.ev("vector", lambda e: e.tensor_copy(
                wsl[0:64, 64 * TB:65 * TB], ksum[:]), deps=[(st, slot_free[0])])
            for m in range(32):
                wt = (wt0, wt1)[m % 2]
                wld = ctx.dma("gpsimd", lambda e, m=m, wt=wt: e.dma_start(
                    out=wt[:].rearrange("p (c m2) -> p c m2", c=8),
                    in_=w3_d[0].rearrange("(c p) m2 -> p c m2", p=128)[:, :, m * 128:(m + 1) * 128]),
                    deps=[wcast[0], (st, wt_free[m % 2])])
                ps = (psA, psB)[m % 2]
                for c in range(8):
                    ctx.ev("tensor", lambda e, c=c, m=m, ps=ps, wt=wt: e.matmul(
                        ps[:], wt[:, c * 128:(c + 1) * 128],
                        ftb[:, c * TB:(c + 1) * TB],
                        start=(c == 0), stop=(c == 7)),
                        deps=([wld] + fmv + [psfree[m % 2], (st, slot_free[0])] if c == 0 else []),
                        inc=(st if c == 7 else None))
                wt_free[m % 2] = st.n
                mk = st.n
                a0 = ctx.ev("scalar", lambda e, ps=ps, m=m: e.activation(
                    qsl[0:64, (2 * m) * TB:(2 * m) * TB + TB], ps[0:64, :],
                    AF.Relu, scale=0.125), deps=[(st, mk)])
                a1 = ctx.ev("scalar", lambda e, ps=ps, m=m: e.activation(
                    qsl[0:64, (2 * m + 1) * TB:(2 * m + 1) * TB + TB], ps[64:128, :],
                    AF.Relu, scale=0.125))
                psfree[m % 2] = (sa, a1)
            for m in range(32):
                wt = (wt0, wt1)[m % 2]
                wld = ctx.dma("gpsimd", lambda e, m=m, wt=wt: e.dma_start(
                    out=wt[:].rearrange("p (c m2) -> p c m2", c=8),
                    in_=w3_d[2].rearrange("(c p) m2 -> p c m2", p=128)[:, :, m * 128:(m + 1) * 128]),
                    deps=[wcast[2], (st, wt_free[m % 2])])
                ps = (psA, psB)[m % 2]
                for c in range(8):
                    ctx.ev("tensor", lambda e, c=c, m=m, ps=ps, wt=wt: e.matmul(
                        ps[:], wt[:, c * 128:(c + 1) * 128],
                        ftb[:, c * TB:(c + 1) * TB],
                        start=(c == 0), stop=(c == 7)),
                        deps=([wld] + fmv + [psfree[m % 2]] if c == 0 else []),
                        inc=(st if c == 7 else None))
                wt_free[m % 2] = st.n
                mk = st.n
                v0 = ctx.ev("vector", lambda e, ps=ps, m=m: e.tensor_mul(
                    wsl[0:64, (2 * m) * TB:(2 * m) * TB + TB], ps[0:64, :], ksum[:]),
                    deps=[(st, mk), (sv, kacc), (sv, ks_v)])
                v1 = ctx.ev("vector", lambda e, ps=ps, m=m: e.tensor_mul(
                    wsl[0:64, (2 * m + 1) * TB:(2 * m + 1) * TB + TB], ps[64:128, :], ksum[:]))
                psfree[m % 2] = (sv, v1)
            slots_v = sv.n
            slots_a = sa.n
            gfree = {0: None, 1: None, 2: None, 3: None}
            for gi, (g0, gn) in enumerate(groups):
                bank = psG[gi % 4]
                for j7 in range(gn):
                    tok = g0 + j7
                    ctx.ev("tensor", lambda e, tok=tok, j7=j7, bank=bank: e.matmul(
                        bank[0:64, j7 * 65:(j7 + 1) * 65],
                        qsl[0:64, :].rearrange("p (s t) -> p s t", s=64)[:, :, tok],
                        wsl[0:64, :].rearrange("p (s t) -> p s t", s=65)[:, :, tok],
                        start=True, stop=True),
                        deps=([(sv, slots_v), (sa, slots_a), (sv, gfree[gi % 4]),
                               (st, z2_free[0])] if j7 == 0 else []),
                        inc=(st if j7 == gn - 1 else None))
                mg = st.n
                d1 = ctx.ev("vector", lambda e, bank=bank, gn=gn: e.tensor_scalar(
                    dsc[0:64, 0:gn],
                    bank[0:64, 0:gn * 65].rearrange("p (t j) -> p t j", j=65)[:, :, 64],
                    1e-6, None, ALU.add), deps=[(st, mg)])
                d2 = ctx.ev("vector", lambda e, gn=gn: e.reciprocal(dsc[0:64, 0:gn], dsc[0:64, 0:gn]))
                for par in range(2):
                    ctx.ev("vector", lambda e, bank=bank, gn=gn, g0=g0, tb=tb, par=par: e.tensor_tensor(
                        z2[par * 64:par * 64 + 64, :]
                          .rearrange("p (jp t) -> p jp t", jp=32)[:, :, g0:g0 + gn],
                        bank[0:64, 0:gn * 65].rearrange("p (t j) -> p t j", j=65)
                            .transpose([0, 2, 1])[:, par:64 + par:2, :],
                        dsc[0:64, 0:gn].unsqueeze(1).to_broadcast([64, 32, gn]),
                        ALU.mult))
                gfree[gi % 4] = sv.n
            evac_done = sv.n
            slot_free[0] = st.n
            banks = [psP, psP2, psG[0], psG[1], psG[2], psG[3], psA, psB]
            for jp in range(32):
                wox = (wo0, wo1)[jp % 2]
                wl0 = ctx.dma("sync", lambda e, jp=jp, wox=wox: e.dma_start(
                    out=wox[0:64, :],
                    in_=wob_d.rearrange("(i j) m2 -> j i m2", j=64)[2 * jp]),
                    deps=[wocast, (st, wo_free[jp % 2])])
                wl1 = ctx.dma("gpsimd", lambda e, jp=jp, wox=wox: e.dma_start(
                    out=wox[64:128, :],
                    in_=wob_d.rearrange("(i j) m2 -> j i m2", j=64)[2 * jp + 1]),
                    deps=[wocast, (st, wo_free[jp % 2])])
                for tt in range(4):
                    for hf in range(2):
                        ps_y = banks[tt * 2 + hf]
                        last = (tt == 3 and hf == 1)
                        ctx.ev("tensor", lambda e, jp=jp, tt=tt, hf=hf, ps_y=ps_y, wox=wox: e.matmul(
                            ps_y[:], z2[:, jp * TB + tt * 128: jp * TB + tt * 128 + 128],
                            wox[:, hf * 512: hf * 512 + 512],
                            start=(jp == 0), stop=False),
                            deps=([wl0, wl1, (sv, evac_done)]
                                  + ([psyfree[tt * 2 + hf]] if jp == 0 else [])),
                            inc=(st if last else None))
                wo_free[jp % 2] = st.n
            for tt in range(4):
                ti = tb * 4 + tt
                bi = ti % 2
                xres = (xres0, xres1)[bi]
                lx = ctx.dma("sync", lambda e, xres=xres, ti=ti: e.dma_start(
                    out=xres[:], in_=xo_d[ti * 128:ti * 128 + 128, :]),
                    deps=[(sv, yfree[bi]), ystore[bi]])
                for hf in range(2):
                    ps_y = banks[tt * 2 + hf]
                    my = ctx.ev("tensor", lambda e, hf=hf, ps_y=ps_y: e.matmul(
                        ps_y[:], onesr[:], bos[0:1, hf * 512: hf * 512 + 512],
                        start=False, stop=True), deps=[bo_t, on_t], inc=st)
                    e1 = ctx.ev("vector", lambda e, ps_y=ps_y: e.tensor_copy(ysc[:], ps_y[:]),
                                deps=[(st, my)])
                    psyfree[tt * 2 + hf] = (sv, e1)
                    ctx.ev("vector", lambda e, hf=hf, xres=xres: e.tensor_add(
                        xres[:, hf * 512:hf * 512 + 512], xres[:, hf * 512:hf * 512 + 512],
                        ysc[:]), deps=[lx])
                yv = sv.n
                yfree[bi] = yv
                ystore[bi] = ctx.dma("sync", lambda e, xres=xres, ti=ti: e.dma_start(
                    out=y_d[ti * 128:ti * 128 + 128, :], in_=xres[:]),
                    deps=[(sv, yv)])
            z2_free[0] = st.n
            ftb_free[0] = st.n
            psfree[0] = psyfree[6]
            psfree[1] = psyfree[7]
        ctx.ev("sync", lambda e: e.nop(), deps=[sq.barrier(), gq.barrier()])
        ctx.flush(block)
    return nc


def run_l0_attn_b(x, ins):
    nc = _get_prog("l0ab", build_l0_attn_b)
    bo = np.ascontiguousarray(ins["L0_attn_b_bo"].reshape(1, DM))
    in_maps = []
    for c in range(NCORE):
        b, p = c // 2, c % 2
        in_maps.append({
            "xo": np.ascontiguousarray(x[b][own_rows(p)]),
            "Wq": ins["L0_attn_b_Wq"], "Wk": ins["L0_attn_b_Wk"],
            "Wv": ins["L0_attn_b_Wv"], "Wo": ins["L0_attn_b_Wo"],
            "bo": bo, "ones": _ONES,
        })
    res = run_bass_kernel_spmd(nc, in_maps, core_ids=list(range(NCORE)))
    y = np.empty((NB, S, DM), np.float32)
    return _scatter(y, [r["y"] for r in res.results])



_prog_cache = {}
_ONES = np.ones((1, P), dtype=np.float32)


def _pc(b):
    return np.ascontiguousarray(b.reshape(-1, P).T)


def _get_prog(key, builder, *args):
    if key not in _prog_cache:
        _prog_cache[key] = builder(*args)
    return _prog_cache[key]


def _scatter(y, res):
    for c in range(NCORE):
        b, p = c // 2, c % 2
        rows = own_rows(p)
        y[b][rows] = res[c]
    return y


def run_l0_attn_a(x, ins):
    nc = _get_prog("l0aa", build_l0_attn_a)
    in_maps = []
    for c in range(NCORE):
        b, p = c // 2, c % 2
        in_maps.append({
            "x": np.ascontiguousarray(x[b]),
            "xo": np.ascontiguousarray(x[b][own_rows(p)]),
            "Wq": ins["L0_attn_a_Wq"], "Wk": ins["L0_attn_a_Wk"],
            "Wv": ins["L0_attn_a_Wv"], "Wo": ins["L0_attn_a_Wo"],
            "bq": _pc(ins["L0_attn_a_bq"]), "bk": _pc(ins["L0_attn_a_bk"]),
            "bv": _pc(ins["L0_attn_a_bv"]), "bo": ins["L0_attn_a_bo"],
            "ones": _ONES,
        })
    res = run_bass_kernel_spmd(nc, in_maps, core_ids=list(range(NCORE)))
    y = np.empty((NB, S, DM), np.float32)
    return _scatter(y, [r["y"] for r in res.results])


SUBLAYERS = [run_l0_attn_a, run_l0_ffn, run_l0_attn_b, run_l0_ffn,
             run_l1_attn_a, run_l1_ffn, run_l1_attn_b, run_l1_ffn]


def kernel(**inputs):
    x = np.ascontiguousarray(np.asarray(inputs["L0_attn_a_x"], dtype=np.float32))
    for f in SUBLAYERS:
        x = f(x, inputs)
    return x


# version 02 (final)
# speedup vs baseline: 1.0425x; 1.0425x over previous
from contextlib import ExitStack
import numpy as np
import ml_dtypes
import concourse.bass as bass
import concourse.mybir as mybir
from concourse.bass_utils import run_bass_kernel_spmd

F32 = mybir.dt.float32
BF16 = mybir.dt.bfloat16
AF = mybir.ActivationFunctionType
ALU = mybir.AluOpType
AX = mybir.AxisListType
BF = ml_dtypes.bfloat16

P = 128
NB, S, DM = 4, 4096, 1024
TOK = 2048
NCORE = 8

ENGINES = ("sync", "scalar", "vector", "tensor", "gpsimd")
SERIALIZE_ENGINES = True


def own_rows(parity):
    return np.concatenate([np.arange(256 * i + 128 * parity, 256 * i + 128 * parity + 128)
                           for i in range(16)])


class Sem:
    __slots__ = ("h", "n", "name")

    def __init__(self, h, name):
        self.h = h
        self.n = 0
        self.name = name


class Qu:

    def __init__(self, sems):
        self.sems = sems
        self.i = 0

    def barrier(self):
        return [(s, s.n) for s in self.sems]


class Ctx:

    def __init__(self, nc):
        self.nc = nc
        self.stream = {e: [] for e in ENGINES}
        self.waited = {e: {} for e in ENGINES}
        self.engsem = {}

    @staticmethod
    def _flat(deps):
        out = []
        for d in deps:
            if d is None:
                continue
            if isinstance(d, list):
                out.extend([x for x in d if x is not None])
            else:
                out.append(d)
        return out

    def ev(self, eng, f, deps=(), inc=None, inc_by=1):
        deps = self._flat(deps)
        es_ = self.engsem.get(eng)
        if es_ is not None:
            s_, b_ = es_
            if SERIALIZE_ENGINES:
                deps = deps + [(s_, s_.n)]
            if inc is None:
                inc = s_
                inc_by = b_
        dl = []
        for s, v in deps:
            if v is None:
                continue
            w = self.waited[eng].get(id(s), 0)
            if v > w:
                dl.append((s, v))
                self.waited[eng][id(s)] = v
        if inc is not None:
            inc.n += inc_by
            ret = inc.n
            tgt = (inc, inc_by)
        else:
            ret = None
            tgt = None

        def body(e, dl=dl, f=f, tgt=tgt):
            for s, v in dl:
                e.wait_ge(s.h, v)
            ins = f(e)
            if tgt is not None:
                ins.then_inc(tgt[0].h, tgt[1])

        self.stream[eng].append(body)
        return ret

    def dma(self, eng, f, deps=()):
        q = self.queues[eng]
        s = q.sems[q.i % len(q.sems)]
        q.i += 1
        deps = self._flat(deps) + [(s, s.n)]
        v = self.ev(eng, f, deps=deps, inc=s, inc_by=16)
        return (s, v)

    def flush(self, block):
        for e in ENGINES:
            fl = self.stream[e]
            if fl:
                getattr(block, e)(lambda eng, fl=fl: [f(eng) for f in fl])


class Prog:

    def __init__(self, nc, ctx, sq, gq, sv, sa, st, t):
        self.nc = nc
        self.ctx = ctx
        self.sq = sq
        self.gq = gq
        self.sv = sv
        self.sa = sa
        self.st = st
        self.t = t




class Env:

    def __init__(self, nc, es, ctx, pr, prefix, inputs, arena, arena_off, psbanks):
        self.nc = nc
        self.es = es
        self.ctx = ctx
        self.pr = pr
        self.prefix = prefix
        self.inputs = inputs
        self.arena = arena
        self.off = arena_off
        self.psbanks = psbanks
        self.ps_used = 0

    def sb(self, name, shape, dt):
        rows, cols = shape
        nbytes = cols * (2 if dt == BF16 else 4)
        nbytes = (nbytes + 127) & ~127
        a = self.off[0]
        assert a + nbytes <= self.arena.shape[1] * 2, \
            f"arena overflow at {name}: {a + nbytes}"
        self.off[0] = a + nbytes
        view = self.arena[0:rows, a // 2:(a + nbytes) // 2]
        if dt == F32:
            view = view.bitcast(F32)
        return view[:, 0:cols]

    def ps(self, name, shape, dt):
        rows, cols = shape
        b = self.psbanks[self.ps_used % 8]
        self.ps_used += 1
        v = b
        if dt == BF16:
            v = b[:].bitcast(BF16)
            return v[0:rows, 0:cols]
        return b[0:rows, 0:cols]

    def dram(self, name, shape, dt, kind="Internal"):
        return self.nc.dram_tensor(self.prefix + name, list(shape), dt, kind=kind).ap()

    def inp(self, name):
        return self.inputs[name]



def emit_ln(pr, x_dram, row0, ntiles, xln_dram, drow0=0):
    ctx, sv, sa = pr.ctx, pr.sv, pr.sa
    t = pr.t
    if not hasattr(pr, "lnfree"):
        pr.lnfree = {0: None, 1: None}
        pr.lnsfree = {0: None, 1: None}
    free = pr.lnfree
    sfree = pr.lnsfree
    for i in range(ntiles):
        bi = i % 2
        xin = t["xin%d" % bi]
        xln = t["xln%d" % bi]
        r = row0 + i * 128
        dr = drow0 + i * 128
        ld = ctx.dma("sync", lambda e, xin=xin, r=r: e.dma_start(
            out=xin[:], in_=x_dram[r:r + 128, :]), deps=[(sv, free[bi])])
        ctx.ev("vector", lambda e, xin=xin: e.bn_stats(t["xst"][:, 0:6], xin[:, 0:512]),
               deps=[ld])
        ctx.ev("vector", lambda e, xin=xin: e.bn_stats(t["xst"][:, 6:12], xin[:, 512:1024]))
        v1 = ctx.ev("vector", lambda e: e.bn_aggr(t["xmv"][:], t["xst"][:]))
        a1 = ctx.ev("scalar", lambda e: e.activation(t["xsd"][:, 0:1], t["xmv"][:, 1:2],
                                                     AF.Sqrt, bias=t["epsc"][:, 0:1]),
                    deps=[(sv, v1)])
        ctx.ev("vector", lambda e: e.reciprocal(t["xsd"][:, 1:2], t["xsd"][:, 0:1]),
               deps=[(sa, a1)])
        v3 = ctx.ev("vector", lambda e, xin=xin, xln=xln: e.tensor_scalar(
            xln[:], xin[:], t["xmv"][:, 0:1], t["xsd"][:, 1:2], ALU.subtract, ALU.mult),
            deps=[sfree[bi]])
        free[bi] = v3
        sfree[bi] = ctx.dma("sync", lambda e, xln=xln, dr=dr: e.dma_start(
            out=xln_dram[dr:dr + 128, :], in_=xln[:]), deps=[(sv, v3)])


def emit_ln_rope(pr, x_dram, row0, ntiles, xln_dram, rxln_dram, sin_dram, cos_dram,
                 xf, rot, sct_s, sct_c, rx0, rx1):
    ctx, sv, sa = pr.ctx, pr.sv, pr.sa
    t = pr.t
    if not hasattr(pr, "lnfree"):
        pr.lnfree = {0: None, 1: None}
        pr.lnsfree = {0: None, 1: None}
    free = pr.lnfree
    sfree = pr.lnsfree
    rxfree = {0: None, 1: None}
    xffree = None
    for i in range(ntiles):
        bi = i % 2
        xin = t["xin%d" % bi]
        xln = t["xln%d" % bi]
        rx = (rx0, rx1)[bi]
        r = row0 + i * 128
        ld = ctx.dma("sync", lambda e, xin=xin, r=r: e.dma_start(
            out=xin[:], in_=x_dram[r:r + 128, :]), deps=[(sv, free[bi])])
        lds = ctx.dma("sync", lambda e, r=r: e.dma_start(
            out=sct_s[:], in_=sin_dram[r:r + 128, :]), deps=[(sv, xffree)])
        ldc = ctx.dma("sync", lambda e, r=r: e.dma_start(
            out=sct_c[:], in_=cos_dram[r:r + 128, :]), deps=[(sv, xffree)])
        ctx.ev("vector", lambda e, xin=xin: e.bn_stats(t["xst"][:, 0:6], xin[:, 0:512]),
               deps=[ld])
        ctx.ev("vector", lambda e, xin=xin: e.bn_stats(t["xst"][:, 6:12], xin[:, 512:1024]))
        v1 = ctx.ev("vector", lambda e: e.bn_aggr(t["xmv"][:], t["xst"][:]))
        a1 = ctx.ev("scalar", lambda e: e.activation(t["xsd"][:, 0:1], t["xmv"][:, 1:2],
                                                     AF.Sqrt, bias=t["epsc"][:, 0:1]),
                    deps=[(sv, v1)])
        ctx.ev("vector", lambda e: e.reciprocal(t["xsd"][:, 1:2], t["xsd"][:, 0:1]),
               deps=[(sa, a1)])
        v3 = ctx.ev("vector", lambda e, xin=xin: e.tensor_scalar(
            xf[:], xin[:], t["xmv"][:, 0:1], t["xsd"][:, 1:2], ALU.subtract, ALU.mult))
        free[bi] = v3
        v4 = ctx.ev("vector", lambda e, xln=xln: e.tensor_copy(xln[:], xf[:]),
                    deps=[(qsf := sfree[bi])])
        sfree[bi] = ctx.dma("sync", lambda e, xln=xln, r=r: e.dma_start(
            out=xln_dram[r:r + 128, :], in_=xln[:]), deps=[(sv, v4)])
        ctx.ev("vector", lambda e: e.tensor_scalar_mul(
            rot[:].rearrange("p (a b) -> p a b", b=2)[:, :, 0],
            xf[:].rearrange("p (a b) -> p a b", b=2)[:, :, 1], -1.0))
        ctx.ev("vector", lambda e: e.tensor_copy(
            rot[:].rearrange("p (a b) -> p a b", b=2)[:, :, 1],
            xf[:].rearrange("p (a b) -> p a b", b=2)[:, :, 0]))
        ctx.ev("vector", lambda e: e.tensor_mul(xf[:], xf[:], sct_c[:]), deps=[ldc])
        ctx.ev("vector", lambda e: e.tensor_mul(rot[:], rot[:], sct_s[:]), deps=[lds])
        v9 = ctx.ev("vector", lambda e, rx=rx: e.tensor_add(rx[:], xf[:], rot[:]),
                    deps=[rxfree[bi]])
        xffree = v9
        rxfree[bi] = ctx.dma("sync", lambda e, rx=rx, r=r: e.dma_start(
            out=rxln_dram[r:r + 128, :], in_=rx[:]), deps=[(sv, v9)])


def emit_wt_loads(pr, wsb, wbf_dram, deps=(), ncol=DM):
    ctx = pr.ctx
    toks = []
    for c in range(8):
        toks.append(ctx.dma("sync", lambda e, c=c: e.dma_start_transpose(
            wsb[:, c * ncol:(c + 1) * ncol],
            wbf_dram[0:ncol, c * 128:(c + 1) * 128]), deps=deps))
    return toks


def emit_fm_loads(pr, xfm, xln_dram, nrows, dep_toks):
    ctx = pr.ctx
    toks = []
    for c in range(8):
        toks.append(ctx.dma("sync", lambda e, c=c: e.dma_start_transpose(
            xfm[:, c * nrows:(c + 1) * nrows],
            xln_dram[0:nrows, c * 128:(c + 1) * 128]), deps=dep_toks))
    return toks


def emit_w_natural(pr, wsb, w_dram, deps=(), kchunks=8):
    return pr.ctx.dma("gpsimd", lambda e: e.dma_start(
        out=wsb[:].rearrange("p (c m) -> p c m", c=kchunks),
        in_=w_dram.rearrange("(c p) m -> p c m", p=128)), deps=deps)


def proj_accum(pr, ps, wsb, wN, m, wcols, xfm, nrows, babs, bn, deps, last_extra=None):
    ctx, st = pr.ctx, pr.st
    for c in range(8):
        ctx.ev("tensor", lambda e, c=c: e.matmul(
            ps[:, 0:bn], wsb[:, c * wN + m * wcols: c * wN + m * wcols + wcols],
            xfm[:, c * nrows + babs: c * nrows + babs + bn],
            start=(c == 0), stop=(c == 7 and last_extra is None)),
            deps=(deps if c == 0 else []),
            inc=(st if (c == 7 and last_extra is None) else None))
    if last_extra is not None:
        lhsT, rhs = last_extra
        ctx.ev("tensor", lambda e: e.matmul(ps[:, 0:bn], lhsT, rhs, start=False, stop=True),
               inc=st)
    return st.n



ARENA_BYTES = 212480
EXT_SPECS = {}


def _scaffold(run_body, input_specs, n_extra_sems=0):
    nc = bass.Bass("TRN2", target_bir_lowering=False, debug=False)
    inputs = {}
    for name, shape, dt, kind in input_specs:
        inputs[name] = nc.dram_tensor(name, list(shape), dt, kind=kind).ap()
    es = ExitStack()
    with es:
        arena = es.enter_context(nc.sbuf_tensor("arena", [P, ARENA_BYTES // 2], BF16))
        psb = [es.enter_context(nc.psum_tensor("psb%d" % i, [P, 512], F32)) for i in range(8)]
        h_qs = [es.enter_context(nc.semaphore("qs%d" % i)) for i in range(4)]
        h_qg = es.enter_context(nc.semaphore("qg"))
        h_qc = es.enter_context(nc.semaphore("qc"))
        h_sv = es.enter_context(nc.semaphore("sv"))
        h_sa = es.enter_context(nc.semaphore("sa"))
        h_st = es.enter_context(nc.semaphore("st"))
        block = es.enter_context(nc.Block())
        ctx = Ctx(nc)
        sq = Qu([Sem(h, "qs%d" % i) for i, h in enumerate(h_qs)])
        gq = Qu([Sem(h_qg, "qg")])
        cq = Qu([Sem(h_qc, "qc")])
        sv = Sem(h_sv, "sv"); sa = Sem(h_sa, "sa"); st = Sem(h_st, "st")
        ctx.queues = {"sync": sq, "gpsimd": gq}
        ctx.engsem = {"vector": (sv, 1), "scalar": (sa, 1)}
        pr = Prog(nc, ctx, sq, gq, sv, sa, st, {})
        pr.cq = cq

        def env_factory(prefix, extra_inputs=None):
            ins = dict(inputs)
            if extra_inputs:
                ins.update(extra_inputs)
            return Env(nc, es, ctx, pr, prefix, ins, arena, [0], psb)

        run_body(nc, es, ctx, pr, env_factory, inputs)
        ctx.ev("sync", lambda e: e.nop(),
               deps=[sq.barrier(), gq.barrier(), cq.barrier(),
                     (sv, sv.n), (sa, sa.n), (st, st.n)])
        ctx.flush(block)
    return nc


def emit_barrier(ctx, pr):
    deps = [pr.sq.barrier(), pr.gq.barrier(),
            (pr.sv, pr.sv.n), (pr.sa, pr.sa.n), (pr.st, pr.st.n)]
    if hasattr(pr, "cq"):
        deps.append(pr.cq.barrier())
    for e in ENGINES:
        ctx.ev(e, lambda en: en.nop(), deps=list(deps))



def emit_l0_attn_a(env):
    nc = env.nc
    ctx = env.ctx
    sq, gq = env.pr.sq, env.pr.gq
    sv, sa, st = env.pr.sv, env.pr.sa, env.pr.st
    x_d = env.inputs.get("x")
    xo_d = env.inp("xo")
    wq_d = env.inp("Wq")
    wk_d = env.inp("Wk")
    wv_d = env.inp("Wv")
    wo_d = env.inp("Wo")
    bq_d = env.inp("bq")
    bk_d = env.inp("bk")
    bv_d = env.inp("bv")
    bo_d = env.inp("bo")
    ones_d = env.inp("ones")
    y_d = env.inp("y")
    xlno_d = env.dram("xlno", [TOK, DM], BF16)

    xin0 = env.sb("xin0", [P, DM], F32)
    xin1 = env.sb("xin1", [P, DM], F32)
    xst = env.sb("xst", [P, 12], F32)
    xmv = env.sb("xmv", [P, 2], F32)
    xsd = env.sb("xsd", [P, 2], F32)
    xln0 = env.sb("xln0", [P, DM], BF16)
    xln1 = env.sb("xln1", [P, DM], BF16)
    xfb0 = env.sb("xfb0", [P, 8 * 512], BF16)
    xfb1 = env.sb("xfb1", [P, 8 * 512], BF16)
    xfmo = env.sb("xfmo", [P, 8 * TOK], BF16)
    wA = env.sb("wA", [P, 8 * DM], BF16)
    wB = env.sb("wB", [P, 8 * DM], BF16)
    bqs = env.sb("bqs", [P, 8], F32)
    bks = env.sb("bks", [P, 8], F32)
    bvs = env.sb("bvs", [P, 8], F32)
    bos = env.sb("bos", [1, DM], BF16)
    onesr = env.sb("onesr", [1, P], BF16)
    qts = env.sb("qts", [P, 8 * TOK], BF16)
    ssc = env.sb("ssc", [P, 6 * 512], F32)
    kvm = env.sb("kvm", [P, 8], F32)
    ksm = env.sb("ksm", [P, 8], F32)
    tcol = env.sb("tcol", [P, 2], F32)
    epsc = env.sb("epsc", [P, 1], F32)
    xres0 = env.sb("xres0", [P, DM], F32)
    xres1 = env.sb("xres1", [P, DM], F32)
    gsc = env.sb("gsc", [P, 3 * 512], F32)
    ybuf0 = env.sb("ybuf0", [P, DM], F32)
    ybuf1 = env.sb("ybuf1", [P, DM], F32)
    ps_k = env.ps("ps_k", [P, 512], F32)
    ps_v = env.ps("ps_v", [P, 512], F32)
    ps_q = env.ps("ps_q", [P, 512], F32)
    ps_o0 = env.ps("ps_o0", [P, 512], F32)
    ps_o1 = env.ps("ps_o1", [P, 512], F32)
    t = dict(xin0=xin0, xin1=xin1, xst=xst, xmv=xmv, xsd=xsd, xln0=xln0, xln1=xln1,
             epsc=epsc)
    pr = env.pr
    pr.t = t
    pr.lnfree = {0: None, 1: None}
    pr.lnsfree = {0: None, 1: None}
    ctx.ev("vector", lambda e: e.memset(epsc[:], 1e-5))

    wk_t = emit_w_natural(pr, wA, wk_d)
    wv_t = emit_w_natural(pr, wB, wv_d)
    bq_t = ctx.dma("sync", lambda e: e.dma_start(out=bqs[:], in_=bq_d[:]))
    bk_t = ctx.dma("sync", lambda e: e.dma_start(out=bks[:], in_=bk_d[:]))
    bv_t = ctx.dma("sync", lambda e: e.dma_start(out=bvs[:], in_=bv_d[:]))
    bo_t = ctx.dma("gpsimd", lambda e: e.dma_start(out=bos[:], in_=bo_d.unsqueeze(0)))
    on_t = ctx.dma("gpsimd", lambda e: e.dma_start(out=onesr[:], in_=ones_d[:]))

    if "xfull" in env.inputs:
        xln_d = env.inp("xfull")
        ln_b = []
    else:
        xln_d = env.dram("xlnb", [S, DM], BF16)
        emit_ln(pr, x_d, 0, 32, xln_d)
        ln_b = sq.barrier()
    emit_ln(pr, xo_d, 0, 16, xlno_d)
    ln_o = sq.barrier()
    fm_o = emit_fm_loads(pr, xfmo, xlno_d, TOK, ln_o)

    u_ = ssc[:, 0:512]
    e_ = ssc[:, 512:1024]
    r_ = ssc[:, 1024:1536]
    ktil = ssc[:, 1536:2048]
    vr = ssc[:, 2048:2560]
    kv = ssc[:, 2560:3072]

    psfree = {"k": None, "v": None, "q": None}

    def fmap(ps, bcol, outsl, outT, mmv, bdep):
        v1 = ctx.ev("vector", lambda e, ps=ps, bcol=bcol: e.tensor_scalar(
            u_, ps[:], bcol, None, ALU.add), deps=[(st, mmv), bdep])
        ctx.ev("vector", lambda e: e.tensor_scalar_min(e_, u_, 0.0))
        a1 = ctx.ev("scalar", lambda e: e.activation(e_, e_, AF.Exp),
                    deps=[(sv, sv.n)])
        ctx.ev("vector", lambda e: e.tensor_scalar_max(r_, u_, 0.0))
        v4 = ctx.ev("vector", lambda e, outT=outT, outsl=outsl: e.tensor_tensor(
            outT[:, outsl[0]:outsl[1]], e_, r_, ALU.add), deps=[(sa, a1)])
        return v4, v1

    xfb_free = {0: None, 1: None}
    for bi in range(8):
        fb = (xfb0, xfb1)[bi % 2]
        fmv = []
        for c in range(8):
            fmv.append(ctx.dma("sync", lambda e, c=c, bi=bi, fb=fb: e.dma_start_transpose(
                fb[:, c * 512:(c + 1) * 512],
                xln_d[bi * 512:(bi + 1) * 512, c * 128:(c + 1) * 128]),
                deps=[ln_b, (st, xfb_free[bi % 2])]))
        for m in range(8):
            mk = proj_accum(pr, ps_k, wA, DM, m, 128, fb, 512, 0, 512,
                            [wk_t, fmv, (sv, psfree["k"])])
            kt, psfree["k"] = fmap(ps_k, bks[:, m:m + 1], (1536, 2048), ssc, mk, bk_t)
            mv_ = proj_accum(pr, ps_v, wB, DM, m, 128, fb, 512, 0, 512,
                             [wv_t, fmv, (sv, psfree["v"])])
            v5 = ctx.ev("vector", lambda e, m=m: e.tensor_scalar(
                vr, ps_v[:], bvs[:, m:m + 1], None, ALU.add),
                deps=[(st, mv_), bv_t])
            psfree["v"] = v5
            ctx.ev("vector", lambda e: e.tensor_tensor(kv, ktil, vr, ALU.mult),
                   deps=[(sv, kt)])
            if bi == 0:
                ctx.ev("vector", lambda e, m=m: e.reduce_sum(
                    kvm[:, m:m + 1].unsqueeze(2),
                    kv.rearrange("p (a b) -> p a b", a=1), axis=AX.X))
                ctx.ev("vector", lambda e, m=m: e.reduce_sum(
                    ksm[:, m:m + 1].unsqueeze(2),
                    ktil.rearrange("p (a b) -> p a b", a=1), axis=AX.X))
            else:
                ctx.ev("vector", lambda e: e.reduce_sum(
                    tcol[:, 0:1].unsqueeze(2),
                    kv.rearrange("p (a b) -> p a b", a=1), axis=AX.X))
                ctx.ev("vector", lambda e, m=m: e.tensor_add(
                    kvm[:, m:m + 1], kvm[:, m:m + 1], tcol[:, 0:1]))
                ctx.ev("vector", lambda e: e.reduce_sum(
                    tcol[:, 1:2].unsqueeze(2),
                    ktil.rearrange("p (a b) -> p a b", a=1), axis=AX.X))
                ctx.ev("vector", lambda e, m=m: e.tensor_add(
                    ksm[:, m:m + 1], ksm[:, m:m + 1], tcol[:, 1:2]))
        xfb_free[bi % 2] = st.n
    sums_done = sv.n

    wq_t = ctx.dma("gpsimd", lambda e: e.dma_start(
        out=wA[:].rearrange("p (c m) -> p c m", c=8),
        in_=wq_d.rearrange("(c p) m -> p c m", p=128)), deps=[(st, st.n)])
    for m in range(8):
        for oi in range(4):
            mq = proj_accum(pr, ps_q, wA, DM, m, 128, xfmo, TOK, oi * 512, 512,
                            [wq_t, fm_o, (sv, psfree["q"])])
            sl = (m * TOK + oi * 512, m * TOK + oi * 512 + 512)
            _, psfree["q"] = fmap(ps_q, bqs[:, m:m + 1], sl, qts, mq, bq_t)
    for m in range(8):
        for oi in range(4):
            sl = slice(m * TOK + oi * 512, m * TOK + oi * 512 + 512)
            ctx.ev("vector", lambda e, sl=sl, m=m: e.tensor_scalar(
                u_, qts[:, sl], ksm[:, m:m + 1], 1e-6, ALU.mult, ALU.add),
                deps=[(sv, sums_done)])
            ctx.ev("vector", lambda e: e.reciprocal(u_, u_))
            ctx.ev("vector", lambda e, sl=sl, m=m: e.tensor_scalar(
                e_, qts[:, sl], kvm[:, m:m + 1], None, ALU.mult))
            ctx.ev("vector", lambda e, sl=sl: e.tensor_tensor(
                qts[:, sl], e_, u_, ALU.mult))
    vp_done = sv.n

    wo_t = ctx.dma("gpsimd", lambda e: e.dma_start(
        out=wB[:].rearrange("p (c m) -> p c m", c=8),
        in_=wo_d.rearrange("(c p) m -> p c m", p=128)), deps=[(st, st.n)])

    s1 = gsc[:, 0:512]
    w_ = gsc[:, 512:1024]
    t_ = gsc[:, 1024:1536]
    ofree = {0: None, 1: None}
    yfree = {0: None, 1: None}
    ystore = {0: None, 1: None}
    for ti in range(16):
        bi = ti % 2
        xres = (xres0, xres1)[bi]
        ybuf = (ybuf0, ybuf1)[bi]
        lx = ctx.dma("sync", lambda e, xres=xres, ti=ti: e.dma_start(
            out=xres[:], in_=xo_d[ti * 128:ti * 128 + 128, :]),
            deps=[(sv, yfree[bi])])
        for hf in range(2):
            ps_o = (ps_o0, ps_o1)[hf]
            for c in range(8):
                ctx.ev("tensor", lambda e, c=c, hf=hf, ti=ti, ps_o=ps_o: e.matmul(
                    ps_o[:], qts[:, c * TOK + ti * 128: c * TOK + ti * 128 + 128],
                    wB[:, c * DM + hf * 512: c * DM + hf * 512 + 512],
                    start=(c == 0), stop=False),
                    deps=([(sv, vp_done), wo_t, (sv, ofree[hf])] if c == 0 else []))
            mo = ctx.ev("tensor", lambda e, hf=hf, ps_o=ps_o: e.matmul(
                ps_o[:], onesr[:], bos[:, hf * 512: hf * 512 + 512],
                start=False, stop=True),
                deps=[bo_t, on_t], inc=st)
            a1 = ctx.ev("scalar", lambda e, ps_o=ps_o: e.activation(
                s1, ps_o[:], AF.Square), deps=[(st, mo)])
            ctx.ev("vector", lambda e, ps_o=ps_o: e.tensor_mul(w_, ps_o[:], s1),
                   deps=[(sa, a1)])
            ctx.ev("vector", lambda e, ps_o=ps_o: e.scalar_tensor_tensor(
                w_, w_, 0.044715, ps_o[:], ALU.mult, ALU.add))
            a2 = ctx.ev("scalar", lambda e: e.activation(
                t_, w_, AF.Tanh, scale=0.7978845608028654), deps=[(sv, sv.n)])
            ctx.ev("vector", lambda e: e.tensor_scalar(
                t_, t_, 0.5, 0.5, ALU.mult, ALU.add), deps=[(sa, a2)])
            gf = ctx.ev("vector", lambda e, ps_o=ps_o, hf=hf, ybuf=ybuf: e.tensor_mul(
                ybuf[:, hf * 512:hf * 512 + 512], t_, ps_o[:]),
                deps=[ystore[bi]])
            ofree[hf] = gf
            ctx.ev("vector", lambda e, hf=hf, ybuf=ybuf, xres=xres: e.tensor_add(
                ybuf[:, hf * 512:hf * 512 + 512], ybuf[:, hf * 512:hf * 512 + 512],
                xres[:, hf * 512:hf * 512 + 512]),
                deps=[lx])
        yv = sv.n
        yfree[bi] = yv
        ystore[bi] = ctx.dma("sync", lambda e, ybuf=ybuf, ti=ti: e.dma_start(
            out=y_d[ti * 128:ti * 128 + 128, :], in_=ybuf[:]),
            deps=[(sv, yv)])



def emit_l0_ffn_counts(env):
    nc = env.nc
    ctx = env.ctx
    sq, gq = env.pr.sq, env.pr.gq
    sv, sa, st = env.pr.sv, env.pr.sa, env.pr.st
    xo_d = env.inp("xo")
    gw_d = env.inp("gw")
    gb_d = env.inp("gb")
    ones_d = env.inp("ones")
    ident_d = env.inp("ident")
    cnt_d = env.inp("cnt")
    xlno_d = env.dram("xlno", [TOK, DM], BF16)

    xin0 = env.sb("xin0", [P, DM], F32)
    xin1 = env.sb("xin1", [P, DM], F32)
    xst = env.sb("xst", [P, 12], F32)
    xmv = env.sb("xmv", [P, 2], F32)
    xsd = env.sb("xsd", [P, 2], F32)
    xln0 = env.sb("xln0", [P, DM], BF16)
    xln1 = env.sb("xln1", [P, DM], BF16)
    epsc = env.sb("epsc", [P, 1], F32)
    xfmo = env.sb("xfmo", [P, 8 * TOK], BF16)
    gwb = env.sb("gwb", [8, DM], BF16)
    gwt = env.sb("gwt", [P, 8 * 8], BF16)
    gbr = env.sb("gbr", [1, 8], BF16)
    idt = env.sb("idt", [P, P], BF16)
    onesr = env.sb("onesr", [1, P], BF16)
    onec = env.sb("onec", [P, 1], BF16)
    gbuf = env.sb("gbuf", [P, 8], F32)
    mx8 = env.sb("mx8", [P, 8], F32)
    indb = env.sb("indb", [P, 8], BF16)
    cbuf = env.sb("cbuf", [8, 1], F32)
    ps_g = env.ps("ps_g", [P, 8], F32)
    ps_t = env.ps("ps_t", [P, 8], BF16)
    ps_c = env.ps("ps_c", [8, 1], F32)
    t = dict(xin0=xin0, xin1=xin1, xst=xst, xmv=xmv, xsd=xsd, xln0=xln0, xln1=xln1,
             epsc=epsc)
    pr = env.pr
    pr.t = t
    pr.lnfree = {0: None, 1: None}
    pr.lnsfree = {0: None, 1: None}
    ctx.ev("vector", lambda e: e.memset(epsc[:], 1e-5))
    ctx.ev("vector", lambda e: e.memset(onec[:], 1.0))

    gw_t = ctx.dma("gpsimd", lambda e: e.dma_start(out=gwb[:], in_=gw_d[:]))
    gb_t = ctx.dma("gpsimd", lambda e: e.dma_start(out=gbr[:], in_=gb_d.unsqueeze(0)))
    id_t = ctx.dma("gpsimd", lambda e: e.dma_start(out=idt[:], in_=ident_d[:]))
    on_t = ctx.dma("gpsimd", lambda e: e.dma_start(out=onesr[:], in_=ones_d[:]))

    tfree = None
    for c in range(8):
        tc = ctx.ev("tensor", lambda e, c=c: e.transpose(
            ps_t[:], gwb[0:8, c * 128:(c + 1) * 128], idt[0:8, 0:8]),
            deps=[gw_t, id_t, (sv, tfree)], inc=st)
        tfree = ctx.ev("vector", lambda e, c=c: e.tensor_copy(
            gwt[:, c * 8:(c + 1) * 8], ps_t[:]), deps=[(st, tc)])
    gwt_v = sv.n

    emit_ln(pr, xo_d, 0, 16, xlno_d)
    ln_o = sq.barrier()
    fm_o = emit_fm_loads(pr, xfmo, xlno_d, TOK, ln_o)

    gfree = None
    for ti in range(16):
        for c in range(8):
            ctx.ev("tensor", lambda e, c=c, ti=ti: e.matmul(
                ps_g[:], xfmo[:, c * TOK + ti * 128: c * TOK + ti * 128 + 128],
                gwt[:, c * 8:(c + 1) * 8], start=(c == 0), stop=False),
                deps=([fm_o, (sv, gwt_v), (sv, gfree)] if c == 0 else []))
        mg = ctx.ev("tensor", lambda e: e.matmul(
            ps_g[:], onesr[:], gbr[:], start=False, stop=True),
            deps=[gb_t, on_t], inc=st)
        gfree = ctx.ev("vector", lambda e: e.tensor_copy(gbuf[:], ps_g[:]),
                       deps=[(st, mg)])
        ctx.ev("vector", lambda e: e.max(out=mx8[:], in_=gbuf[:]))
        iv = ctx.ev("vector", lambda e: e.tensor_scalar(
            indb[:], gbuf[:], mx8[:, 1:2], None, ALU.is_ge))
        ctx.ev("tensor", lambda e, ti=ti: e.matmul(
            ps_c[:], indb[:], onec[:], start=(ti == 0), stop=(ti == 15)),
            deps=[(sv, iv)], inc=(st if ti == 15 else None))
    cv = ctx.ev("vector", lambda e: e.tensor_copy(cbuf[:], ps_c[:]), deps=[(st, st.n)])
    ctx.dma("sync", lambda e: e.dma_start(out=cnt_d[:], in_=cbuf[:]), deps=[(sv, cv)])


def emit_l0_ffn_main(env):
    nc = env.nc
    ctx = env.ctx
    sq, gq = env.pr.sq, env.pr.gq
    sv, sa, st = env.pr.sv, env.pr.sa, env.pr.st
    xo_d = env.inp("xo")
    parts_d = env.inputs.get("parts")
    we_d = env.inp("we")
    web_d = env.inp("web")
    ones_d = env.inp("ones")
    onesc8_d = env.inp("onesc8")
    y_d = env.inp("y")
    xlno_d = env.dram("xlno", [TOK, DM], BF16)
    webf_d = env.dram("webf", [8, DM, DM], BF16)

    xin0 = env.sb("xin0", [P, DM], F32)
    xin1 = env.sb("xin1", [P, DM], F32)
    xst = env.sb("xst", [P, 12], F32)
    xmv = env.sb("xmv", [P, 2], F32)
    xsd = env.sb("xsd", [P, 2], F32)
    xln0 = env.sb("xln0", [P, DM], BF16)
    xln1 = env.sb("xln1", [P, DM], BF16)
    epsc = env.sb("epsc", [P, 1], F32)
    xfmo = env.sb("xfmo", [P, 8 * TOK], BF16)
    xacc = env.sb("xacc", [P, 16 * DM], F32)
    wetA = env.sb("wetA", [P, 8 * DM], BF16)
    wetB = env.sb("wetB", [P, 8 * DM], BF16)
    webr = env.sb("webr", [1, 8 * DM], BF16)
    parts_s = env.sb("parts_s", [8, 9], F32)
    onesf = env.sb("onesf", [1, P], F32)
    onesr = env.sb("onesr", [1, P], BF16)
    oc8 = env.sb("oc8", [8, 1], F32)
    crow = env.sb("crow", [1, 8], F32)
    cbc = env.sb("cbc", [P, 8], F32)
    etmp0 = env.sb("etmp0", [P, 512], F32)
    etmp1 = env.sb("etmp1", [P, 512], F32)
    ps_e0 = env.ps("ps_e0", [P, 512], F32)
    ps_e1 = env.ps("ps_e1", [P, 512], F32)
    ps_m = env.ps("ps_m", [P, 8], F32)
    t = dict(xin0=xin0, xin1=xin1, xst=xst, xmv=xmv, xsd=xsd, xln0=xln0, xln1=xln1,
             epsc=epsc)
    pr = env.pr
    pr.t = t
    pr.lnfree = {0: None, 1: None}
    pr.lnsfree = {0: None, 1: None}
    ctx.ev("vector", lambda e: e.memset(epsc[:], 1e-5))

    of_t = ctx.dma("sync", lambda e: e.dma_start(out=onesf[:], in_=ones_d[:]))
    o8_t = ctx.dma("sync", lambda e: e.dma_start(out=oc8[:], in_=onesc8_d[:]))
    on_t = ctx.dma("gpsimd", lambda e: e.dma_start(out=onesr[:], in_=ones_d[:]))
    wb_t = ctx.dma("gpsimd", lambda e: e.dma_start(out=webr[:], in_=web_d[:]))
    if "cnts" in env.inputs:
        cnts_d = env.inp("cnts")
        id8 = env.inp("identf")
        pt_t = ctx.dma("sync", lambda e: e.dma_start(out=parts_s[:, 0:1], in_=cnts_d[:]))
        i8_t = ctx.dma("sync", lambda e: e.dma_start(out=parts_s[0:8, 1:9] if False else parts_s[0:8, 1:9], in_=id8[0:8, 0:8]))
        m1 = ctx.ev("tensor", lambda e: e.matmul(
            ps_m[0:1, 0:8], parts_s[0:8, 0:1], parts_s[0:8, 1:9], start=True, stop=True),
            deps=[pt_t, i8_t], inc=st)
    else:
        pt_t = ctx.dma("sync", lambda e: e.dma_start(out=parts_s[0:8, 0:8], in_=parts_d[:]))
        m1 = ctx.ev("tensor", lambda e: e.matmul(
            ps_m[0:1, 0:8], oc8[:], parts_s[0:8, 0:8], start=True, stop=True),
            deps=[pt_t, o8_t], inc=st)
    c1 = ctx.ev("vector", lambda e: e.tensor_scalar(
        crow[:], ps_m[0:1, 0:8], 1.0 / 32768.0, None, ALU.mult), deps=[(st, m1)])
    m2 = ctx.ev("tensor", lambda e: e.matmul(
        ps_m[:, 0:8], onesf[:], crow[:], start=True, stop=True),
        deps=[of_t, (sv, c1)], inc=st)
    cb_v = ctx.ev("vector", lambda e: e.tensor_copy(cbc[:], ps_m[:, 0:8]),
                  deps=[(st, m2)])

    we_t = ctx.dma("gpsimd", lambda e: e.dma_start(
        out=webf_d.rearrange("e a b -> e a b"), in_=we_d.rearrange("e a b -> e a b")))

    emit_ln(pr, xo_d, 0, 16, xlno_d)
    ln_o = sq.barrier()
    fm_o = emit_fm_loads(pr, xfmo, xlno_d, TOK, ln_o)

    x_toks = []
    for ti in range(16):
        x_toks.append(ctx.dma("sync", lambda e, ti=ti: e.dma_start(
            out=xacc[:, ti * DM:(ti + 1) * DM], in_=xo_d[ti * 128:ti * 128 + 128, :])))

    wet_tok = {0: None, 1: None}
    wet_free = {0: None, 1: None}
    psfree = {0: None, 1: None}
    etfree = {0: None, 1: None}
    for ei in range(8):
        wet = (wetA, wetB)[ei % 2]
        toks = []
        for c in range(8):
            toks.append(ctx.dma("sync", lambda e, c=c, ei=ei, wet=wet: e.dma_start_transpose(
                wet[:, c * DM:(c + 1) * DM],
                webf_d[ei, 0:DM, c * 128:(c + 1) * 128]),
                deps=[we_t, (st, wet_free[ei % 2])]))
        wet_tok[ei % 2] = toks
        for ti in range(16):
            for hf in range(2):
                ps_e = (ps_e0, ps_e1)[hf]
                for c in range(8):
                    ctx.ev("tensor", lambda e, c=c, ti=ti, hf=hf, ps_e=ps_e, wet=wet: e.matmul(
                        ps_e[:], xfmo[:, c * TOK + ti * 128: c * TOK + ti * 128 + 128],
                        wet[:, c * DM + hf * 512: c * DM + hf * 512 + 512],
                        start=(c == 0), stop=False),
                        deps=([toks, fm_o, (sa, psfree[hf])] if c == 0 else []))
                me = ctx.ev("tensor", lambda e, ei=ei, hf=hf, ps_e=ps_e: e.matmul(
                    ps_e[:], onesr[:], webr[0:1, ei * DM + hf * 512: ei * DM + hf * 512 + 512],
                    start=False, stop=True),
                    deps=[wb_t, on_t], inc=st)
                etmp = (etmp0, etmp1)[hf]
                a1 = ctx.ev("scalar", lambda e, ps_e=ps_e, ei=ei, etmp=etmp: e.activation(
                    etmp[:], ps_e[:], AF.Relu, scale=cbc[:, ei:ei + 1]),
                    deps=[(st, me), (sv, cb_v), (sv, etfree[hf])])
                psfree[hf] = a1
                etfree[hf] = ctx.ev("vector", lambda e, ti=ti, hf=hf, etmp=etmp: e.tensor_add(
                    xacc[:, ti * DM + hf * 512: ti * DM + hf * 512 + 512],
                    xacc[:, ti * DM + hf * 512: ti * DM + hf * 512 + 512], etmp[:]),
                    deps=[(sa, a1), x_toks[ti]])
        wet_free[ei % 2] = st.n
    acc_v = sv.n
    for ti in range(16):
        ctx.dma("sync", lambda e, ti=ti: e.dma_start(
            out=y_d[ti * 128:ti * 128 + 128, :], in_=xacc[:, ti * DM:(ti + 1) * DM]),
            deps=[(sv, acc_v)])


def run_l0_ffn(x, ins):
    nc_a = _get_prog("l0f_cnt", build_l0_ffn_counts)
    nc_b = _get_prog("l0f_main", build_l0_ffn_main)
    ident = np.eye(P, dtype=np.float32)
    in_a = []
    for c in range(NCORE):
        b, p = c // 2, c % 2
        in_a.append({
            "xo": np.ascontiguousarray(x[b][own_rows(p)]),
            "gw": ins["L0_ffn_gate_w"], "gb": ins["L0_ffn_gate_b"],
            "ones": _ONES, "ident": ident,
        })
    res_a = run_bass_kernel_spmd(nc_a, in_a, core_ids=list(range(NCORE)))
    parts = np.stack([res_a.results[c]["cnt"][:, 0] for c in range(NCORE)])
    web = np.ascontiguousarray(ins["L0_ffn_expert_b"].reshape(1, 8 * DM))
    in_b = []
    for c in range(NCORE):
        b, p = c // 2, c % 2
        in_b.append({
            "xo": in_a[c]["xo"],
            "parts": parts, "we": ins["L0_ffn_expert_w"], "web": web,
            "ones": _ONES, "onesc8": np.ones((8, 1), np.float32),
        })
    res_b = run_bass_kernel_spmd(nc_b, in_b, core_ids=list(range(NCORE)))
    y = np.empty((NB, S, DM), np.float32)
    return _scatter(y, [r["y"] for r in res_b.results])



def emit_l1_attn_a(env):
    nc = env.nc
    ctx = env.ctx
    sq, gq = env.pr.sq, env.pr.gq
    sv, sa, st = env.pr.sv, env.pr.sa, env.pr.st
    x_d = env.inputs.get("x")
    xo_d = env.inp("xo")
    wa_d = env.inp("Wa")
    bqk_d = env.inp("bqk")
    bav_d = env.inp("bav")
    wp_d = env.inp("Wp")
    bp_d = env.inp("bp")
    ones_d = env.inp("ones")
    identf_d = env.inp("identf")
    mmq_d = env.inp("mmq")
    y_d = env.inp("y")
    xln_d = env.dram("xln", [S, DM], BF16)
    xlno_d = env.dram("xlno", [TOK, DM], BF16)
    k_d = env.dram("kst", [8, P, S], BF16)
    mrow_d = env.dram("mrowd", [32, P], F32)
    v_d = env.dram("vst", [S, DM], BF16)

    xin0 = env.sb("xin0", [P, DM], F32)
    xin1 = env.sb("xin1", [P, DM], F32)
    xst = env.sb("xst", [P, 12], F32)
    xmv = env.sb("xmv", [P, 2], F32)
    xsd = env.sb("xsd", [P, 2], F32)
    xln0 = env.sb("xln0", [P, DM], BF16)
    xln1 = env.sb("xln1", [P, DM], BF16)
    epsc = env.sb("epsc", [P, 1], F32)
    xfmo = env.sb("xfmo", [P, 8 * TOK], BF16)
    xfb0 = env.sb("xfb0", [P, 8 * 512], BF16)
    wA = env.sb("wA", [P, 8 * DM], BF16)
    wB = env.sb("wB", [P, 8 * DM], BF16)
    bqk = env.sb("bqks", [P, 16], F32)
    bavr = env.sb("bavr", [1, DM], BF16)
    bpr = env.sb("bpr", [1, DM], BF16)
    onesr = env.sb("onesr", [1, P], BF16)
    onesf = env.sb("onesf", [1, P], F32)
    identf = env.sb("identfs", [P, P], F32)
    qstE = env.sb("qstE", [65, 8 * TOK], BF16)
    qstO = env.sb("qstO", [65, 8 * TOK], BF16)
    kcur0 = env.sb("kcur0", [65, S], BF16)
    kcur1 = env.sb("kcur1", [65, S], BF16)
    vg0 = env.sb("vg0", [P, 32 * 65], BF16)
    vg1 = env.sb("vg1", [P, 32 * 65], BF16)
    mmq = env.sb("mmqs", [P, 4 * 256], BF16)
    m_all = env.sb("m_all", [P, 32], F32)
    mcols = env.sb("mcols", [P, 8], F32)
    mrowt = env.sb("mrowt", [32, P], F32)
    ptb0 = env.sb("ptb0", [P, 256], BF16)
    ptb1 = env.sb("ptb1", [P, 256], BF16)
    lrow = env.sb("lrow", [1, 256], F32)
    invlb = env.sb("invlb", [64, 256], F32)
    xres0 = env.sb("xres0", [P, DM], F32)
    xres1 = env.sb("xres1", [P, DM], F32)
    ybuf0 = env.sb("ybuf0", [P, DM], F32)
    ybuf1 = env.sb("ybuf1", [P, DM], F32)
    psA = env.ps("psA", [P, 512], F32)
    psB = env.ps("psB", [P, 512], F32)
    psP = env.ps("psP", [P, 512], F32)
    psP2 = env.ps("psP2", [P, 512], F32)
    psT0 = env.ps("psT0", [P, 256], F32)
    psT1 = env.ps("psT1", [P, 256], F32)
    psO0 = env.ps("psO0", [P, 512], F32)
    psO1 = env.ps("psO1", [P, 512], F32)
    t = dict(xin0=xin0, xin1=xin1, xst=xst, xmv=xmv, xsd=xsd, xln0=xln0, xln1=xln1,
             epsc=epsc)
    pr = env.pr
    pr.t = t
    pr.lnfree = {0: None, 1: None}
    pr.lnsfree = {0: None, 1: None}
    ctx.ev("vector", lambda e: e.memset(epsc[:], 1e-5))
    ctx.ev("vector", lambda e: e.memset(kcur0[64:65, :], 1.0))
    kcinit = ctx.ev("vector", lambda e: e.memset(kcur1[64:65, :], 1.0))
    ctx.ev("vector", lambda e: e.memset(vg0[:], 1.0))
    vginit = ctx.ev("vector", lambda e: e.memset(vg1[:], 1.0))

    bqk_t = ctx.dma("sync", lambda e: e.dma_start(out=bqk[:], in_=bqk_d[:]))
    onf_t = ctx.dma("sync", lambda e: e.dma_start(out=onesf[:], in_=ones_d[:]))
    idf_t = ctx.dma("sync", lambda e: e.dma_start(out=identf[:], in_=identf_d[:]))
    bav_t = ctx.dma("gpsimd", lambda e: e.dma_start(out=bavr[:], in_=bav_d[:]))
    bp_t = ctx.dma("gpsimd", lambda e: e.dma_start(out=bpr[:], in_=bp_d[:]))
    on_t = ctx.dma("gpsimd", lambda e: e.dma_start(out=onesr[:], in_=ones_d[:]))
    mm_t = ctx.dma("gpsimd", lambda e: e.dma_start(out=mmq[:], in_=mmq_d[:]))

    if "xfull" in env.inputs:
        xln_d = env.inp("xfull")
        ln_b = []
    else:
        xln_d = env.dram("xlnb", [S, DM], BF16)
        emit_ln(pr, x_d, 0, 32, xln_d)
        ln_b = sq.barrier()
    emit_ln(pr, xo_d, 0, 16, xlno_d)
    ln_o = sq.barrier()
    fm_o = emit_fm_loads(pr, xfmo, xlno_d, TOK, ln_o)
    ktile = xln1[:, 0:512]
    vtile = xln0

    wq_t = ctx.dma("gpsimd", lambda e: e.dma_start(
        out=wA[:].rearrange("p (c m) -> p c m", c=8),
        in_=wa_d[:, 0:DM].rearrange("(c p) m -> p c m", p=128)))
    qev = None
    for m in range(8):
        for ob in range(4):
            mq = proj_accum(pr, psP, wA, DM, m, 128, xfmo, TOK, ob * 512, 512,
                            [wq_t, fm_o, (sv, qev)])
            for hh in range(2):
                qst = (qstE, qstO)[hh]
                qev = ctx.ev("vector", lambda e, m=m, ob=ob, hh=hh, qst=qst: e.tensor_scalar(
                    qst[0:64, m * TOK + ob * 512: m * TOK + ob * 512 + 512],
                    psP[hh * 64:hh * 64 + 64, :], bqk[hh * 64:hh * 64 + 64, m:m + 1],
                    None, ALU.add),
                    deps=[(st, mq), bqk_t])
    q_mm_done = st.n

    wv_t = ctx.dma("gpsimd", lambda e: e.dma_start(
        out=wA[:].rearrange("p (c m) -> p c m", c=8),
        in_=wa_d[:, 2 * DM:3 * DM].rearrange("(c p) m -> p c m", p=128)),
        deps=[(st, q_mm_done)])
    wk_t = ctx.dma("gpsimd", lambda e: e.dma_start(
        out=wB[:].rearrange("p (c m) -> p c m", c=8),
        in_=wa_d[:, DM:2 * DM].rearrange("(c p) m -> p c m", p=128)))
    xfb_free = {0: None, 1: None}
    kev = qev
    kst_tok = None
    vst_tok = None
    vev = {0: None, 1: None}
    for bi in range(8):
        fb = xfb0
        fmv = []
        for c in range(8):
            fmv.append(ctx.dma("sync", lambda e, c=c, bi=bi, fb=fb: e.dma_start_transpose(
                fb[:, c * 512:(c + 1) * 512],
                xln_d[bi * 512:(bi + 1) * 512, c * 128:(c + 1) * 128]),
                deps=[ln_b, (st, xfb_free[0])]))
        for m in range(8):
            mk = proj_accum(pr, psP, wB, DM, m, 128, fb, 512, 0, 512,
                            [wk_t, fmv, (sv, kev)])
            kev = ctx.ev("vector", lambda e, m=m: e.tensor_scalar(
                ktile[:], psP[:], bqk[:, 8 + m:8 + m + 1], None, ALU.add),
                deps=[(st, mk), bqk_t, kst_tok] + ln_o)
            kst_tok = ctx.dma("sync", lambda e, m=m, bi=bi: e.dma_start(
                out=k_d[m, :, bi * 512:(bi + 1) * 512], in_=ktile[:]),
                deps=[(sv, kev)])
        for tt in range(4):
            for hf in range(2):
                ps_v = (psA, psB)[hf]
                for c in range(8):
                    ctx.ev("tensor", lambda e, c=c, tt=tt, hf=hf, ps_v=ps_v, fb=fb: e.matmul(
                        ps_v[:], fb[:, c * 512 + tt * 128: c * 512 + tt * 128 + 128],
                        wA[:, c * DM + hf * 512: c * DM + hf * 512 + 512],
                        start=(c == 0), stop=False),
                        deps=([wv_t, fmv, (sv, vev[hf])] if c == 0 else []))
                mv = ctx.ev("tensor", lambda e, hf=hf, ps_v=ps_v: e.matmul(
                    ps_v[:], onesr[:], bavr[0:1, hf * 512: hf * 512 + 512],
                    start=False, stop=True), deps=[bav_t, on_t], inc=st)
                vev[hf] = ctx.ev("vector", lambda e, hf=hf, ps_v=ps_v: e.tensor_copy(
                    vtile[:, hf * 512:hf * 512 + 512], ps_v[:]),
                    deps=[(st, mv), vst_tok] + ln_o)
            vst_tok = ctx.dma("sync", lambda e, bi=bi, tt=tt: e.dma_start(
                out=v_d[bi * 512 + tt * 128: bi * 512 + tt * 128 + 128, :], in_=vtile[:]),
                deps=[(sv, vev[0]), (sv, vev[1])])
        xfb_free[0] = st.n
    kv_store = sq.barrier()
    proj_done = st.n
    wh_toks = []
    for h in range(16):
        wHx = (wA, wB)[h // 8]
        wh_toks.append(ctx.dma("gpsimd", lambda e, h=h, wHx=wHx: e.dma_start(
            out=wHx[0:64, (h % 8) * DM:(h % 8) * DM + DM],
            in_=wp_d[h * 64:(h + 1) * 64, :]),
            deps=[(st, proj_done)]))

    kc_free = {0: None, 1: None}
    vg_free = {0: None, 1: None}
    ptfree = {0: None, 1: None}
    ptbfree = {0: None, 1: None}
    ovfree = {0: None, 1: None}
    bcfree = None
    lrfree = None
    tr_prev = None
    nm_prev = []
    hdone = None
    for h2 in range(8):
        kc_toks = []
        for hh in range(2):
            kc = (kcur0, kcur1)[hh]
            kc_toks.append(ctx.dma("sync", lambda e, h2=h2, hh=hh, kc=kc: e.dma_start(
                out=kc[0:64, :], in_=k_d[h2, hh * 64:hh * 64 + 64, :]),
                deps=[kv_store, (st, kc_free[hh])]))
        pafree = (st, tr_prev)
        mav = None
        first_red = True
        for hh in range(2):
            kc = (kcur0, kcur1)[hh]
            qst = (qstE, qstO)[hh]
            slot = h2 * TOK
            for i in range(16):
                nb = (2 * i + 2 + 3) // 4
                for b in range(nb):
                    cols = min(512, (2 * i + 2) * 128 - b * 512)
                    ma = ctx.ev("tensor", lambda e, kc=kc, qst=qst, slot=slot, i=i, b=b, cols=cols: e.matmul(
                        psA[:, 0:cols], qst[0:64, slot + i * 128: slot + i * 128 + 128],
                        kc[0:64, b * 512: b * 512 + cols], start=True, stop=True),
                        deps=[kc_toks[hh], pafree], inc=st)
                    pa_deps = [(st, ma)] + ([(st, tr_prev)] if first_red else [])
                    first_red = False
                    pv_ = ctx.ev("vector", lambda e, b=b, cols=cols: e.reduce_max(
                        mcols[:, b:b + 1].unsqueeze(2),
                        psA[:, 0:cols].rearrange("p (a b) -> p a b", a=1), axis=AX.X),
                        deps=pa_deps)
                    pafree = (sv, pv_)
                mav = ctx.ev("vector", lambda e, hh=hh, i=i, nb=nb: e.reduce_max(
                    m_all[:, hh * 16 + i:hh * 16 + i + 1].unsqueeze(2),
                    mcols[:, 0:nb].rearrange("p (a b) -> p a b", a=1), axis=AX.X))
        tr = ctx.ev("tensor", lambda e: e.transpose(psB[0:32, 0:128], m_all[:], identf[:]),
                    deps=[(sv, mav), idf_t, (sv, bcfree)], inc=st)
        tr_prev = tr
        mr = ctx.ev("vector", lambda e: e.tensor_scalar(
            mrowt[:], psB[0:32, 0:128], -1.0, None, ALU.mult),
            deps=[(st, tr)] + nm_prev)
        mr_st = ctx.dma("sync", lambda e: e.dma_start(out=mrow_d[:], in_=mrowt[:]),
                        deps=[(sv, mr)])
        nm_toks = []
        for hh in range(2):
            qst = (qstE, qstO)[hh]
            nm_toks.append(ctx.dma("gpsimd", lambda e, hh=hh, qst=qst, h2=h2: e.dma_start(
                out=qst[64:65, h2 * TOK: h2 * TOK + TOK],
                in_=mrow_d[hh * 16:hh * 16 + 16, :].rearrange("a b -> (a b)").unsqueeze(0)),
                deps=[mr_st]))
        nm_prev = list(nm_toks)
        for hh in range(2):
            kc = (kcur0, kcur1)[hh]
            qst = (qstE, qstO)[hh]
            h = 2 * h2 + hh
            slot = h2 * TOK
            vg = (vg0, vg1)[hh]
            vg_toks = []
            for qq in range(4):
                vg_toks.append(ctx.dma("sync", lambda e, qq=qq, h=h, vg=vg: e.dma_start(
                    out=vg[:].rearrange("p (n d) -> p n d", n=32)[:, qq * 8:(qq + 1) * 8, 0:64],
                    in_=v_d[qq * 1024:(qq + 1) * 1024, h * 64:(h + 1) * 64]
                        .rearrange("(n p) d -> p n d", p=128)),
                    deps=[kv_store, (st, vg_free[hh]), (sv, vginit)]))
            for qb in range(8):
                nkt = 4 * qb + 4
                psO = (psO0, psO1)[qb % 2]
                for kt in range(nkt):
                    psT = (psT0, psT1)[kt % 2]
                    ptb = (ptb0, ptb1)[kt % 2]
                    mt = ctx.ev("tensor", lambda e, kc=kc, qst=qst, slot=slot, qb=qb, kt=kt, psT=psT: e.matmul(
                        psT[:], kc[0:65, kt * 128: kt * 128 + 128],
                        qst[0:65, slot + qb * 256: slot + qb * 256 + 256],
                        start=True, stop=True),
                        deps=[nm_toks[hh], (sa, ptfree[kt % 2]), (sv, kcinit)], inc=st)
                    av = ctx.ev("scalar", lambda e, psT=psT, ptb=ptb: e.activation(
                        ptb[:], psT[:], AF.Exp, scale=0.125),
                        deps=[(st, mt), (st, ptbfree[kt % 2])])
                    ptfree[kt % 2] = av
                    lastop = (sa, av)
                    if kt >= 4 * qb:
                        mk = ctx.ev("vector", lambda e, ptb=ptb, kt=kt, qb=qb: e.tensor_mul(
                            ptb[:], ptb[:], mmq[:, (kt - 4 * qb) * 256: (kt - 4 * qb) * 256 + 256]),
                            deps=[(sa, av), mm_t])
                        lastop = (sv, mk)
                    mpv = ctx.ev("tensor", lambda e, vg=vg, kt=kt, ptb=ptb, psO=psO, nkt=nkt: e.matmul(
                        psO[0:65, 0:256], vg[:, kt * 65: kt * 65 + 65], ptb[:],
                        start=(kt == 0), stop=(kt == nkt - 1)),
                        deps=[lastop] + vg_toks + ([(sv, ovfree[qb % 2])] if kt == 0 else []),
                        inc=st)
                    ptbfree[kt % 2] = mpv
                pv_done = st.n
                c1 = ctx.ev("vector", lambda e, psO=psO: e.tensor_copy(lrow[:], psO[64:65, 0:256]),
                            deps=[(st, pv_done), (st, lrfree)])
                c2 = ctx.ev("vector", lambda e: e.reciprocal(lrow[:], lrow[:]))
                mb = ctx.ev("tensor", lambda e: e.matmul(
                    psB[0:64, 0:256], onesf[0:1, 0:64], lrow[:], start=True, stop=True),
                    deps=[(sv, c2), onf_t, (sv, bcfree)], inc=st)
                lrfree = mb
                c3 = ctx.ev("vector", lambda e: e.tensor_copy(invlb[:], psB[0:64, 0:256]),
                            deps=[(st, mb)])
                bcfree = c3
                hv = ctx.ev("vector", lambda e, psO=psO, qst=qst, slot=slot, qb=qb: e.tensor_mul(
                    qst[0:64, slot + qb * 256: slot + qb * 256 + 256],
                    psO[0:64, 0:256], invlb[:]), deps=[(sv, c3)])
                ovfree[qb % 2] = hv
                hdone = hv
            vg_free[hh] = st.n
        kc_free = {0: st.n, 1: st.n}

    yfree = {0: None, 1: None}
    ystore = {0: None, 1: None}
    opfree = {0: None, 1: None}
    for ti in range(16):
        bi = ti % 2
        xres = (xres0, xres1)[bi]
        ybuf = (ybuf0, ybuf1)[bi]
        lx = ctx.dma("sync", lambda e, xres=xres, ti=ti: e.dma_start(
            out=xres[:], in_=xo_d[ti * 128:ti * 128 + 128, :]),
            deps=[(sv, yfree[bi])])
        for hf in range(2):
            ps_y = (psP, psP2)[hf]
            for h in range(16):
                qst = (qstE, qstO)[h % 2]
                wHx = (wA, wB)[h // 8]
                ctx.ev("tensor", lambda e, h=h, ti=ti, hf=hf, ps_y=ps_y, qst=qst, wHx=wHx: e.matmul(
                    ps_y[:], qst[0:64, (h // 2) * TOK + ti * 128: (h // 2) * TOK + ti * 128 + 128],
                    wHx[0:64, (h % 8) * DM + hf * 512: (h % 8) * DM + hf * 512 + 512],
                    start=(h == 0), stop=False),
                    deps=(([(sv, hdone), (sv, opfree[hf])] + wh_toks) if h == 0 else []))
            my = ctx.ev("tensor", lambda e, hf=hf, ps_y=ps_y: e.matmul(
                ps_y[:], onesr[:], bpr[0:1, hf * 512: hf * 512 + 512],
                start=False, stop=True), deps=[bp_t], inc=st)
            opfree[hf] = ctx.ev("vector", lambda e, ps_y=ps_y, hf=hf, ybuf=ybuf, xres=xres: e.tensor_add(
                ybuf[:, hf * 512:hf * 512 + 512], ps_y[:], xres[:, hf * 512:hf * 512 + 512]),
                deps=[(st, my), lx, ystore[bi]])
        yv = sv.n
        yfree[bi] = yv
        ystore[bi] = ctx.dma("sync", lambda e, ybuf=ybuf, ti=ti: e.dma_start(
            out=y_d[ti * 128:ti * 128 + 128, :], in_=ybuf[:]),
            deps=[(sv, yv)])


def run_l1_attn_a(x, ins):
    nc = _get_prog("l1aa", build_l1_attn_a)
    ba = ins["L1_attn_a_b_attn"]
    bqk = np.ascontiguousarray(np.concatenate(
        [_pc(ba[0:DM]), _pc(ba[DM:2 * DM])], axis=1))
    bav = np.ascontiguousarray(ba[2 * DM:3 * DM].reshape(1, DM))
    bp = np.ascontiguousarray(ins["L1_attn_a_b_proj"].reshape(1, DM))
    identf = np.eye(P, dtype=np.float32)
    tri = np.tril(np.ones((P, P), np.float32)).T
    onesm = np.ones((P, P), np.float32)
    zerosm = np.zeros((P, P), np.float32)
    in_maps = []
    for c in range(NCORE):
        b, p = c // 2, c % 2
        mm = np.empty((4, P, 256), np.float32)
        for dkt in range(4):
            left = tri if dkt == p else (onesm if dkt < p else zerosm)
            right = tri if dkt == 2 + p else (onesm if dkt < 2 + p else zerosm)
            mm[dkt, :, 0:128] = left
            mm[dkt, :, 128:256] = right
        in_maps.append({
            "x": np.ascontiguousarray(x[b]),
            "xo": np.ascontiguousarray(x[b][own_rows(p)]),
            "Wa": ins["L1_attn_a_W_attn"], "bqk": bqk, "bav": bav,
            "Wp": ins["L1_attn_a_W_proj"], "bp": bp,
            "ones": _ONES, "identf": identf,
            "mmq": np.ascontiguousarray(mm.transpose(1, 0, 2).reshape(P, 4 * 256)),
        })
    res = run_bass_kernel_spmd(nc, in_maps, core_ids=list(range(NCORE)))
    y = np.empty((NB, S, DM), np.float32)
    return _scatter(y, [r["y"] for r in res.results])



def emit_l1_attn_b(env):
    nc = env.nc
    ctx = env.ctx
    sq, gq = env.pr.sq, env.pr.gq
    sv, sa, st = env.pr.sv, env.pr.sa, env.pr.st
    x_d = env.inputs.get("x")
    xo_d = env.inp("xo")
    sinb_d = env.inputs.get("sinb")
    cosb_d = env.inputs.get("cosb")
    sino_d = env.inp("sino")
    coso_d = env.inp("coso")
    wq_d = env.inp("Wq")
    wk_d = env.inp("Wk")
    wv_d = env.inp("Wv")
    wo_d = env.inp("Wo")
    bo_d = env.inp("bo")
    ones_d = env.inp("ones")
    identf_d = env.inp("identf")
    mmq_d = env.inp("mmq")
    y_d = env.inp("y")
    xln_d = env.dram("xln", [S, DM], BF16)
    rxln_d = env.dram("rxln", [S, DM], BF16)
    xlno_d = env.dram("xlno", [TOK, DM], BF16)
    rxlno_d = env.dram("rxlno", [TOK, DM], BF16)
    wbf_d = env.dram("wbf", [4, DM, DM], BF16)
    k_d = env.dram("kst", [8, P, S], BF16)
    v_d = env.dram("vst", [S, DM], BF16)
    mrow_d = env.dram("mrowd", [32, P], F32)

    xin0 = env.sb("xin0", [P, DM], F32)
    xin1 = env.sb("xin1", [P, DM], F32)
    xst = env.sb("xst", [P, 12], F32)
    xmv = env.sb("xmv", [P, 2], F32)
    xsd = env.sb("xsd", [P, 2], F32)
    xln0 = env.sb("xln0", [P, DM], BF16)
    xln1 = env.sb("xln1", [P, DM], BF16)
    epsc = env.sb("epsc", [P, 1], F32)
    xfmo = env.sb("xfmo", [P, 8 * TOK], BF16)
    xfb0 = env.sb("xfb0", [P, 8 * 512], BF16)
    wA = env.sb("wA", [P, 8 * DM], BF16)
    wB = env.sb("wB", [P, 8 * DM], BF16)
    onesr = env.sb("oness", [1, P], BF16)
    onesf = env.sb("onesf", [1, P], F32)
    identf = env.sb("identfs", [P, P], F32)
    bor = env.sb("bor", [1, DM], BF16)
    qstE = env.sb("qstE", [65, 8 * TOK], BF16)
    qstO = env.sb("qstO", [65, 8 * TOK], BF16)
    kcur0 = env.sb("kcur0", [65, S], BF16)
    kcur1 = env.sb("kcur1", [65, S], BF16)
    vg0 = env.sb("vg0", [P, 32 * 65], BF16)
    vg1 = env.sb("vg1", [P, 32 * 65], BF16)
    mmq = env.sb("mmqs", [P, 4 * 256], BF16)
    m_all = env.sb("m_all", [P, 32], F32)
    mcols = env.sb("mcols", [P, 8], F32)
    mrowt = env.sb("mrowt", [32, P], F32)
    ptb0 = env.sb("ptb0", [P, 256], BF16)
    ptb1 = env.sb("ptb1", [P, 256], BF16)
    lrow = env.sb("lrow", [1, 256], F32)
    invlb = env.sb("invlb", [64, 256], F32)
    xres0 = env.sb("xres0", [P, DM], F32)
    xres1 = env.sb("xres1", [P, DM], F32)
    ybuf0 = env.sb("ybuf0", [P, DM], F32)
    ybuf1 = env.sb("ybuf1", [P, DM], F32)
    psA = env.ps("psA", [P, 512], F32)
    psB = env.ps("psB", [P, 512], F32)
    psP = env.ps("psP", [P, 512], F32)
    psP2 = env.ps("psP2", [P, 512], F32)
    psT0 = env.ps("psT0", [P, 256], F32)
    psT1 = env.ps("psT1", [P, 256], F32)
    psO0 = env.ps("psO0", [P, 512], F32)
    psO1 = env.ps("psO1", [P, 512], F32)
    t = dict(xin0=xin0, xin1=xin1, xst=xst, xmv=xmv, xsd=xsd, xln0=xln0, xln1=xln1,
             epsc=epsc)
    pr = env.pr
    pr.t = t
    pr.lnfree = {0: None, 1: None}
    pr.lnsfree = {0: None, 1: None}
    ctx.ev("vector", lambda e: e.memset(epsc[:], 1e-5))
    ctx.ev("vector", lambda e: e.memset(kcur0[64:65, :], 1.0))
    kcinit = ctx.ev("vector", lambda e: e.memset(kcur1[64:65, :], 1.0))
    ctx.ev("vector", lambda e: e.memset(vg0[:], 1.0))
    vginit = ctx.ev("vector", lambda e: e.memset(vg1[:], 1.0))

    onf_t = ctx.dma("sync", lambda e: e.dma_start(out=onesf[:], in_=ones_d[:]))
    idf_t = ctx.dma("sync", lambda e: e.dma_start(out=identf[:], in_=identf_d[:]))
    bo_t = ctx.dma("gpsimd", lambda e: e.dma_start(out=bor[:], in_=bo_d[:]))
    on_t = ctx.dma("gpsimd", lambda e: e.dma_start(out=onesr[:], in_=ones_d[:]))
    mm_t = ctx.dma("gpsimd", lambda e: e.dma_start(out=mmq[:], in_=mmq_d[:]))
    wcast = []
    for i, wd in enumerate((wq_d, wk_d, wv_d, wo_d)):
        wcast.append(ctx.dma("gpsimd", lambda e, i=i, wd=wd: e.dma_start(
            out=wbf_d[i], in_=wd[:])))

    sct_s, sct_c, xf, rot = xres1, ybuf1, xres0, ybuf0
    rx0, rx1 = xfb0[:, 0:DM], xfb0[:, DM:2 * DM]
    if "xfull" in env.inputs:
        xln_d = env.inp("xfull")
        rxln_d = env.inp("rxfull")
    else:
        emit_ln_rope(pr, x_d, 0, 32, xln_d, rxln_d, sinb_d, cosb_d,
                     xf, rot, sct_s, sct_c, rx0, rx1)
    emit_ln_rope(pr, xo_d, 0, 16, xlno_d, rxlno_d, sino_d, coso_d,
                 xf, rot, sct_s, sct_c, rx0, rx1)
    ln_all = sq.barrier()
    fm_o = emit_fm_loads(pr, xfmo, rxlno_d, TOK, ln_all)

    wqt = emit_wt_loads(pr, wA, wbf_d[0], deps=[wcast[0]])
    qev = None
    for m in range(8):
        for ob in range(4):
            mq = proj_accum(pr, psP, wA, DM, m, 128, xfmo, TOK, ob * 512, 512,
                            wqt + [fm_o, (sv, qev)])
            for hh in range(2):
                qst = (qstE, qstO)[hh]
                qev = ctx.ev("vector", lambda e, m=m, ob=ob, hh=hh, qst=qst: e.tensor_copy(
                    qst[0:64, m * TOK + ob * 512: m * TOK + ob * 512 + 512],
                    psP[hh * 64:hh * 64 + 64, :]),
                    deps=[(st, mq)])
    q_mm_done = st.n

    wkt = emit_wt_loads(pr, wB, wbf_d[1], deps=[wcast[1]])
    ktile = xln1[:, 0:512]
    xfb_free = [q_mm_done]
    kev = qev
    kst_tok = None
    for bi in range(8):
        fb = xfb0
        fmv = []
        for c in range(8):
            fmv.append(ctx.dma("sync", lambda e, c=c, bi=bi, fb=fb: e.dma_start_transpose(
                fb[:, c * 512:(c + 1) * 512],
                rxln_d[bi * 512:(bi + 1) * 512, c * 128:(c + 1) * 128]),
                deps=ln_all + [(st, xfb_free[0])]))
        for m in range(8):
            mk = proj_accum(pr, psP, wB, DM, m, 128, fb, 512, 0, 512,
                            wkt + fmv + [(sv, kev)])
            kev = ctx.ev("vector", lambda e, m=m: e.tensor_copy(ktile, psP[:]),
                         deps=[(st, mk), kst_tok] + ln_all)
            kst_tok = ctx.dma("sync", lambda e, m=m, bi=bi: e.dma_start(
                out=k_d[m, :, bi * 512:(bi + 1) * 512], in_=ktile),
                deps=[(sv, kev)])
        xfb_free[0] = st.n
    wvt = emit_wt_loads(pr, wA, wbf_d[2], deps=[wcast[2], (st, q_mm_done)])
    vtile = xln0
    vev = {0: None, 1: None}
    vst_tok = None
    for bi in range(8):
        fb = xfb0
        fmv = []
        for c in range(8):
            fmv.append(ctx.dma("sync", lambda e, c=c, bi=bi, fb=fb: e.dma_start_transpose(
                fb[:, c * 512:(c + 1) * 512],
                xln_d[bi * 512:(bi + 1) * 512, c * 128:(c + 1) * 128]),
                deps=ln_all + [(st, xfb_free[0])]))
        for tt in range(4):
            for hf in range(2):
                ps_v = (psA, psB)[hf]
                for c in range(8):
                    ctx.ev("tensor", lambda e, c=c, tt=tt, hf=hf, ps_v=ps_v, fb=fb: e.matmul(
                        ps_v[:], fb[:, c * 512 + tt * 128: c * 512 + tt * 128 + 128],
                        wA[:, c * DM + hf * 512: c * DM + hf * 512 + 512],
                        start=(c == 0), stop=(c == 7)),
                        deps=(wvt + fmv + [(sv, vev[hf])] if c == 0 else []),
                        inc=(st if c == 7 else None))
                mv = st.n
                vev[hf] = ctx.ev("vector", lambda e, hf=hf, ps_v=ps_v: e.tensor_copy(
                    vtile[:, hf * 512:hf * 512 + 512], ps_v[:]),
                    deps=[(st, mv), vst_tok] + ln_all)
            vst_tok = ctx.dma("sync", lambda e, bi=bi, tt=tt: e.dma_start(
                out=v_d[bi * 512 + tt * 128: bi * 512 + tt * 128 + 128, :], in_=vtile[:]),
                deps=[(sv, vev[0]), (sv, vev[1])])
        xfb_free[0] = st.n
    kv_store = sq.barrier()
    proj_done = st.n
    wot = emit_wt_loads(pr, wA, wbf_d[3], deps=[wcast[3], (st, proj_done)])
    wodd = None
    for j in range(8):
        wodd = ctx.ev("vector", lambda e, j=j: e.tensor_copy(
            wB[0:64, j * DM:(j + 1) * DM], wA[64:128, j * DM:(j + 1) * DM]),
            deps=[wot[j], (st, proj_done)])
    wot = wot + [(sv, wodd)]

    kc_free = {0: None, 1: None}
    vg_free = {0: None, 1: None}
    ptfree = {0: None, 1: None}
    ptbfree = {0: None, 1: None}
    ovfree = {0: None, 1: None}
    bcfree = None
    lrfree = None
    tr_prev = None
    nm_prev = []
    hdone = None
    for h2 in range(8):
        kc_toks = []
        for hh in range(2):
            kc = (kcur0, kcur1)[hh]
            kc_toks.append(ctx.dma("sync", lambda e, h2=h2, hh=hh, kc=kc: e.dma_start(
                out=kc[0:64, :], in_=k_d[h2, hh * 64:hh * 64 + 64, :]),
                deps=kv_store + [(st, kc_free[hh])]))
        pafree = (st, tr_prev)
        mav = None
        first_red = True
        for hh in range(2):
            kc = (kcur0, kcur1)[hh]
            qst = (qstE, qstO)[hh]
            slot = h2 * TOK
            for i in range(16):
                nb = (2 * i + 2 + 3) // 4
                for b in range(nb):
                    cols = min(512, (2 * i + 2) * 128 - b * 512)
                    ma = ctx.ev("tensor", lambda e, kc=kc, qst=qst, slot=slot, i=i, b=b, cols=cols: e.matmul(
                        psA[:, 0:cols], qst[0:64, slot + i * 128: slot + i * 128 + 128],
                        kc[0:64, b * 512: b * 512 + cols], start=True, stop=True),
                        deps=[kc_toks[hh], pafree], inc=st)
                    pa_deps = [(st, ma)] + ([(st, tr_prev)] if first_red else [])
                    first_red = False
                    pv_ = ctx.ev("vector", lambda e, b=b, cols=cols: e.reduce_max(
                        mcols[:, b:b + 1].unsqueeze(2),
                        psA[:, 0:cols].rearrange("p (a b) -> p a b", a=1), axis=AX.X),
                        deps=pa_deps)
                    pafree = (sv, pv_)
                mav = ctx.ev("vector", lambda e, hh=hh, i=i, nb=nb: e.reduce_max(
                    m_all[:, hh * 16 + i:hh * 16 + i + 1].unsqueeze(2),
                    mcols[:, 0:nb].rearrange("p (a b) -> p a b", a=1), axis=AX.X))
        tr = ctx.ev("tensor", lambda e: e.transpose(psB[0:32, 0:128], m_all[:], identf[:]),
                    deps=[(sv, mav), idf_t, (sv, bcfree)], inc=st)
        tr_prev = tr
        mr = ctx.ev("vector", lambda e: e.tensor_scalar(
            mrowt[:], psB[0:32, 0:128], -1.0, None, ALU.mult),
            deps=[(st, tr)] + nm_prev)
        mr_st = ctx.dma("sync", lambda e: e.dma_start(out=mrow_d[:], in_=mrowt[:]),
                        deps=[(sv, mr)])
        nm_toks = []
        for hh in range(2):
            qst = (qstE, qstO)[hh]
            nm_toks.append(ctx.dma("gpsimd", lambda e, hh=hh, qst=qst, h2=h2: e.dma_start(
                out=qst[64:65, h2 * TOK: h2 * TOK + TOK],
                in_=mrow_d[hh * 16:hh * 16 + 16, :].rearrange("a b -> (a b)").unsqueeze(0)),
                deps=[mr_st]))
        nm_prev = list(nm_toks)
        for hh in range(2):
            kc = (kcur0, kcur1)[hh]
            qst = (qstE, qstO)[hh]
            h = 2 * h2 + hh
            slot = h2 * TOK
            vg = (vg0, vg1)[hh]
            vg_toks = []
            for qq in range(4):
                vg_toks.append(ctx.dma("sync", lambda e, qq=qq, h=h, vg=vg: e.dma_start(
                    out=vg[:].rearrange("p (n d) -> p n d", n=32)[:, qq * 8:(qq + 1) * 8, 0:64],
                    in_=v_d[qq * 1024:(qq + 1) * 1024, h * 64:(h + 1) * 64]
                        .rearrange("(n p) d -> p n d", p=128)),
                    deps=kv_store + [(st, vg_free[hh]), (sv, vginit)]))
            for qb in range(8):
                nkt = 4 * qb + 4
                psO = (psO0, psO1)[qb % 2]
                for kt in range(nkt):
                    psT = (psT0, psT1)[kt % 2]
                    ptb = (ptb0, ptb1)[kt % 2]
                    mt = ctx.ev("tensor", lambda e, kc=kc, qst=qst, slot=slot, qb=qb, kt=kt, psT=psT: e.matmul(
                        psT[:], kc[0:65, kt * 128: kt * 128 + 128],
                        qst[0:65, slot + qb * 256: slot + qb * 256 + 256],
                        start=True, stop=True),
                        deps=[nm_toks[hh], (sa, ptfree[kt % 2]), (sv, kcinit)], inc=st)
                    av = ctx.ev("scalar", lambda e, psT=psT, ptb=ptb: e.activation(
                        ptb[:], psT[:], AF.Exp, scale=0.125),
                        deps=[(st, mt), (st, ptbfree[kt % 2])])
                    ptfree[kt % 2] = av
                    lastop = (sa, av)
                    if kt >= 4 * qb:
                        mk = ctx.ev("vector", lambda e, ptb=ptb, kt=kt, qb=qb: e.tensor_mul(
                            ptb[:], ptb[:], mmq[:, (kt - 4 * qb) * 256: (kt - 4 * qb) * 256 + 256]),
                            deps=[(sa, av), mm_t])
                        lastop = (sv, mk)
                    mpv = ctx.ev("tensor", lambda e, vg=vg, kt=kt, ptb=ptb, psO=psO, nkt=nkt: e.matmul(
                        psO[0:65, 0:256], vg[:, kt * 65: kt * 65 + 65], ptb[:],
                        start=(kt == 0), stop=(kt == nkt - 1)),
                        deps=[lastop] + vg_toks + ([(sv, ovfree[qb % 2])] if kt == 0 else []),
                        inc=st)
                    ptbfree[kt % 2] = mpv
                pv_done = st.n
                c1 = ctx.ev("vector", lambda e, psO=psO: e.tensor_copy(lrow[:], psO[64:65, 0:256]),
                            deps=[(st, pv_done), (st, lrfree)])
                c2 = ctx.ev("vector", lambda e: e.reciprocal(lrow[:], lrow[:]))
                mb = ctx.ev("tensor", lambda e: e.matmul(
                    psB[0:64, 0:256], onesf[0:1, 0:64], lrow[:], start=True, stop=True),
                    deps=[(sv, c2), onf_t, (sv, bcfree)], inc=st)
                lrfree = mb
                c3 = ctx.ev("vector", lambda e: e.tensor_copy(invlb[:], psB[0:64, 0:256]),
                            deps=[(st, mb)])
                bcfree = c3
                hv = ctx.ev("vector", lambda e, psO=psO, qst=qst, slot=slot, qb=qb: e.tensor_mul(
                    qst[0:64, slot + qb * 256: slot + qb * 256 + 256],
                    psO[0:64, 0:256], invlb[:]), deps=[(sv, c3)])
                ovfree[qb % 2] = hv
                hdone = hv
            vg_free[hh] = st.n
        kc_free = {0: st.n, 1: st.n}

    yfree = {0: None, 1: None}
    ystore = {0: None, 1: None}
    opfree = {0: None, 1: None}
    for ti in range(16):
        bi = ti % 2
        xres = (xres0, xres1)[bi]
        ybuf = (ybuf0, ybuf1)[bi]
        lx = ctx.dma("sync", lambda e, xres=xres, ti=ti: e.dma_start(
            out=xres[:], in_=xo_d[ti * 128:ti * 128 + 128, :]),
            deps=[(sv, yfree[bi])])
        for hf in range(2):
            ps_y = (psP, psP2)[hf]
            for h in range(16):
                qst = (qstE, qstO)[h % 2]
                ctx.ev("tensor", lambda e, h=h, ti=ti, hf=hf, ps_y=ps_y, qst=qst: e.matmul(
                    ps_y[:], qst[0:64, (h // 2) * TOK + ti * 128: (h // 2) * TOK + ti * 128 + 128],
                    ((wA, wB)[h % 2])[0:64,
                       (h // 2) * DM + hf * 512: (h // 2) * DM + hf * 512 + 512],
                    start=(h == 0), stop=False),
                    deps=(([(sv, hdone), (sv, opfree[hf])] + wot) if h == 0 else []))
            my = ctx.ev("tensor", lambda e, hf=hf, ps_y=ps_y: e.matmul(
                ps_y[:], onesr[:], bor[0:1, hf * 512: hf * 512 + 512],
                start=False, stop=True), deps=[bo_t, on_t], inc=st)
            opfree[hf] = ctx.ev("vector", lambda e, ps_y=ps_y, hf=hf, ybuf=ybuf, xres=xres: e.tensor_add(
                ybuf[:, hf * 512:hf * 512 + 512], ps_y[:], xres[:, hf * 512:hf * 512 + 512]),
                deps=[(st, my), lx, ystore[bi]])
        yv = sv.n
        yfree[bi] = yv
        ystore[bi] = ctx.dma("sync", lambda e, ybuf=ybuf, ti=ti: e.dma_start(
            out=y_d[ti * 128:ti * 128 + 128, :], in_=ybuf[:]),
            deps=[(sv, yv)])


_ROPE_CACHE = {}


def _rope_tables():
    if "t" not in _ROPE_CACHE:
        th = np.repeat(10000.0 ** (-np.arange(0, DM, 2, dtype=np.float64) / DM), 2)
        ang = np.arange(S, dtype=np.float64)[:, None] * th[None, :]
        _ROPE_CACHE["t"] = (np.sin(ang).astype(np.float32), np.cos(ang).astype(np.float32))
    return _ROPE_CACHE["t"]


def run_l1_attn_b(x, ins):
    nc = _get_prog("l1ab", build_l1_attn_b)
    sinb, cosb = _rope_tables()
    bo = np.ascontiguousarray(ins["L1_attn_b_bo"].reshape(1, DM))
    identf = np.eye(P, dtype=np.float32)
    tri = np.tril(np.ones((P, P), np.float32)).T
    onesm = np.ones((P, P), np.float32)
    zerosm = np.zeros((P, P), np.float32)
    in_maps = []
    for c in range(NCORE):
        b, p = c // 2, c % 2
        mm = np.empty((4, P, 256), np.float32)
        for dkt in range(4):
            left = tri if dkt == p else (onesm if dkt < p else zerosm)
            right = tri if dkt == 2 + p else (onesm if dkt < 2 + p else zerosm)
            mm[dkt, :, 0:128] = left
            mm[dkt, :, 128:256] = right
        rows = own_rows(p)
        in_maps.append({
            "x": np.ascontiguousarray(x[b]),
            "xo": np.ascontiguousarray(x[b][rows]),
            "sinb": sinb, "cosb": cosb,
            "sino": np.ascontiguousarray(sinb[rows]),
            "coso": np.ascontiguousarray(cosb[rows]),
            "Wq": ins["L1_attn_b_Wq"], "Wk": ins["L1_attn_b_Wk"],
            "Wv": ins["L1_attn_b_Wv"], "Wo": ins["L1_attn_b_Wo"],
            "bo": bo, "ones": _ONES, "identf": identf,
            "mmq": np.ascontiguousarray(mm.transpose(1, 0, 2).reshape(P, 4 * 256)),
        })
    res = run_bass_kernel_spmd(nc, in_maps, core_ids=list(range(NCORE)))
    y = np.empty((NB, S, DM), np.float32)
    return _scatter(y, [r["y"] for r in res.results])



def emit_l1_ffn(env):
    nc = env.nc
    ctx = env.ctx
    sq, gq = env.pr.sq, env.pr.gq
    sv, sa, st = env.pr.sv, env.pr.sa, env.pr.st
    x_d = env.inputs.get("x")
    xo_d = env.inp("xo")
    wq_d = env.inp("Wq")
    wk_d = env.inp("Wk")
    wv_d = env.inp("Wv")
    wp_d = env.inp("Wp")
    bqk_d = env.inp("bqk")
    bav_d = env.inp("bav")
    bp_d = env.inp("bp")
    wrs_d = env.inp("wrs")
    brs_d = env.inp("brs")
    temp_d = env.inp("temp")
    embc_d = env.inp("embc")
    ohalf_d = env.inp("ohalf")
    ones_d = env.inp("ones")
    identf_d = env.inp("identf")
    y_d = env.inp("y")
    xln_d = env.dram("xln", [S, DM], BF16)
    xlno_d = env.dram("xlno", [TOK, DM], BF16)
    wbf_d = env.dram("wbf", [4, DM, DM], BF16)
    k_d = env.dram("kst", [8, P, S], BF16)
    v_d = env.dram("vst", [S, DM], BF16)
    mrow_d = env.dram("mrowd", [32, P], F32)

    xin0 = env.sb("xin0", [P, DM], F32)
    xin1 = env.sb("xin1", [P, DM], F32)
    xst = env.sb("xst", [P, 12], F32)
    xmv = env.sb("xmv", [P, 2], F32)
    xsd = env.sb("xsd", [P, 2], F32)
    xln0 = env.sb("xln0", [P, DM], BF16)
    xln1 = env.sb("xln1", [P, DM], BF16)
    epsc = env.sb("epsc", [P, 1], F32)
    xfmo = env.sb("xfmo", [P, 8 * TOK], BF16)
    xfb0 = env.sb("xfb0", [P, 8 * 512], BF16)
    wA = env.sb("wA", [P, 8 * DM], BF16)
    wB = env.sb("wB", [P, 8 * DM], BF16)
    bqk = env.sb("bqks", [P, 16], F32)
    bavr = env.sb("bavr", [1, DM], BF16)
    onesr = env.sb("oness", [1, P], BF16)
    onesf = env.sb("onesf", [1, P], F32)
    ohalf = env.sb("ohalfs", [P, 2], F32)
    identf = env.sb("identfs", [P, P], F32)
    qstE = env.sb("qstE", [65, 8 * TOK], BF16)
    qstO = env.sb("qstO", [65, 8 * TOK], BF16)
    kcur0 = env.sb("kcur0", [65, S], BF16)
    kcur1 = env.sb("kcur1", [65, S], BF16)
    vg0 = env.sb("vg0", [P, 32 * 65], BF16)
    vg1 = env.sb("vg1", [P, 32 * 65], BF16)
    m_all = env.sb("m_all", [P, 32], F32)
    mcols = env.sb("mcols", [P, 8], F32)
    mrowt = env.sb("mrowt", [32, P], F32)
    ptb0 = env.sb("ptb0", [P, 512], BF16)
    ptb1 = env.sb("ptb1", [P, 512], BF16)
    ptb2 = env.sb("ptb2", [P, 512], BF16)
    lrow = env.sb("lrow", [1, 1024], F32)
    invlb = env.sb("invlb", [64, 512], F32)
    qsc = env.sb("qsc", [P, 512], F32)
    qsq = env.sb("qsq", [P, 512], F32)
    snr = env.sb("snr", [1, DM], F32)
    spx = env.sb("spx", [1, 64], F32)
    spbc = env.sb("spbc", [P, 16], F32)
    embc = env.sb("embcs", [64, 16], F32)
    wsr = env.sb("wsr", [64, 16 * 17], BF16)
    brs = env.sb("brss", [1, 17], BF16)
    grb = env.sb("grb", [P, 24], F32)
    gcols = env.sb("gcols", [P, 16], F32)
    bpb = env.sb("bpb", [P, DM], BF16)
    xres0 = env.sb("xres0", [P, DM], F32)
    xres1 = env.sb("xres1", [P, DM], F32)
    psA = env.ps("psA", [P, 512], F32)
    psB = env.ps("psB", [P, 512], F32)
    psP = env.ps("psP", [P, 512], F32)
    psT0 = env.ps("psT0", [P, 512], F32)
    psT1 = env.ps("psT1", [P, 512], F32)
    psT2 = env.ps("psT2", [P, 512], F32)
    psO0 = env.ps("psO0", [P, 512], F32)
    psO1 = env.ps("psO1", [P, 512], F32)
    t = dict(xin0=xin0, xin1=xin1, xst=xst, xmv=xmv, xsd=xsd, xln0=xln0, xln1=xln1,
             epsc=epsc)
    pr = env.pr
    pr.t = t
    pr.lnfree = {0: None, 1: None}
    pr.lnsfree = {0: None, 1: None}
    ctx.ev("vector", lambda e: e.memset(epsc[:], 1e-5))
    ctx.ev("vector", lambda e: e.memset(kcur0[64:65, :], 1.0))
    kcinit0 = ctx.ev("vector", lambda e: e.memset(kcur1[64:65, :], 1.0))
    ctx.ev("vector", lambda e: e.memset(vg0[:], 1.0))
    vginit = ctx.ev("vector", lambda e: e.memset(vg1[:], 1.0))

    bqk_t = ctx.dma("sync", lambda e: e.dma_start(out=bqk[:], in_=bqk_d[:]))
    onf_t = ctx.dma("sync", lambda e: e.dma_start(out=onesf[:], in_=ones_d[:]))
    idf_t = ctx.dma("sync", lambda e: e.dma_start(out=identf[:], in_=identf_d[:]))
    emb_t = ctx.dma("sync", lambda e: e.dma_start(out=embc[:], in_=embc_d[:]))
    tmp_t = ctx.dma("sync", lambda e: e.dma_start(out=spx[0:1, 0:16], in_=temp_d[:]))
    bav_t = ctx.dma("gpsimd", lambda e: e.dma_start(out=bavr[:], in_=bav_d[:]))
    on_t = ctx.dma("gpsimd", lambda e: e.dma_start(out=onesr[:], in_=ones_d[:]))
    oh_t = ctx.dma("sync", lambda e: e.dma_start(out=ohalf[:], in_=ohalf_d[:]))
    wsr_t = ctx.dma("gpsimd", lambda e: e.dma_start(
        out=wsr[:].rearrange("p (h m) -> p h m", h=16),
        in_=wrs_d.rearrange("(h p) m -> p h m", p=64)))
    brs_t = ctx.dma("gpsimd", lambda e: e.dma_start(out=brs[:], in_=brs_d[:]))
    wcast = []
    for i, wd in enumerate((wq_d, wk_d, wv_d, wp_d)):
        wcast.append(ctx.dma("gpsimd", lambda e, i=i, wd=wd: e.dma_start(
            out=wbf_d[i], in_=wd[:])))

    s1 = ctx.ev("scalar", lambda e: e.activation(spx[0:1, 16:32], spx[0:1, 0:16], AF.Abs),
                deps=[tmp_t])
    s2 = ctx.ev("scalar", lambda e: e.activation(spx[0:1, 32:48], spx[0:1, 16:32],
                                                 AF.Exp, scale=-1.0))
    s3 = ctx.ev("scalar", lambda e: e.activation(spx[0:1, 32:48], spx[0:1, 32:48],
                                                 AF.Ln, bias=onesf[0:1, 0:1]),
                deps=[onf_t])
    v1 = ctx.ev("vector", lambda e: e.tensor_scalar_max(spx[0:1, 48:64], spx[0:1, 0:16], 0.0),
                deps=[(sa, s3)])
    v2 = ctx.ev("vector", lambda e: e.tensor_add(spx[0:1, 48:64], spx[0:1, 48:64],
                                                 spx[0:1, 32:48]))
    v3 = ctx.ev("vector", lambda e: e.tensor_scalar_mul(spx[0:1, 48:64], spx[0:1, 48:64],
                                                        0.125))
    msp = ctx.ev("tensor", lambda e: e.matmul(psB[:, 0:16], onesf[:], spx[0:1, 48:64],
                                              start=True, stop=True),
                 deps=[(sv, v3), onf_t], inc=st)
    spv = ctx.ev("vector", lambda e: e.tensor_copy(spbc[:], psB[:, 0:16]),
                 deps=[(st, msp)])
    bpld = ctx.dma("sync", lambda e: e.dma_start(out=snr[0:1, :], in_=bp_d[:]))
    bpm0 = ctx.ev("tensor", lambda e: e.matmul(psB[:, 0:512], onesf[:], snr[0:1, 0:512],
                                               start=True, stop=True),
                  deps=[bpld, (sv, spv)], inc=st)
    bpv0 = ctx.ev("vector", lambda e: e.tensor_copy(bpb[:, 0:512], psB[:, 0:512]),
                  deps=[(st, bpm0)])
    bpm1 = ctx.ev("tensor", lambda e: e.matmul(psB[:, 0:512], onesf[:], snr[0:1, 512:1024],
                                               start=True, stop=True),
                  deps=[(sv, bpv0)], inc=st)
    bpv1 = ctx.ev("vector", lambda e: e.tensor_copy(bpb[:, 512:1024], psB[:, 0:512]),
                  deps=[(st, bpm1)])

    if "xfull" in env.inputs:
        xln_d = env.inp("xfull")
        ln_b = []
    else:
        xln_d = env.dram("xlnb", [S, DM], BF16)
        emit_ln(pr, x_d, 0, 32, xln_d)
        ln_b = sq.barrier()
    emit_ln(pr, xo_d, 0, 16, xlno_d)
    ln_o = sq.barrier()
    fm_o = emit_fm_loads(pr, xfmo, xlno_d, TOK, ln_o)

    wqt = emit_wt_loads(pr, wA, wbf_d[0], deps=[wcast[0]])
    qev = None
    psnfree = None
    qn_prev = None
    for m in range(8):
        for ob in range(4):
            mq = proj_accum(pr, psP, wA, DM, m, 128, xfmo, TOK, ob * 512, 512,
                            wqt + [fm_o, (sv, qev)])
            e1 = ctx.ev("vector", lambda e, m=m: e.tensor_scalar(
                qsc[:], psP[:], bqk[:, m:m + 1], None, ALU.add),
                deps=[(st, mq), bqk_t])
            qev = e1
            e2 = ctx.ev("vector", lambda e: e.tensor_mul(qsq[:], qsc[:], qsc[:]))
            ms0 = ctx.ev("tensor", lambda e: e.matmul(
                psT0[0:1, 0:512], ohalf[:, 0:1], qsq[:], start=True, stop=True),
                deps=[(sv, e2), oh_t, (sa, psnfree)], inc=st)
            ms1 = ctx.ev("tensor", lambda e: e.matmul(
                psT0[64:65, 0:512], ohalf[:, 1:2], qsq[:], start=True, stop=True),
                inc=st)
            a0 = ctx.ev("scalar", lambda e: e.activation(snr[0:1, 0:512],
                                                         psT0[0:1, 0:512], AF.Sqrt),
                        deps=[(st, ms0), (sv, bpv1)])
            a1 = ctx.ev("scalar", lambda e: e.activation(snr[0:1, 512:1024],
                                                         psT0[64:65, 0:512], AF.Sqrt),
                        deps=[(st, ms1)])
            psnfree = a1
            r1 = ctx.ev("vector", lambda e: e.reciprocal(snr[:], snr[:]),
                        deps=[(sa, a1)])
            mb0 = ctx.ev("tensor", lambda e: e.matmul(
                psA[0:64, 0:512], onesf[0:1, 0:64], snr[0:1, 0:512],
                start=True, stop=True), deps=[(sv, r1), (sv, qn_prev)], inc=st)
            mb1 = ctx.ev("tensor", lambda e: e.matmul(
                psA[64:128, 0:512], onesf[0:1, 0:64], snr[0:1, 512:1024],
                start=True, stop=True), inc=st)
            qn = ctx.ev("vector", lambda e: e.tensor_mul(qsq[:], qsc[:], psA[:]),
                        deps=[(st, mb1)])
            qn_prev = qn
            for hh in range(2):
                qst = (qstE, qstO)[hh]
                ctx.ev("vector", lambda e, m=m, ob=ob, hh=hh, qst=qst: e.tensor_scalar(
                    qst[0:64, m * TOK + ob * 512: m * TOK + ob * 512 + 512],
                    qsq[hh * 64:hh * 64 + 64, :], embc[:, 2 * m + hh:2 * m + hh + 1],
                    None, ALU.add),
                    deps=[emb_t])
    q_mm_done = st.n

    wkt = emit_wt_loads(pr, wB, wbf_d[1], deps=[wcast[1]])
    ktile = xln1[:, 0:512]
    xfb_free = [None]
    kev = qev
    kst_tok = None
    for bi in range(8):
        fb = xfb0
        fmv = []
        for c in range(8):
            fmv.append(ctx.dma("sync", lambda e, c=c, bi=bi, fb=fb: e.dma_start_transpose(
                fb[:, c * 512:(c + 1) * 512],
                xln_d[bi * 512:(bi + 1) * 512, c * 128:(c + 1) * 128]),
                deps=ln_b + [(st, xfb_free[0])]))
        for m in range(8):
            mk = proj_accum(pr, psP, wB, DM, m, 128, fb, 512, 0, 512,
                            wkt + fmv + [(sv, kev)])
            kev = ctx.ev("vector", lambda e, m=m: e.tensor_scalar(
                ktile, psP[:], bqk[:, 8 + m:8 + m + 1], None, ALU.add),
                deps=[(st, mk), bqk_t, kst_tok] + ln_o)
            kst_tok = ctx.dma("sync", lambda e, m=m, bi=bi: e.dma_start(
                out=k_d[m, :, bi * 512:(bi + 1) * 512], in_=ktile),
                deps=[(sv, kev)])
        xfb_free[0] = st.n
    wvt = emit_wt_loads(pr, wA, wbf_d[2], deps=[wcast[2], (st, q_mm_done)])
    vtile = xln0
    vev = {0: None, 1: None}
    vst_tok = None
    for bi in range(8):
        fb = xfb0
        fmv = []
        for c in range(8):
            fmv.append(ctx.dma("sync", lambda e, c=c, bi=bi, fb=fb: e.dma_start_transpose(
                fb[:, c * 512:(c + 1) * 512],
                xln_d[bi * 512:(bi + 1) * 512, c * 128:(c + 1) * 128]),
                deps=ln_b + [(st, xfb_free[0])]))
        for tt in range(4):
            for hf in range(2):
                ps_v = (psA, psB)[hf]
                for c in range(8):
                    ctx.ev("tensor", lambda e, c=c, tt=tt, hf=hf, ps_v=ps_v, fb=fb: e.matmul(
                        ps_v[:], fb[:, c * 512 + tt * 128: c * 512 + tt * 128 + 128],
                        wA[:, c * DM + hf * 512: c * DM + hf * 512 + 512],
                        start=(c == 0), stop=False),
                        deps=(wvt + fmv + [(sv, vev[hf])] if c == 0 else []))
                mv = ctx.ev("tensor", lambda e, hf=hf, ps_v=ps_v: e.matmul(
                    ps_v[:], onesr[:], bavr[0:1, hf * 512: hf * 512 + 512],
                    start=False, stop=True), deps=[bav_t, on_t], inc=st)
                vev[hf] = ctx.ev("vector", lambda e, hf=hf, ps_v=ps_v: e.tensor_copy(
                    vtile[:, hf * 512:hf * 512 + 512], ps_v[:]),
                    deps=[(st, mv), vst_tok] + ln_o)
            vst_tok = ctx.dma("sync", lambda e, bi=bi, tt=tt: e.dma_start(
                out=v_d[bi * 512 + tt * 128: bi * 512 + tt * 128 + 128, :], in_=vtile[:]),
                deps=[(sv, vev[0]), (sv, vev[1])])
        xfb_free[0] = st.n
    kv_store = sq.barrier()
    proj_done = st.n
    wot = emit_wt_loads(pr, wA, wbf_d[3], deps=[wcast[3], (st, proj_done)])
    wodd = None
    for j in range(8):
        wodd = ctx.ev("vector", lambda e, j=j: e.tensor_copy(
            wB[0:64, j * DM:(j + 1) * DM], wA[64:128, j * DM:(j + 1) * DM]),
            deps=[wot[j], (st, proj_done)])
    wot = wot + [(sv, wodd)]

    kcur2 = xfmo[0:65, 0:S]
    kcur3 = xfmo[0:65, S:2 * S]
    ctx.ev("vector", lambda e: e.memset(kcur2[64:65, :], 1.0), deps=[(st, st.n)])
    kcinit = ctx.ev("vector", lambda e: e.memset(kcur3[64:65, :], 1.0))
    kc_bufs = [[kcur0, kcur1], [kcur2, kcur3]]
    stt = {
        "kc_free": {0: None, 1: None},
        "vg_free": {0: None, 1: None},
        "ptfree": {0: None, 1: None, 2: None},
        "ptbfree": {0: None, 1: None, 2: None},
        "ovfree": {0: None, 1: None},
        "bcfree": None, "lrfree": None, "tr_prev": None,
        "nm_prev": [], "pafree": None, "hdone": None,
        "kc_toks": {}, "nm_toks": {}, "av": {0: None, 1: None, 2: None},
    }
    psArot = [psA, psP]

    def passA_steps(h2):
        steps = []
        par = h2 % 2
        kct = []
        for hh in range(2):
            kc = kc_bufs[par][hh]
            def ld(hh=hh, kc=kc, h2=h2):
                stt["kc_toks"][(h2, hh)] = ctx.dma(
                    "sync", lambda e, h2=h2, hh=hh, kc=kc: e.dma_start(
                        out=kc[0:64, :], in_=k_d[h2, hh * 64:hh * 64 + 64, :]),
                    deps=kv_store + [(st, stt["kc_free"][par])])
            steps.append(ld)
        cnt = [0]
        for hh in range(2):
            qst = (qstE, qstO)[hh]
            slot = h2 * TOK
            for i in range(16):
                for b in range(8):
                    kc = kc_bufs[par][hh]
                    def stp(hh=hh, i=i, b=b, kc=kc, qst=qst, slot=slot, h2=h2):
                        ps = psArot[cnt[0] % len(psArot)]
                        cnt[0] += 1
                        ma = ctx.ev("tensor", lambda e, kc=kc, qst=qst, slot=slot, i=i, b=b, ps=ps: e.matmul(
                            ps[:], qst[0:64, slot + i * 128: slot + i * 128 + 128],
                            kc[0:64, b * 512: b * 512 + 512], start=True, stop=True),
                            deps=[stt["kc_toks"][(h2, hh)], stt["pafree"],
                                  (st, stt["tr_prev"])], inc=st)
                        pv_ = ctx.ev("vector", lambda e, b=b, ps=ps: e.reduce_max(
                            mcols[:, b:b + 1].unsqueeze(2),
                            ps[:].rearrange("p (a b) -> p a b", a=1), axis=AX.X),
                            deps=[(st, ma), (st, stt["tr_prev"])])
                        stt["pafree"] = (sv, pv_)
                        if b == 7:
                            ctx.ev("vector", lambda e, hh=hh, i=i: e.reduce_max(
                                m_all[:, hh * 16 + i:hh * 16 + i + 1].unsqueeze(2),
                                mcols[:].rearrange("p (a b) -> p a b", a=1), axis=AX.X))
                    steps.append(stp)
        return steps

    def emit_negm(h2):
        mav = sv.n
        tr = ctx.ev("tensor", lambda e: e.transpose(psB[0:32, 0:128], m_all[:], identf[:]),
                    deps=[(sv, mav), idf_t, (sv, stt["bcfree"])], inc=st)
        stt["tr_prev"] = tr
        mr = ctx.ev("vector", lambda e: e.tensor_scalar(
            mrowt[:], psB[0:32, 0:128], -1.0, None, ALU.mult),
            deps=[(st, tr)] + stt["nm_prev"])
        mr_st = ctx.dma("sync", lambda e: e.dma_start(out=mrow_d[:], in_=mrowt[:]),
                        deps=[(sv, mr)])
        toks = []
        for hh in range(2):
            qst = (qstE, qstO)[hh]
            toks.append(ctx.dma("gpsimd", lambda e, hh=hh, qst=qst, h2=h2: e.dma_start(
                out=qst[64:65, h2 * TOK: h2 * TOK + TOK],
                in_=mrow_d[hh * 16:hh * 16 + 16, :].rearrange("a b -> (a b)").unsqueeze(0)),
                deps=[mr_st]))
        stt["nm_toks"][h2] = toks
        stt["nm_prev"] = list(toks)

    def passB_steps(h2):
        steps = []
        par = h2 % 2
        for hh in range(2):
            kc = kc_bufs[par][hh]
            qst = (qstE, qstO)[hh]
            h = 2 * h2 + hh
            slot = h2 * TOK
            vg = (vg0, vg1)[hh]
            vgt = {}
            for qq in range(4):
                def vld(qq=qq, h=h, vg=vg, hh=hh, vgt=vgt):
                    tok = ctx.dma("sync", lambda e, qq=qq, h=h, vg=vg: e.dma_start(
                        out=vg[:].rearrange("p (n d) -> p n d", n=32)[:, qq * 8:(qq + 1) * 8, 0:64],
                        in_=v_d[qq * 1024:(qq + 1) * 1024, h * 64:(h + 1) * 64]
                            .rearrange("(n p) d -> p n d", p=128)),
                        deps=kv_store + [(st, stt["vg_free"][hh]), (sv, vginit)])
                    vgt[qq] = tok
                steps.append(vld)
            for qb in range(4):
                psO = (psO0, psO1)[qb % 2]
                LAG = 2
                def mk_mt(kt, kc=kc, qst=qst, slot=slot, h=h, hh=hh, h2=h2, qb=qb):
                    psT = (psT0, psT1, psT2)[kt % 3]
                    ptb = (ptb0, ptb1, ptb2)[kt % 3]
                    mt = ctx.ev("tensor", lambda e, kc=kc, qst=qst, slot=slot, kt=kt, psT=psT, qb=qb: e.matmul(
                        psT[:], kc[0:65, kt * 128: kt * 128 + 128],
                        qst[0:65, slot + qb * 512: slot + qb * 512 + 512],
                        start=True, stop=True),
                        deps=[stt["nm_toks"][h2][hh], (sa, stt["ptfree"][kt % 3]),
                              (sv, kcinit)], inc=st)
                    av = ctx.ev("scalar", lambda e, psT=psT, ptb=ptb, h=h: e.activation(
                        ptb[:], psT[:], AF.Exp, scale=spbc[:, h:h + 1]),
                        deps=[(st, mt), (st, stt["ptbfree"][kt % 3]), (sv, spv)])
                    stt["ptfree"][kt % 3] = av
                    stt["av"][kt % 3] = av
                def mk_pv(kt, vg=vg, vgt=vgt, psO=psO, qb=qb):
                    ptb = (ptb0, ptb1, ptb2)[kt % 3]
                    mpv = ctx.ev("tensor", lambda e, vg=vg, kt=kt, ptb=ptb, psO=psO: e.matmul(
                        psO[0:65, :], vg[:, kt * 65: kt * 65 + 65], ptb[:],
                        start=(kt == 0), stop=(kt == 31)),
                        deps=[(sa, stt["av"][kt % 3])] + list(vgt.values()) +
                             ([(sv, stt["ovfree"][qb % 2])] if kt == 0 else []),
                        inc=st)
                    stt["ptbfree"][kt % 3] = mpv
                for kt in range(32):
                    def stp(kt=kt, mk_mt=mk_mt, mk_pv=mk_pv):
                        mk_mt(kt)
                        if kt >= LAG:
                            mk_pv(kt - LAG)
                    steps.append(stp)
                def tail(mk_pv=mk_pv):
                    mk_pv(30)
                    mk_pv(31)
                steps.append(tail)
                def fin(qb=qb, psO=psO, qst=qst, slot=slot):
                    pv_done = st.n
                    c1 = ctx.ev("vector", lambda e, psO=psO: e.tensor_copy(
                        lrow[0:1, 0:512], psO[64:65, :]),
                        deps=[(st, pv_done), (st, stt["lrfree"])])
                    c2 = ctx.ev("vector", lambda e: e.reciprocal(
                        lrow[0:1, 0:512], lrow[0:1, 0:512]))
                    mb = ctx.ev("tensor", lambda e: e.matmul(
                        psB[0:64, 0:512], onesf[0:1, 0:64], lrow[0:1, 0:512],
                        start=True, stop=True),
                        deps=[(sv, c2), onf_t, (sv, stt["bcfree"])], inc=st)
                    stt["lrfree"] = mb
                    c3 = ctx.ev("vector", lambda e: e.tensor_copy(invlb[:], psB[0:64, 0:512]),
                                deps=[(st, mb)])
                    stt["bcfree"] = c3
                    hv = ctx.ev("vector", lambda e, psO=psO, qst=qst, slot=slot, qb=qb: e.tensor_mul(
                        qst[0:64, slot + qb * 512: slot + qb * 512 + 512],
                        psO[0:64, :], invlb[:]), deps=[(sv, c3)])
                    stt["ovfree"][qb % 2] = hv
                    stt["hdone"] = hv
                steps.append(fin)
            def endh(hh=hh):
                stt["vg_free"][hh] = st.n
            steps.append(endh)
        def endp(par=par):
            stt["kc_free"][par] = st.n
        steps.append(endp)
        return steps

    def zip_emit(a, b):
        na, nb = len(a), len(b)
        ia = ib = 0
        while ia < na or ib < nb:
            if ib < nb and (ia >= na or ib * na <= ia * nb):
                b[ib]()
                ib += 1
            else:
                a[ia]()
                ia += 1

    prevB = []
    for h2 in range(8):
        stepsA = passA_steps(h2)
        zip_emit(stepsA, prevB)
        emit_negm(h2)
        prevB = passB_steps(h2)
    for f in prevB:
        f()
    hdone = stt["hdone"]

    grfree = None
    for ti in range(16):
        for h in range(16):
            qst = (qstE, qstO)[h % 2]
            ctx.ev("tensor", lambda e, h=h, ti=ti, qst=qst: e.matmul(
                psT0[:, 0:17], qst[0:64, (h // 2) * TOK + ti * 128: (h // 2) * TOK + ti * 128 + 128],
                wsr[:, h * 17:(h + 1) * 17], start=(h == 0), stop=False),
                deps=(([(sv, hdone), wsr_t, (sv, grfree)]) if h == 0 else []))
        mgr = ctx.ev("tensor", lambda e: e.matmul(
            psT0[:, 0:17], onesr[:], brs[:], start=False, stop=True),
            deps=[brs_t], inc=st)
        g1 = ctx.ev("vector", lambda e: e.tensor_copy(grb[:, 0:17], psT0[:, 0:17]),
                    deps=[(st, mgr)])
        grfree = g1
        ctx.ev("vector", lambda e: e.max(out=grb[:, 17:25] if False else mcols[:], in_=grb[:, 0:15]))
        ctx.ev("vector", lambda e: e.tensor_scalar_mul(xsd[:, 0:1], mcols[:, 0:1], -1.0))
        aex = ctx.ev("scalar", lambda e: e.activation(
            grb[:, 0:15], grb[:, 0:15], AF.Exp, bias=xsd[:, 0:1],
            accum_out=xsd[:, 1:2]), deps=[(sv, sv.n)])
        ae3 = ctx.ev("scalar", lambda e: e.activation(
            mcols[:, 0:3], mcols[:, 0:3], AF.Exp, bias=xsd[:, 0:1]))
        v3_ = ctx.ev("vector", lambda e: e.reduce_sum(
            mcols[:, 3:4].unsqueeze(2), mcols[:, 0:3].rearrange("p (a b) -> p a b", a=1),
            axis=AX.X), deps=[(sa, ae3)])
        vm = ctx.ev("vector", lambda e: e.tensor_tensor(
            mcols[:, 4:5], grb[:, 15:16], grb[:, 16:17], ALU.max))
        vn = ctx.ev("vector", lambda e: e.tensor_scalar_mul(mcols[:, 4:5], mcols[:, 4:5], -1.0))
        ae2 = ctx.ev("scalar", lambda e: e.activation(
            grb[:, 15:17], grb[:, 15:17], AF.Exp, bias=mcols[:, 4:5],
            accum_out=mcols[:, 5:6]), deps=[(sv, vn)])
        r1_ = ctx.ev("vector", lambda e: e.reciprocal(mcols[:, 6:7], xsd[:, 1:2]),
                     deps=[(sa, aex)])
        r2_ = ctx.ev("vector", lambda e: e.reciprocal(mcols[:, 7:8], mcols[:, 5:6]),
                     deps=[(sa, ae2)])
        t1_ = ctx.ev("vector", lambda e: e.tensor_mul(mcols[:, 3:4], mcols[:, 3:4], mcols[:, 6:7]))
        t2_ = ctx.ev("vector", lambda e: e.tensor_mul(mcols[:, 3:4], mcols[:, 3:4], grb[:, 16:17]))
        t3_ = ctx.ev("vector", lambda e: e.scalar_tensor_tensor(
            mcols[:, 3:4], mcols[:, 3:4], 3.0, grb[:, 15:16], ALU.mult, ALU.add))
        t4_ = ctx.ev("vector", lambda e: e.tensor_scalar_mul(mcols[:, 3:4], mcols[:, 3:4], 2.0))
        ctx.ev("vector", lambda e, ti=ti: e.tensor_mul(
            gcols[:, ti:ti + 1], mcols[:, 3:4], mcols[:, 7:8]))
    g_done = sv.n

    yfree = {0: None, 1: None}
    ystore = {0: None, 1: None}
    opfree = {0: None, 1: None}
    for ti in range(16):
        bi = ti % 2
        xres = (xres0, xres1)[bi]
        lx = ctx.dma("sync", lambda e, xres=xres, ti=ti: e.dma_start(
            out=xres[:], in_=xo_d[ti * 128:ti * 128 + 128, :]),
            deps=[(sv, yfree[bi]), ystore[bi]])
        for hf in range(2):
            ps_y = (psP, psO1)[hf]
            for h in range(16):
                qst = (qstE, qstO)[h % 2]
                ctx.ev("tensor", lambda e, h=h, ti=ti, hf=hf, ps_y=ps_y, qst=qst: e.matmul(
                    ps_y[:], qst[0:64, (h // 2) * TOK + ti * 128: (h // 2) * TOK + ti * 128 + 128],
                    ((wA, wB)[h % 2])[0:64,
                       (h // 2) * DM + hf * 512: (h // 2) * DM + hf * 512 + 512],
                    start=(h == 0), stop=(h == 15)),
                    deps=(([(sv, g_done), (sv, opfree[hf])] + wot) if h == 0 else []),
                    inc=(st if h == 15 else None))
            my = st.n
            e1 = ctx.ev("vector", lambda e, ps_y=ps_y, hf=hf, ti=ti: e.scalar_tensor_tensor(
                qsc[:], ps_y[:], gcols[:, ti:ti + 1],
                bpb[:, hf * 512:hf * 512 + 512], ALU.mult, ALU.add),
                deps=[(st, my), ystore[bi], (sv, bpv1)])
            opfree[hf] = ctx.ev("vector", lambda e, hf=hf, xres=xres: e.tensor_add(
                xres[:, hf * 512:hf * 512 + 512], xres[:, hf * 512:hf * 512 + 512],
                qsc[:]),
                deps=[lx])
        yv = sv.n
        yfree[bi] = yv
        ystore[bi] = ctx.dma("sync", lambda e, xres=xres, ti=ti: e.dma_start(
            out=y_d[ti * 128:ti * 128 + 128, :], in_=xres[:]),
            deps=[(sv, yv)])


def run_l1_ffn(x, ins):
    nc = _get_prog("l1f", build_l1_ffn)
    bqk = np.ascontiguousarray(np.concatenate(
        [_pc(ins["L1_ffn_bq"]), _pc(ins["L1_ffn_bk"])], axis=1))
    bav = np.ascontiguousarray(ins["L1_ffn_bv"].reshape(1, DM))
    bp = np.ascontiguousarray(ins["L1_ffn_bp"].reshape(1, DM))
    wrs = np.ascontiguousarray(
        np.concatenate([ins["L1_ffn_Wr"].T, ins["L1_ffn_Ws"].T], axis=1))
    brs = np.ascontiguousarray(
        np.concatenate([ins["L1_ffn_br"], ins["L1_ffn_bs"]]).reshape(1, 17))
    temp = np.ascontiguousarray(ins["L1_ffn_temperature"].reshape(1, 16))
    embc = np.ascontiguousarray(ins["L1_ffn_query_embedding"].reshape(16, 64).T)
    ohalf = np.zeros((P, 2), np.float32)
    ohalf[0:64, 0] = 1.0
    ohalf[64:128, 1] = 1.0
    identf = np.eye(P, dtype=np.float32)
    in_maps = []
    for c in range(NCORE):
        b, p = c // 2, c % 2
        in_maps.append({
            "x": np.ascontiguousarray(x[b]),
            "xo": np.ascontiguousarray(x[b][own_rows(p)]),
            "Wq": ins["L1_ffn_Wq"], "Wk": ins["L1_ffn_Wk"],
            "Wv": ins["L1_ffn_Wv"], "Wp": ins["L1_ffn_Wp"],
            "bqk": bqk, "bav": bav, "bp": bp,
            "wrs": wrs, "brs": brs, "temp": temp, "embc": embc,
            "ohalf": ohalf, "ones": _ONES, "identf": identf,
        })
    res = run_bass_kernel_spmd(nc, in_maps, core_ids=list(range(NCORE)))
    y = np.empty((NB, S, DM), np.float32)
    return _scatter(y, [r["y"] for r in res.results])



def emit_l0_attn_b(env):
    nc = env.nc
    ctx = env.ctx
    sq, gq = env.pr.sq, env.pr.gq
    sv, sa, st = env.pr.sv, env.pr.sa, env.pr.st
    xo_d = env.inp("xo")
    wq_d = env.inp("Wq")
    wk_d = env.inp("Wk")
    wv_d = env.inp("Wv")
    wo_d = env.inp("Wo")
    bo_d = env.inp("bo")
    ones_d = env.inp("ones")
    y_d = env.inp("y")
    xlno_d = env.dram("xlno", [TOK, DM], BF16)
    w3_d = env.dram("w3bf", [3, DM, 4 * DM], BF16)
    wob_d = env.dram("wobf", [4 * DM, DM], BF16)

    TB = 512
    xin0 = env.sb("xin0", [P, DM], F32)
    xin1 = env.sb("xin1", [P, DM], F32)
    xst = env.sb("xst", [P, 12], F32)
    xmv = env.sb("xmv", [P, 2], F32)
    xsd = env.sb("xsd", [P, 2], F32)
    xln0 = env.sb("xln0", [P, DM], BF16)
    xln1 = env.sb("xln1", [P, DM], BF16)
    epsc = env.sb("epsc", [P, 1], F32)
    ftb = env.sb("ftb", [P, 8 * TB], BF16)
    wt0 = env.sb("wt0", [P, 8 * 128], BF16)
    wt1 = env.sb("wt1", [P, 8 * 128], BF16)
    wo0 = env.sb("wo0", [P, DM], BF16)
    wo1 = env.sb("wo1", [P, DM], BF16)
    qsl = env.sb("qsl", [64, 64 * TB], BF16)
    wsl = env.sb("wsl", [64, 65 * TB], BF16)
    z2 = env.sb("z2", [P, 32 * TB], BF16)
    ksum = env.sb("ksum", [64, TB], F32)
    scr4k = env.sb("scr4k", [P, 2 * TB], F32)
    ktmp = scr4k[0:64, :]
    dsc = env.sb("dsc", [64, 32], F32)
    bos = env.sb("bos", [1, DM], BF16)
    onesr = env.sb("oness", [1, P], BF16)
    xres0 = env.sb("xres0", [P, DM], F32)
    xres1 = env.sb("xres1", [P, DM], F32)
    ysc = scr4k[:, 0:512]
    psA = env.ps("psA", [P, 512], F32)
    psB = env.ps("psB", [P, 512], F32)
    psP = env.ps("psP", [P, 512], F32)
    psP2 = env.ps("psP2", [P, 512], F32)
    psG = [env.ps("psG%d" % i, [P, 512], F32) for i in range(4)]
    t = dict(xin0=xin0, xin1=xin1, xst=xst, xmv=xmv, xsd=xsd, xln0=xln0, xln1=xln1,
             epsc=epsc)
    pr = env.pr
    pr.t = t
    pr.lnfree = {0: None, 1: None}
    pr.lnsfree = {0: None, 1: None}
    ctx.ev("vector", lambda e: e.memset(epsc[:], 1e-5))

    bo_t = ctx.dma("gpsimd", lambda e: e.dma_start(out=bos[:], in_=bo_d[:]))
    on_t = ctx.dma("gpsimd", lambda e: e.dma_start(out=onesr[:], in_=ones_d[:]))
    wcast = []
    for i, wd in enumerate((wq_d, wk_d, wv_d)):
        wcast.append(ctx.dma("gpsimd", lambda e, i=i, wd=wd: e.dma_start(
            out=w3_d[i], in_=wd[:])))
    wocast = ctx.dma("gpsimd", lambda e: e.dma_start(out=wob_d[:], in_=wo_d[:]))

    emit_ln(pr, xo_d, 0, 16, xlno_d)
    ln_o = sq.barrier()

    groups = []
    t0 = 0
    while t0 < TB:
        gn = min(7, TB - t0)
        groups.append((t0, gn))
        t0 += gn

    ftb_free = [None]
    slot_free = [None]
    z2_free = [None]
    wt_free = {0: None, 1: None}
    wo_free = {0: None, 1: None}
    psfree = {0: None, 1: None}
    psyfree = {i: None for i in range(8)}
    yfree = {0: None, 1: None}
    ystore = {0: None, 1: None}
    for tb in range(4):
        fmv = []
        for c in range(8):
            fmv.append(ctx.dma("sync", lambda e, c=c, tb=tb: e.dma_start_transpose(
                ftb[:, c * TB:(c + 1) * TB],
                xlno_d[tb * TB:(tb + 1) * TB, c * 128:(c + 1) * 128]),
                deps=ln_o + [(st, ftb_free[0])]))
        kacc = None
        for m in range(32):
            wt = (wt0, wt1)[m % 2]
            wld = ctx.dma("gpsimd", lambda e, m=m, wt=wt: e.dma_start(
                out=wt[:].rearrange("p (c m2) -> p c m2", c=8),
                in_=w3_d[1].rearrange("(c p) m2 -> p c m2", p=128)[:, :, m * 128:(m + 1) * 128]),
                deps=[wcast[1], (st, wt_free[m % 2])])
            ps = (psA, psB)[m % 2]
            for c in range(8):
                ctx.ev("tensor", lambda e, c=c, m=m, ps=ps, wt=wt: e.matmul(
                    ps[:], wt[:, c * 128:(c + 1) * 128],
                    ftb[:, c * TB:(c + 1) * TB],
                    start=(c == 0), stop=(c == 7)),
                    deps=([wld] + fmv + [psfree[m % 2]] if c == 0 else []),
                    inc=(st if c == 7 else None))
            wt_free[m % 2] = st.n
            mk = st.n
            ctx.ev("vector", lambda e, ps=ps: e.tensor_scalar_max(
                ktmp[:, 0:TB], ps[0:64, :], 0.0), deps=[(st, mk)])
            kr = ctx.ev("vector", lambda e, ps=ps: e.tensor_scalar_max(
                ktmp[:, TB:2 * TB], ps[64:128, :], 0.0))
            psfree[m % 2] = (sv, kr)
            if m == 0:
                kacc = ctx.ev("vector", lambda e: e.tensor_add(
                    ksum[:], ktmp[:, 0:TB], ktmp[:, TB:2 * TB]))
            else:
                ctx.ev("vector", lambda e: e.tensor_add(ksum[:], ksum[:], ktmp[:, 0:TB]))
                kacc = ctx.ev("vector", lambda e: e.tensor_add(
                    ksum[:], ksum[:], ktmp[:, TB:2 * TB]))
        ks_v = ctx.ev("vector", lambda e: e.tensor_copy(
            wsl[0:64, 64 * TB:65 * TB], ksum[:]), deps=[(st, slot_free[0])])
        for m in range(32):
            wt = (wt0, wt1)[m % 2]
            wld = ctx.dma("gpsimd", lambda e, m=m, wt=wt: e.dma_start(
                out=wt[:].rearrange("p (c m2) -> p c m2", c=8),
                in_=w3_d[0].rearrange("(c p) m2 -> p c m2", p=128)[:, :, m * 128:(m + 1) * 128]),
                deps=[wcast[0], (st, wt_free[m % 2])])
            ps = (psA, psB)[m % 2]
            for c in range(8):
                ctx.ev("tensor", lambda e, c=c, m=m, ps=ps, wt=wt: e.matmul(
                    ps[:], wt[:, c * 128:(c + 1) * 128],
                    ftb[:, c * TB:(c + 1) * TB],
                    start=(c == 0), stop=(c == 7)),
                    deps=([wld] + fmv + [psfree[m % 2], (st, slot_free[0])] if c == 0 else []),
                    inc=(st if c == 7 else None))
            wt_free[m % 2] = st.n
            mk = st.n
            a0 = ctx.ev("scalar", lambda e, ps=ps, m=m: e.activation(
                qsl[0:64, (2 * m) * TB:(2 * m) * TB + TB], ps[0:64, :],
                AF.Relu, scale=0.125), deps=[(st, mk)])
            a1 = ctx.ev("scalar", lambda e, ps=ps, m=m: e.activation(
                qsl[0:64, (2 * m + 1) * TB:(2 * m + 1) * TB + TB], ps[64:128, :],
                AF.Relu, scale=0.125))
            psfree[m % 2] = (sa, a1)
        for m in range(32):
            wt = (wt0, wt1)[m % 2]
            wld = ctx.dma("gpsimd", lambda e, m=m, wt=wt: e.dma_start(
                out=wt[:].rearrange("p (c m2) -> p c m2", c=8),
                in_=w3_d[2].rearrange("(c p) m2 -> p c m2", p=128)[:, :, m * 128:(m + 1) * 128]),
                deps=[wcast[2], (st, wt_free[m % 2])])
            ps = (psA, psB)[m % 2]
            for c in range(8):
                ctx.ev("tensor", lambda e, c=c, m=m, ps=ps, wt=wt: e.matmul(
                    ps[:], wt[:, c * 128:(c + 1) * 128],
                    ftb[:, c * TB:(c + 1) * TB],
                    start=(c == 0), stop=(c == 7)),
                    deps=([wld] + fmv + [psfree[m % 2]] if c == 0 else []),
                    inc=(st if c == 7 else None))
            wt_free[m % 2] = st.n
            mk = st.n
            v0 = ctx.ev("vector", lambda e, ps=ps, m=m: e.tensor_mul(
                wsl[0:64, (2 * m) * TB:(2 * m) * TB + TB], ps[0:64, :], ksum[:]),
                deps=[(st, mk), (sv, kacc), (sv, ks_v)])
            v1 = ctx.ev("vector", lambda e, ps=ps, m=m: e.tensor_mul(
                wsl[0:64, (2 * m + 1) * TB:(2 * m + 1) * TB + TB], ps[64:128, :], ksum[:]))
            psfree[m % 2] = (sv, v1)
        slots_v = sv.n
        slots_a = sa.n
        gfree = {0: None, 1: None, 2: None, 3: None}
        for gi, (g0, gn) in enumerate(groups):
            bank = psG[gi % 4]
            for j7 in range(gn):
                tok = g0 + j7
                ctx.ev("tensor", lambda e, tok=tok, j7=j7, bank=bank: e.matmul(
                    bank[0:64, j7 * 65:(j7 + 1) * 65],
                    qsl[0:64, :].rearrange("p (s t) -> p s t", s=64)[:, :, tok],
                    wsl[0:64, :].rearrange("p (s t) -> p s t", s=65)[:, :, tok],
                    start=True, stop=True),
                    deps=([(sv, slots_v), (sa, slots_a), (sv, gfree[gi % 4]),
                           (st, z2_free[0])] if j7 == 0 else []),
                    inc=(st if j7 == gn - 1 else None))
            mg = st.n
            d1 = ctx.ev("vector", lambda e, bank=bank, gn=gn: e.tensor_scalar(
                dsc[0:64, 0:gn],
                bank[0:64, 0:gn * 65].rearrange("p (t j) -> p t j", j=65)[:, :, 64],
                1e-6, None, ALU.add), deps=[(st, mg)])
            d2 = ctx.ev("vector", lambda e, gn=gn: e.reciprocal(dsc[0:64, 0:gn], dsc[0:64, 0:gn]))
            for par in range(2):
                ctx.ev("vector", lambda e, bank=bank, gn=gn, g0=g0, tb=tb, par=par: e.tensor_tensor(
                    z2[par * 64:par * 64 + 64, :]
                      .rearrange("p (jp t) -> p jp t", jp=32)[:, :, g0:g0 + gn],
                    bank[0:64, 0:gn * 65].rearrange("p (t j) -> p t j", j=65)
                        .transpose([0, 2, 1])[:, par:64 + par:2, :],
                    dsc[0:64, 0:gn].unsqueeze(1).to_broadcast([64, 32, gn]),
                    ALU.mult))
            gfree[gi % 4] = sv.n
        evac_done = sv.n
        slot_free[0] = st.n
        banks = [psP, psP2, psG[0], psG[1], psG[2], psG[3], psA, psB]
        for jp in range(32):
            wox = (wo0, wo1)[jp % 2]
            wl0 = ctx.dma("sync", lambda e, jp=jp, wox=wox: e.dma_start(
                out=wox[0:64, :],
                in_=wob_d.rearrange("(i j) m2 -> j i m2", j=64)[2 * jp]),
                deps=[wocast, (st, wo_free[jp % 2])])
            wl1 = ctx.dma("gpsimd", lambda e, jp=jp, wox=wox: e.dma_start(
                out=wox[64:128, :],
                in_=wob_d.rearrange("(i j) m2 -> j i m2", j=64)[2 * jp + 1]),
                deps=[wocast, (st, wo_free[jp % 2])])
            for tt in range(4):
                for hf in range(2):
                    ps_y = banks[tt * 2 + hf]
                    last = (tt == 3 and hf == 1)
                    ctx.ev("tensor", lambda e, jp=jp, tt=tt, hf=hf, ps_y=ps_y, wox=wox: e.matmul(
                        ps_y[:], z2[:, jp * TB + tt * 128: jp * TB + tt * 128 + 128],
                        wox[:, hf * 512: hf * 512 + 512],
                        start=(jp == 0), stop=False),
                        deps=([wl0, wl1, (sv, evac_done)]
                              + ([psyfree[tt * 2 + hf]] if jp == 0 else [])),
                        inc=(st if last else None))
            wo_free[jp % 2] = st.n
        for tt in range(4):
            ti = tb * 4 + tt
            bi = ti % 2
            xres = (xres0, xres1)[bi]
            lx = ctx.dma("sync", lambda e, xres=xres, ti=ti: e.dma_start(
                out=xres[:], in_=xo_d[ti * 128:ti * 128 + 128, :]),
                deps=[(sv, yfree[bi]), ystore[bi]])
            for hf in range(2):
                ps_y = banks[tt * 2 + hf]
                my = ctx.ev("tensor", lambda e, hf=hf, ps_y=ps_y: e.matmul(
                    ps_y[:], onesr[:], bos[0:1, hf * 512: hf * 512 + 512],
                    start=False, stop=True), deps=[bo_t, on_t], inc=st)
                e1 = ctx.ev("vector", lambda e, ps_y=ps_y: e.tensor_copy(ysc[:], ps_y[:]),
                            deps=[(st, my)])
                psyfree[tt * 2 + hf] = (sv, e1)
                ctx.ev("vector", lambda e, hf=hf, xres=xres: e.tensor_add(
                    xres[:, hf * 512:hf * 512 + 512], xres[:, hf * 512:hf * 512 + 512],
                    ysc[:]), deps=[lx])
            yv = sv.n
            yfree[bi] = yv
            ystore[bi] = ctx.dma("sync", lambda e, xres=xres, ti=ti: e.dma_start(
                out=y_d[ti * 128:ti * 128 + 128, :], in_=xres[:]),
                deps=[(sv, yv)])
        z2_free[0] = st.n
        ftb_free[0] = st.n
        psfree[0] = psyfree[6]
        psfree[1] = psyfree[7]


def run_l0_attn_b(x, ins):
    nc = _get_prog("l0ab", build_l0_attn_b)
    bo = np.ascontiguousarray(ins["L0_attn_b_bo"].reshape(1, DM))
    in_maps = []
    for c in range(NCORE):
        b, p = c // 2, c % 2
        in_maps.append({
            "xo": np.ascontiguousarray(x[b][own_rows(p)]),
            "Wq": ins["L0_attn_b_Wq"], "Wk": ins["L0_attn_b_Wk"],
            "Wv": ins["L0_attn_b_Wv"], "Wo": ins["L0_attn_b_Wo"],
            "bo": bo, "ones": _ONES,
        })
    res = run_bass_kernel_spmd(nc, in_maps, core_ids=list(range(NCORE)))
    y = np.empty((NB, S, DM), np.float32)
    return _scatter(y, [r["y"] for r in res.results])



_SPECS = {
    "l0aa": [("x", [S, DM], F32, "ExternalInput"), ("xo", [TOK, DM], F32, "ExternalInput"),
             ("Wq", [DM, DM], F32, "ExternalInput"), ("Wk", [DM, DM], F32, "ExternalInput"),
             ("Wv", [DM, DM], F32, "ExternalInput"), ("Wo", [DM, DM], F32, "ExternalInput"),
             ("bq", [P, 8], F32, "ExternalInput"), ("bk", [P, 8], F32, "ExternalInput"),
             ("bv", [P, 8], F32, "ExternalInput"), ("bo", [DM], F32, "ExternalInput"),
             ("ones", [1, P], F32, "ExternalInput"), ("y", [TOK, DM], F32, "ExternalOutput")],
    "l0cnt": [("xo", [TOK, DM], F32, "ExternalInput"), ("gw", [8, DM], F32, "ExternalInput"),
              ("gb", [8], F32, "ExternalInput"), ("ones", [1, P], F32, "ExternalInput"),
              ("ident", [P, P], F32, "ExternalInput"), ("cnt", [8, 1], F32, "ExternalOutput")],
    "l0ffn": [("xo", [TOK, DM], F32, "ExternalInput"), ("parts", [8, 8], F32, "ExternalInput"),
              ("we", [8, DM, DM], F32, "ExternalInput"), ("web", [1, 8 * DM], F32, "ExternalInput"),
              ("ones", [1, P], F32, "ExternalInput"), ("onesc8", [8, 1], F32, "ExternalInput"),
              ("y", [TOK, DM], F32, "ExternalOutput")],
    "l1aa": [("x", [S, DM], F32, "ExternalInput"), ("xo", [TOK, DM], F32, "ExternalInput"),
             ("Wa", [DM, 3 * DM], F32, "ExternalInput"), ("bqk", [P, 16], F32, "ExternalInput"),
             ("bav", [1, DM], F32, "ExternalInput"), ("Wp", [DM, DM], F32, "ExternalInput"),
             ("bp", [1, DM], F32, "ExternalInput"), ("ones", [1, P], F32, "ExternalInput"),
             ("identf", [P, P], F32, "ExternalInput"), ("mmq", [P, 4 * 256], F32, "ExternalInput"),
             ("y", [TOK, DM], F32, "ExternalOutput")],
    "l1f": [("x", [S, DM], F32, "ExternalInput"), ("xo", [TOK, DM], F32, "ExternalInput"),
            ("Wq", [DM, DM], F32, "ExternalInput"), ("Wk", [DM, DM], F32, "ExternalInput"),
            ("Wv", [DM, DM], F32, "ExternalInput"), ("Wp", [DM, DM], F32, "ExternalInput"),
            ("bqk", [P, 16], F32, "ExternalInput"), ("bav", [1, DM], F32, "ExternalInput"),
            ("bp", [1, DM], F32, "ExternalInput"), ("wrs", [DM, 17], F32, "ExternalInput"),
            ("brs", [1, 17], F32, "ExternalInput"), ("temp", [1, 16], F32, "ExternalInput"),
            ("embc", [64, 16], F32, "ExternalInput"), ("ohalf", [P, 2], F32, "ExternalInput"),
            ("ones", [1, P], F32, "ExternalInput"), ("identf", [P, P], F32, "ExternalInput"),
            ("y", [TOK, DM], F32, "ExternalOutput")],
    "l1ab": [("x", [S, DM], F32, "ExternalInput"), ("xo", [TOK, DM], F32, "ExternalInput"),
             ("sinb", [S, DM], F32, "ExternalInput"), ("cosb", [S, DM], F32, "ExternalInput"),
             ("sino", [TOK, DM], F32, "ExternalInput"), ("coso", [TOK, DM], F32, "ExternalInput"),
             ("Wq", [DM, DM], F32, "ExternalInput"), ("Wk", [DM, DM], F32, "ExternalInput"),
             ("Wv", [DM, DM], F32, "ExternalInput"), ("Wo", [DM, DM], F32, "ExternalInput"),
             ("bo", [1, DM], F32, "ExternalInput"), ("ones", [1, P], F32, "ExternalInput"),
             ("identf", [P, P], F32, "ExternalInput"), ("mmq", [P, 4 * 256], F32, "ExternalInput"),
             ("y", [TOK, DM], F32, "ExternalOutput")],
    "l0ab": [("xo", [TOK, DM], F32, "ExternalInput"),
             ("Wq", [DM, 4 * DM], F32, "ExternalInput"), ("Wk", [DM, 4 * DM], F32, "ExternalInput"),
             ("Wv", [DM, 4 * DM], F32, "ExternalInput"), ("Wo", [4 * DM, DM], F32, "ExternalInput"),
             ("bo", [1, DM], F32, "ExternalInput"), ("ones", [1, P], F32, "ExternalInput"),
             ("y", [TOK, DM], F32, "ExternalOutput")],
}

_EMITS = {}


def _register_emits():
    _EMITS.update({"l0aa": emit_l0_attn_a, "l0cnt": emit_l0_ffn_counts,
                   "l0ffn": emit_l0_ffn_main, "l1aa": emit_l1_attn_a,
                   "l1f": emit_l1_ffn, "l1ab": emit_l1_attn_b,
                   "l0ab": emit_l0_attn_b})


def _mk_standalone(key):
    _register_emits()
    def body(nc, es, ctx, pr, env_factory, inputs):
        env = env_factory("")
        _EMITS[key](env)
    return _scaffold(body, _SPECS[key])


def build_l0_attn_a():
    return _mk_standalone("l0aa")


def build_l0_ffn_counts():
    return _mk_standalone("l0cnt")


def build_l0_ffn_main():
    return _mk_standalone("l0ffn")


def build_l1_attn_a():
    return _mk_standalone("l1aa")


def build_l1_ffn():
    return _mk_standalone("l1f")


def build_l1_attn_b():
    return _mk_standalone("l1ab")


def build_l0_attn_b():
    return _mk_standalone("l0ab")


def emit_exchange(env, src_d, xg_d, xfull_d):
    ctx, pr = env.ctx, env.pr
    cq = pr.cq
    s = cq.sems[0]
    deps = [pr.sq.barrier(), (s, s.n)]
    v = ctx.ev("gpsimd", lambda e: e.collective_compute(
        "AllGather", ALU.bypass, [[0, 1], [2, 3], [4, 5], [6, 7]],
        [src_d.opt()], [xg_d.opt()]), deps=deps, inc=s, inc_by=1)
    toks = []
    for r in range(2):
        toks.append(ctx.dma("sync", lambda e, r=r: e.dma_start(
            out=xfull_d.rearrange("(i q p) d -> q i p d", q=2, p=128)[r],
            in_=xg_d[r * TOK:(r + 1) * TOK, :].rearrange("(i p) d -> i p d", p=128)),
            deps=[(s, v)]))
    return toks


def emit_rope_full(env, xfull_d, rxfull_d, sinb_d, cosb_d, dep):
    ctx, pr = env.ctx, env.pr
    sv = pr.sv
    xbt = env.sb("rpf_x", [P, DM], BF16)
    rot = env.sb("rpf_r", [P, DM], F32)
    sc_s = env.sb("rpf_s", [P, DM], F32)
    sc_c = env.sb("rpf_c", [P, DM], F32)
    rx = env.sb("rpf_o", [P, DM], BF16)
    free = None
    stok = None
    for i in range(32):
        r = i * 128
        ld = ctx.dma("sync", lambda e, r=r: e.dma_start(
            out=xbt[:], in_=xfull_d[r:r + 128, :]),
            deps=list(dep) + [(sv, free)])
        l2 = ctx.dma("sync", lambda e, r=r: e.dma_start(out=sc_s[:], in_=sinb_d[r:r + 128, :]),
                     deps=[(sv, free)])
        l3 = ctx.dma("sync", lambda e, r=r: e.dma_start(out=sc_c[:], in_=cosb_d[r:r + 128, :]),
                     deps=[(sv, free)])
        xb = xbt[:]
        ctx.ev("vector", lambda e, xb=xb: e.tensor_scalar_mul(
            rot[:].rearrange("p (a b) -> p a b", b=2)[:, :, 0],
            xb.rearrange("p (a b) -> p a b", b=2)[:, :, 1], -1.0),
            deps=[ld, stok])
        ctx.ev("vector", lambda e, xb=xb: e.tensor_copy(
            rot[:].rearrange("p (a b) -> p a b", b=2)[:, :, 1],
            xb.rearrange("p (a b) -> p a b", b=2)[:, :, 0]))
        ctx.ev("vector", lambda e: e.tensor_mul(rot[:], rot[:], sc_s[:]), deps=[l2])
        v4 = ctx.ev("vector", lambda e, xb=xb: e.scalar_tensor_tensor(
            rx[:], xb, 1.0, sc_c[:], ALU.mult, ALU.mult), deps=[l3])
        free = ctx.ev("vector", lambda e: e.tensor_add(rx[:], rx[:], rot[:]))
        stok = ctx.dma("sync", lambda e, r=r: e.dma_start(
            out=rxfull_d[r:r + 128, :], in_=rx[:]), deps=[(sv, free)])
    return stok


def build_fused(n_sub=8, n_start=0, idxs=None):
    _register_emits()
    specs = [("x0o", [TOK, DM], F32, "ExternalInput"),
             ("y", [TOK, DM], F32, "ExternalOutput")]
    ren = {}
    for key, pref in (("l0aa", "A"), ("l0cnt", "C"), ("l0ffn", "F"),
                      ("l1aa", "D"), ("l1f", "G"), ("l1ab", "H"), ("l0ab", "B")):
        ren[key] = {}
        for nm, shape, dt, kind in _SPECS[key]:
            if nm in ("x", "xo", "y", "parts", "cnt"):
                continue
            g = pref + "_" + nm
            ren[key][nm] = g
            specs.append((g, shape, dt, "ExternalInput"))

    def body(nc, es, ctx, pr, env_factory, inputs):
        xown = nc.dram_tensor("xown", [TOK, DM], F32, kind="Internal").ap()
        xg = nc.dram_tensor("xg", [2 * TOK, DM], BF16, kind="Internal").ap()
        xfull = nc.dram_tensor("xfull", [S, DM], BF16, kind="Internal").ap()
        rxfull = nc.dram_tensor("rxfull", [S, DM], BF16, kind="Internal").ap()
        cnt8 = nc.dram_tensor("cnt8", [8, 1], F32, kind="Internal").ap()
        cnts = nc.dram_tensor("cnts", [8, 1], F32, kind="Internal").ap()

        env0 = env_factory("i_")
        t0 = ctx.dma("sync", lambda e: e.dma_start(out=xown[:], in_=inputs["x0o"][:]))
        emit_barrier(ctx, pr)

        order_full = [("l0aa", "A", "s0", True, False, False),
                 ("l0ffn", "F", "s1", False, False, True),
                 ("l0ab", "B", "s2", False, False, False),
                 ("l0ffn", "F", "s3", False, False, True),
                 ("l1aa", "D", "s4", True, False, False),
                 ("l1f", "G", "s5", True, False, False),
                 ("l1ab", "H", "s6", True, True, False),
                 ("l1f", "G", "s7", True, False, False)]
        order = [order_full[i] for i in idxs] if idxs is not None else order_full[n_start:n_sub]
        for idx, (key, pref, spfx, need_full, need_rope, need_cnt) in enumerate(order):
            extra = {nm: inputs[g] for nm, g in ren[key].items()}
            extra["xo"] = xown
            extra["y"] = inputs["y"] if idx == len(order) - 1 else xown
            env = env_factory(spfx + "_", extra)
            if need_full or need_cnt:
                pass
            if need_full:
                xlnx = nc.dram_tensor(spfx + "_xlnx", [TOK, DM], BF16, kind="Internal").ap()
                envx = env_factory(spfx + "x_", {"xo": xown})
                t = dict(xin0=envx.sb("xin0", [P, DM], F32),
                         xin1=envx.sb("xin1", [P, DM], F32),
                         xst=envx.sb("xst", [P, 12], F32),
                         xmv=envx.sb("xmv", [P, 2], F32),
                         xsd=envx.sb("xsd", [P, 2], F32),
                         xln0=envx.sb("xln0", [P, DM], BF16),
                         xln1=envx.sb("xln1", [P, DM], BF16),
                         epsc=envx.sb("epsc", [P, 1], F32))
                pr.t = t
                pr.lnfree = {0: None, 1: None}
                pr.lnsfree = {0: None, 1: None}
                ctx.ev("vector", lambda e, t=t: e.memset(t["epsc"][:], 1e-5))
                emit_ln(pr, xown, 0, 16, xlnx)
                tok = emit_exchange(envx, xlnx, xg, xfull)
                if need_rope:
                    emit_rope_full(envx, xfull, rxfull,
                                   inputs[ren[key]["sinb"]], inputs[ren[key]["cosb"]], tok)
                    extra["rxfull"] = rxfull
                extra["xfull"] = xfull
                emit_barrier(ctx, pr)
                env = env_factory(spfx + "_", extra)
            if need_cnt:
                envc = env_factory(spfx + "c_",
                                   {nm: inputs[g] for nm, g in ren["l0cnt"].items()}
                                   | {"xo": xown, "cnt": cnt8})
                _EMITS["l0cnt"](envc)
                s = pr.cq.sems[0]
                v = ctx.ev("gpsimd", lambda e: e.collective_compute(
                    "AllReduce", ALU.add, [list(range(8))],
                    [cnt8.opt()], [cnts.opt()]),
                    deps=[pr.sq.barrier(), (s, s.n)], inc=s, inc_by=1)
                extra["cnts"] = cnts
                extra["identf"] = inputs[ren["l0cnt"]["ident"]]
                emit_barrier(ctx, pr)
                ctx.ev("sync", lambda e, s=s, v=v: e.nop(), deps=[(s, v)])
                env = env_factory(spfx + "_", extra)
            _EMITS[key](env)
            emit_barrier(ctx, pr)

    return _scaffold(body, specs)

_prog_cache = {}
_ONES = np.ones((1, P), dtype=np.float32)


def _pc(b):
    return np.ascontiguousarray(b.reshape(-1, P).T)


def _get_prog(key, builder, *args):
    if key not in _prog_cache:
        _prog_cache[key] = builder(*args)
    return _prog_cache[key]


def _scatter(y, res):
    for c in range(NCORE):
        b, p = c // 2, c % 2
        rows = own_rows(p)
        y[b][rows] = res[c]
    return y


def run_l0_attn_a(x, ins):
    nc = _get_prog("l0aa", build_l0_attn_a)
    in_maps = []
    for c in range(NCORE):
        b, p = c // 2, c % 2
        in_maps.append({
            "x": np.ascontiguousarray(x[b]),
            "xo": np.ascontiguousarray(x[b][own_rows(p)]),
            "Wq": ins["L0_attn_a_Wq"], "Wk": ins["L0_attn_a_Wk"],
            "Wv": ins["L0_attn_a_Wv"], "Wo": ins["L0_attn_a_Wo"],
            "bq": _pc(ins["L0_attn_a_bq"]), "bk": _pc(ins["L0_attn_a_bk"]),
            "bv": _pc(ins["L0_attn_a_bv"]), "bo": ins["L0_attn_a_bo"],
            "ones": _ONES,
        })
    res = run_bass_kernel_spmd(nc, in_maps, core_ids=list(range(NCORE)))
    y = np.empty((NB, S, DM), np.float32)
    return _scatter(y, [r["y"] for r in res.results])


SUBLAYERS = [run_l0_attn_a, run_l0_ffn, run_l0_attn_b, run_l0_ffn,
             run_l1_attn_a, run_l1_ffn, run_l1_attn_b, run_l1_ffn]


FUSED = True


def _fused_in_maps(x, ins):
    sinb, cosb = _rope_tables()
    identf = np.eye(P, dtype=np.float32)
    tri = np.tril(np.ones((P, P), np.float32)).T
    onesm = np.ones((P, P), np.float32)
    zerosm = np.zeros((P, P), np.float32)
    ba = ins["L1_attn_a_b_attn"]
    com = {
        "A_Wq": ins["L0_attn_a_Wq"], "A_Wk": ins["L0_attn_a_Wk"],
        "A_Wv": ins["L0_attn_a_Wv"], "A_Wo": ins["L0_attn_a_Wo"],
        "A_bq": _pc(ins["L0_attn_a_bq"]), "A_bk": _pc(ins["L0_attn_a_bk"]),
        "A_bv": _pc(ins["L0_attn_a_bv"]), "A_bo": ins["L0_attn_a_bo"],
        "A_ones": _ONES,
        "C_gw": ins["L0_ffn_gate_w"], "C_gb": ins["L0_ffn_gate_b"],
        "C_ones": _ONES, "C_ident": identf,
        "F_we": ins["L0_ffn_expert_w"],
        "F_web": np.ascontiguousarray(ins["L0_ffn_expert_b"].reshape(1, 8 * DM)),
        "F_ones": _ONES, "F_onesc8": np.ones((8, 1), np.float32),
        "D_Wa": ins["L1_attn_a_W_attn"],
        "D_bqk": np.ascontiguousarray(np.concatenate([_pc(ba[0:DM]), _pc(ba[DM:2 * DM])], axis=1)),
        "D_bav": np.ascontiguousarray(ba[2 * DM:3 * DM].reshape(1, DM)),
        "D_Wp": ins["L1_attn_a_W_proj"],
        "D_bp": np.ascontiguousarray(ins["L1_attn_a_b_proj"].reshape(1, DM)),
        "D_ones": _ONES, "D_identf": identf,
        "G_Wq": ins["L1_ffn_Wq"], "G_Wk": ins["L1_ffn_Wk"],
        "G_Wv": ins["L1_ffn_Wv"], "G_Wp": ins["L1_ffn_Wp"],
        "G_bqk": np.ascontiguousarray(np.concatenate([_pc(ins["L1_ffn_bq"]), _pc(ins["L1_ffn_bk"])], axis=1)),
        "G_bav": np.ascontiguousarray(ins["L1_ffn_bv"].reshape(1, DM)),
        "G_bp": np.ascontiguousarray(ins["L1_ffn_bp"].reshape(1, DM)),
        "G_wrs": np.ascontiguousarray(np.concatenate([ins["L1_ffn_Wr"].T, ins["L1_ffn_Ws"].T], axis=1)),
        "G_brs": np.ascontiguousarray(np.concatenate([ins["L1_ffn_br"], ins["L1_ffn_bs"]]).reshape(1, 17)),
        "G_temp": np.ascontiguousarray(ins["L1_ffn_temperature"].reshape(1, 16)),
        "G_embc": np.ascontiguousarray(ins["L1_ffn_query_embedding"].reshape(16, 64).T),
        "G_ohalf": None, "G_ones": _ONES, "G_identf": identf,
        "H_sinb": sinb, "H_cosb": cosb,
        "H_Wq": ins["L1_attn_b_Wq"], "H_Wk": ins["L1_attn_b_Wk"],
        "H_Wv": ins["L1_attn_b_Wv"], "H_Wo": ins["L1_attn_b_Wo"],
        "H_bo": np.ascontiguousarray(ins["L1_attn_b_bo"].reshape(1, DM)),
        "H_ones": _ONES, "H_identf": identf,
        "B_Wq": ins["L0_attn_b_Wq"], "B_Wk": ins["L0_attn_b_Wk"],
        "B_Wv": ins["L0_attn_b_Wv"], "B_Wo": ins["L0_attn_b_Wo"],
        "B_bo": np.ascontiguousarray(ins["L0_attn_b_bo"].reshape(1, DM)),
        "B_ones": _ONES,
    }
    oh = np.zeros((P, 2), np.float32)
    oh[0:64, 0] = 1.0
    oh[64:128, 1] = 1.0
    com["G_ohalf"] = oh
    in_maps = []
    for c in range(NCORE):
        b, p = c // 2, c % 2
        mm = np.empty((4, P, 256), np.float32)
        for dkt in range(4):
            left = tri if dkt == p else (onesm if dkt < p else zerosm)
            right = tri if dkt == 2 + p else (onesm if dkt < 2 + p else zerosm)
            mm[dkt, :, 0:128] = left
            mm[dkt, :, 128:256] = right
        mmq = np.ascontiguousarray(mm.transpose(1, 0, 2).reshape(P, 4 * 256))
        rows = own_rows(p)
        m = dict(com)
        m["x0o"] = np.ascontiguousarray(x[b][rows])
        m["D_mmq"] = mmq
        m["H_mmq"] = mmq
        m["H_sino"] = np.ascontiguousarray(sinb[rows])
        m["H_coso"] = np.ascontiguousarray(cosb[rows])
        in_maps.append(m)
    return in_maps


def run_fused(x, ins, n_sub=8, n_start=0, idxs=None):
    nc = _get_prog(("fused", n_sub, n_start, tuple(idxs) if idxs else None), build_fused, n_sub, n_start, idxs)
    in_maps = _fused_in_maps(x, ins)
    res = run_bass_kernel_spmd(nc, in_maps, core_ids=list(range(NCORE)))
    y = np.empty((NB, S, DM), np.float32)
    return _scatter(y, [r["y"] for r in res.results])


def kernel(**inputs):
    x = np.ascontiguousarray(np.asarray(inputs["L0_attn_a_x"], dtype=np.float32))
    if FUSED:
        return run_fused(x, inputs)
    for f in SUBLAYERS:
        x = f(x, inputs)
    return x
